# version 14 (final)
# speedup vs baseline: 1.0404x; 1.0031x over previous
import numpy as np
import ml_dtypes
from contextlib import ExitStack
import concourse.bass as bass
import concourse.mybir as mybir
from concourse.bass_utils import run_bass_kernel_spmd



F32 = mybir.dt.float32
BF16 = mybir.dt.bfloat16
I32 = mybir.dt.int32
AF = mybir.ActivationFunctionType
AX = mybir.AxisListType
OP = mybir.AluOpType

ENGINES = ("sync", "vector", "scalar", "tensor", "gpsimd")


class Sem:
    __slots__ = ("h", "n", "name")

    def __init__(self, handle, name):
        self.h = handle
        self.n = 0
        self.name = name


class OpRef:
    __slots__ = ("fname", "args", "kw", "waits", "incs")

    def __init__(self, fname, args, kw):
        self.fname = fname
        self.args = args
        self.kw = kw
        self.waits = []
        self.incs = []


class Seg:

    def __init__(self, nc):
        self.nc = nc
        self.es = ExitStack()
        self.ops = {e: [] for e in ENGINES}
        self._semid = 0
        self._last = {e: None for e in ENGINES}

    def sem(self, name=None):
        self._semid += 1
        nm = name or f"s{self._semid}"
        return Sem(self.es.enter_context(self.nc.semaphore(nm)), nm)

    def sbuf(self, name, shape, dt=F32, stack=None, side=None):
        if side is not None:
            return (stack or self.es).enter_context(
                self.nc.sbuf_tensor(name, shape, dt, side=side))
        return (stack or self.es).enter_context(self.nc.sbuf_tensor(name, shape, dt))

    def psum(self, name, shape, dt=F32, stack=None):
        return (stack or self.es).enter_context(self.nc.psum_tensor(name, shape, dt))

    def op(self, eng, fname, *args, **kw):
        r = OpRef(fname, args, kw)
        self.ops[eng].append(r)
        self._last[eng] = r
        return r

    def wait(self, eng, sem, val=None):
        v = sem.n if val is None else val
        if v == 0:
            return None
        r = OpRef("wait_ge", (sem.h, v), {})
        r.waits = None
        self.ops[eng].append(r)
        return r

    def inc(self, sem, k=1, eng=None, ref=None):
        r = ref or self._last[eng]
        r.incs.append((sem.h, k))
        sem.n += 1 if k is None else k
        return r

    def chain(self, eng, sem):
        self.inc(sem, 1, eng=eng)
        self.wait(eng, sem)

    def dma(self, eng, out, in_, sem, **kw):
        r = self.op(eng, "dma_start", out=out, in_=in_, **kw)
        self.inc(sem, 16, eng=eng)
        return r

    def emit(self):
        nc = self.nc
        with nc.Block() as block:
            for ename in ENGINES:
                oplist = self.ops[ename]
                if not oplist:
                    continue

                def run(eng, oplist=oplist):
                    for r in oplist:
                        if r.waits is None:
                            eng.wait_ge(*r.args)
                            continue
                        inst = getattr(eng, r.fname)(*r.args, **r.kw)
                        for h, k in r.incs:
                            if k is None:
                                inst.then_inc(h)
                            else:
                                inst.then_inc(h, k)

                getattr(block, ename)(run)

    def close(self):
        self.es.close()


def ceil_div(a, b):
    return -(-a // b)


class Common:
    _uid = [0]

    def __init__(self, seg):
        self.seg = seg
        Common._uid[0] += 1
        u = f"cm{Common._uid[0]}_"
        self.cpool = seg.sbuf(u + "cpool", [128, 16], F32)
        self.ones_col = seg.sbuf(u + "ones_col", [128, 1], BF16)
        self.ones_row = seg.sbuf(u + "ones_row", [1, 128], BF16)
        self.s_init = seg.sem("s_init")
        seg.op("vector", "memset", self.cpool[:, 0:1], 1e-5)
        seg.op("vector", "memset", self.cpool[:, 1:2], -0.5)
        seg.op("vector", "memset", self.ones_col[:], 1.0)
        seg.op("vector", "memset", self.ones_row[:], 1.0)
        seg.inc(self.s_init, 1, eng="vector")
        self.s_v = seg.sem("s_v")
        self.s_s = seg.sem("s_s")
        self.s_t = seg.sem("s_t")
        self.s_w = seg.sem("s_w")
        self.s_ld = seg.sem("s_ld")
        self.s_st = seg.sem("s_st")
        self.eps = self.cpool[:, 0:1]
        self.eps1 = self.cpool[0:1, 0:1]
        self.mhalf = self.cpool[:, 1:2]
        self.mhalf1 = self.cpool[0:1, 1:2]
        self._cvals = {1e-5: 0, -0.5: 1}
        self._cnext = 2

    def constf(self, v):
        v = float(v)
        if v not in self._cvals:
            assert self._cnext < 16, "const pool full"
            col = self._cnext
            self._cnext += 1
            self._cvals[v] = col
            self.seg.op("vector", "memset", self.cpool[:, col:col + 1], v)
            self.seg.inc(self.s_init, 1, eng="vector")
        col = self._cvals[v]
        return self.cpool[:, col:col + 1]

    def constf1(self, v):
        ap = self.constf(v)
        col = self._cvals[float(v)]
        return self.cpool[0:1, col:col + 1]


def emit_ln_fm(seg, cm, xs, T, h_bf, es_scope, out_scale=1.0,
               name="ln", xs_ready=None, gb=None, gb_ready=None):
    NC = 8
    ntc = ceil_div(T, 512)
    x_bf = seg.sbuf(f"{name}_xbf", [128, NC * 512], BF16, stack=es_scope)
    xsq_bf = seg.sbuf(f"{name}_xsq", [128, NC * 512], BF16, stack=es_scope)
    rows_f = seg.sbuf(f"{name}_rows_f", [1, 4 * 512], F32, stack=es_scope)
    rows_bf = seg.sbuf(f"{name}_rows_bf", [1, 2 * 512], BF16, stack=es_scope)
    sum_ps = seg.psum(f"{name}_sum_ps", [1, 512], F32, stack=es_scope)
    sq_ps = seg.psum(f"{name}_sq_ps", [1, 512], F32, stack=es_scope)
    bc_m = seg.psum(f"{name}_bcm", [128, 512], F32, stack=es_scope)
    bc_r = seg.psum(f"{name}_bcr", [128, 512], F32, stack=es_scope)

    first = {"vector": True, "scalar": True, "tensor": True}

    def w(eng):
        if first[eng]:
            first[eng] = False
            for s, v in (xs_ready or []):
                seg.wait(eng, s, v)

    seg.wait("tensor", cm.s_init)
    cast_done_v = 0
    cast_done_s = 0
    for tc in range(ntc):
        t0, t1 = tc * 512, min(T, (tc + 1) * 512)
        n = t1 - t0
        sl = lambda c: slice(c * T + t0, c * T + t1)
        csl = lambda c: slice(c * 512, c * 512 + n)
        w("vector")
        if tc > 0:
            seg.wait("vector", cm.s_t)
        for c in range(NC):
            seg.op("vector", "tensor_copy", out=x_bf[:, csl(c)], in_=xs[:, sl(c)])
        seg.inc(cm.s_v, 1, eng="vector")
        cast_done_v = cm.s_v.n
        w("scalar")
        if tc > 0:
            seg.wait("scalar", cm.s_t)
        for c in range(NC):
            seg.op("scalar", "activation", out=xsq_bf[:, csl(c)], in_=xs[:, sl(c)], func=AF.Square)
        seg.inc(cm.s_s, 1, eng="scalar")
        cast_done_s = cm.s_s.n

        seg.wait("tensor", cm.s_v, cast_done_v)
        seg.wait("tensor", cm.s_s, cast_done_s)
        if tc > 0:
            seg.wait("tensor", cm.s_v)
        for c in range(NC):
            seg.op("tensor", "matmul", sum_ps[:, 0:n], cm.ones_col[:], x_bf[:, csl(c)],
                   start=(c == 0), stop=(c == NC - 1))
        for c in range(NC):
            seg.op("tensor", "matmul", sq_ps[:, 0:n], cm.ones_col[:], xsq_bf[:, csl(c)],
                   start=(c == 0), stop=(c == NC - 1))
        seg.inc(cm.s_t, 1, eng="tensor")

        mean = rows_f[0:1, 0:n]
        var = rows_f[0:1, 512:512 + n]
        rstd = rows_f[0:1, 1024:1024 + n]
        msq = rows_f[0:1, 1536:1536 + n]
        mean_b = rows_bf[0:1, 0:n]
        rstd_b = rows_bf[0:1, 512:512 + n]
        seg.wait("vector", cm.s_t)
        seg.op("vector", "tensor_scalar_mul", out=mean, in0=sum_ps[:, 0:n], scalar1=1.0 / 1024)
        seg.chain("vector", cm.s_v)
        seg.op("vector", "tensor_tensor", out=msq, in0=mean, in1=mean, op=OP.mult)
        seg.chain("vector", cm.s_v)
        seg.op("vector", "scalar_tensor_tensor", out=var, in0=sq_ps[:, 0:n], scalar=1.0 / 1024,
               in1=msq, op0=OP.mult, op1=OP.subtract)
        seg.inc(cm.s_v, 1, eng="vector")
        seg.wait("scalar", cm.s_v)
        seg.op("scalar", "activation", out=rstd, in_=var, func=AF.Ln, bias=cm.eps1)
        seg.chain("scalar", cm.s_s)
        seg.op("scalar", "activation", out=rstd, in_=rstd, func=AF.Exp, scale=cm.mhalf1)
        seg.inc(cm.s_s, 1, eng="scalar")
        seg.wait("vector", cm.s_s)
        if out_scale != 1.0:
            seg.op("vector", "tensor_scalar_mul", out=rstd, in0=rstd, scalar1=float(out_scale))
            seg.chain("vector", cm.s_v)
        seg.op("vector", "tensor_copy", out=mean_b, in_=mean)
        seg.op("vector", "tensor_copy", out=rstd_b, in_=rstd)
        seg.inc(cm.s_v, 1, eng="vector")

        seg.wait("tensor", cm.s_v)
        seg.op("tensor", "matmul", bc_m[:, 0:n], cm.ones_row[:], mean_b, start=True, stop=True)
        seg.op("tensor", "matmul", bc_r[:, 0:n], cm.ones_row[:], rstd_b, start=True, stop=True)
        seg.inc(cm.s_t, 1, eng="tensor")

        seg.wait("vector", cm.s_t)
        if tc == 0 and gb is not None and gb_ready is not None:
            seg.wait("vector", gb_ready[0], gb_ready[1])
        for c in range(NC):
            seg.op("vector", "tensor_tensor", out=h_bf[:, sl(c)], in0=xs[:, sl(c)],
                   in1=bc_m[:, 0:n], op=OP.subtract)
        seg.chain("vector", cm.s_v)
        for c in range(NC):
            seg.op("vector", "tensor_tensor", out=h_bf[:, sl(c)], in0=h_bf[:, sl(c)],
                   in1=bc_r[:, 0:n], op=OP.mult)
        seg.inc(cm.s_v, 1, eng="vector")
        if gb is not None:
            g_sb, b_sb = gb
            seg.wait("vector", cm.s_v)
            for c in range(NC):
                seg.op("vector", "tensor_scalar", out=h_bf[:, sl(c)], in0=h_bf[:, sl(c)],
                       scalar1=g_sb[:, c:c + 1], scalar2=b_sb[:, c:c + 1],
                       op0=OP.mult, op1=OP.add)
            seg.inc(cm.s_v, 1, eng="vector")
    return {"h_v": cm.s_v.n, "t_done": cm.s_t.n, "v_done": cm.s_v.n}


B, SEQ, D = 4, 4096, 1024
NCORES = 8
NT_OWN = 16
NT_ALL = 32


def own_token_index(core):
    par = core % 2
    g = (2 * np.arange(NT_OWN) + par)[:, None] * 128 + np.arange(128)[None, :]
    return g.reshape(-1)


def slice_x_fm(x, core):
    b = core // 2
    tok = x[b, own_token_index(core)]
    return np.ascontiguousarray(tok.T)


def unslice_x_fm(slabs, dtype=np.float32):
    x = np.empty((B, SEQ, D), dtype=dtype)
    for core in range(NCORES):
        b = core // 2
        x[b, own_token_index(core)] = slabs[core].T
    return x


def slice_rows(table, core):
    return np.ascontiguousarray(table[own_token_index(core)])


def slice_cols(table, core):
    return np.ascontiguousarray(table[..., own_token_index(core)])


RG_PAIRS = [[0, 1], [2, 3], [4, 5], [6, 7]]




def build_s1(nc, standalone=True, xs=None, cm=None, seg=None, T=2048, name="s1",
             rg=RG_PAIRS, dram_prefix="", io_dram=None):
    NC = 8
    KEYS = 2 * T
    NKT = KEYS // 128
    NKT_OWN = T // 128
    NQC = T // 512
    SC = 1.0 / (512.0 ** 0.5)

    if standalone:
        seg = Seg(nc)
        x_in = nc.dram_tensor("x_in", [1024, T], F32, kind="ExternalInput").ap()
        x_out = nc.dram_tensor("x_out", [1024, T], F32, kind="ExternalOutput").ap()
    wq_e = nc.dram_tensor(dram_prefix + "wqT_e", [1024, 512], F32, kind="ExternalInput").ap()
    wq_o = nc.dram_tensor(dram_prefix + "wqT_o", [1024, 512], F32, kind="ExternalInput").ap()
    wk_e = nc.dram_tensor(dram_prefix + "wkT_e", [1024, 512], F32, kind="ExternalInput").ap()
    wk_o = nc.dram_tensor(dram_prefix + "wkT_o", [1024, 512], F32, kind="ExternalInput").ap()
    wv = nc.dram_tensor(dram_prefix + "wvT", [1024, 1024], F32, kind="ExternalInput").ap()
    b_eo = nc.dram_tensor(dram_prefix + "b_eo", [128, 16], F32, kind="ExternalInput").ap()
    bv_d = nc.dram_tensor(dram_prefix + "bv2", [128, 8], F32, kind="ExternalInput").ap()
    t_e_d = nc.dram_tensor(dram_prefix + "t_e_own", [512, T], BF16, kind="ExternalInput").ap()
    CCN = 1024 * T + T * 1024 + 2048
    cc_in = nc.dram_tensor(name + "cc_in", [CCN], BF16, kind="Internal").ap()
    cck_out = nc.dram_tensor(name + "cck_out", [2 * 1024 * T], BF16, kind="Internal").ap()
    ccv_out = nc.dram_tensor(name + "ccv_out", [2 * T * 1024], BF16, kind="Internal").ap()
    ccn_out = nc.dram_tensor(name + "ccn_out", [2 * 2048], BF16, kind="Internal").ap()
    ck_in = cc_in[0:1024 * T].rearrange("(a b) -> a b", b=T)
    cv_in = cc_in[1024 * T:2 * 1024 * T].rearrange("(a b) -> a b", b=1024)
    cn_in = cc_in[2 * 1024 * T:2 * 1024 * T + T][None, :]
    ck_out = [cck_out[r * 1024 * T: (r + 1) * 1024 * T].rearrange("(a b) -> a b", b=T) for r in range(2)]
    cv_out = [ccv_out[r * T * 1024: (r + 1) * T * 1024].rearrange("(a b) -> a b", b=1024) for r in range(2)]
    cn_out = [ccn_out[r * 2048: r * 2048 + T][None, :] for r in range(2)]

    if standalone:
        cm = Common(seg)
        xs = seg.sbuf("xs", [128, NC * T], F32)
        for c in range(NC):
            seg.dma("sync", xs[:, c * T:(c + 1) * T], x_in[c * 128:(c + 1) * 128, :], cm.s_ld)

    sc_ap = cm.constf(SC)

    qp_e = seg.sbuf(name + "qpe", [128, 4 * T], BF16)
    qp_o = seg.sbuf(name + "qpo", [128, 4 * T], BF16)
    rows = seg.sbuf(name + "rows", [1, 2 * T], F32)
    bv_sb = seg.sbuf(name + "bv", [128, NC], F32)
    b_sb = seg.sbuf(name + "b", [128, 16], F32)
    eo_sb = [seg.sbuf(name + f"eosb{i}", [128, 512], BF16) for i in range(2)]
    ebias = seg.sbuf(name + "ebiassb", [128, 16], F32)

    es_ln = ExitStack()
    h = seg.sbuf(name + "h", [128, NC * T], BF16)
    ln = emit_ln_fm(seg, cm, xs, T, h, es_ln, name=name + "ln",
                    xs_ready=[(cm.s_ld, cm.s_ld.n)])
    es_ln.close()

    es2 = ExitStack()
    es2t = ExitStack()
    es2p = ExitStack()
    wq_e_sb = seg.sbuf(name + "wqe", [128, NC * 512], BF16, stack=es2t)
    wq_o_sb = seg.sbuf(name + "wqo", [128, NC * 512], BF16, stack=es2t)
    wk_e_sb = seg.sbuf(name + "wke", [128, NC * 512], BF16, stack=es2t)
    wk_o_sb = seg.sbuf(name + "wko", [128, NC * 512], BF16, stack=es2t)
    t_e_sb = seg.sbuf(name + "te", [128, 4 * T], BF16, stack=es2t)
    t_o_sb = t_e_sb
    s_wq = seg.sem(name + "swq")
    seg.wait("gpsimd", cm.s_t, ln["t_done"])
    seg.wait("gpsimd", cm.s_v, ln["v_done"])
    seg.wait("gpsimd", cm.s_s)
    for ap_sb, ap_d in ((wq_e_sb, wq_e), (wq_o_sb, wq_o),
                        (wk_e_sb, wk_e), (wk_o_sb, wk_o)):
        seg.dma("gpsimd", ap_sb[:].rearrange("p (c m) -> p c m", c=NC),
                ap_d.rearrange("(c p) m -> p c m", p=128), s_wq)
    seg.dma("gpsimd", b_sb[:], b_eo[:], s_wq)
    seg.dma("gpsimd", bv_sb[:], bv_d[:], s_wq)
    seg.dma("gpsimd", t_e_sb[:].rearrange("p (c t) -> p c t", c=4),
            t_e_d.rearrange("(c p) t -> p c t", p=128), s_wq)
    w_loaded = s_wq.n

    e_ps = [seg.psum(name + f"eps{i}", [128, 512], F32, stack=es2p) for i in range(2)]
    o_ps = [seg.psum(name + f"ops{i}", [128, 512], F32, stack=es2p) for i in range(2)]
    nsum_ps = seg.psum(name + "nsum", [1, 512], F32, stack=es2p)
    ebias_ps = seg.psum(name + "ebias", [128, 16], F32, stack=es2p)

    seg.wait("tensor", s_wq, w_loaded)
    seg.wait("tensor", cm.s_v, ln["h_v"])
    seg.wait("tensor", cm.s_init)

    ps_read = {}

    def qk_tile(side, oc, tc, par, pe_out, po_out, grp_note):
        we_sb = wq_e_sb if side == "q" else wk_e_sb
        wo_sb = wq_o_sb if side == "q" else wk_o_sb
        bia0 = 0 if side == "q" else 8
        bia1 = 4 if side == "q" else 12
        if par in ps_read:
            seg.wait("tensor", cm.s_v, ps_read[par])
        for c in range(NC):
            seg.op("tensor", "matmul", e_ps[par][:],
                   we_sb[:, c * 512 + oc * 128: c * 512 + oc * 128 + 128],
                   h[:, c * T + tc * 512: c * T + (tc + 1) * 512],
                   start=(c == 0), stop=(c == NC - 1))
        for c in range(NC):
            seg.op("tensor", "matmul", o_ps[par][:],
                   wo_sb[:, c * 512 + oc * 128: c * 512 + oc * 128 + 128],
                   h[:, c * T + tc * 512: c * T + (tc + 1) * 512],
                   start=(c == 0), stop=(c == NC - 1))
        seg.inc(cm.s_t, 1, eng="tensor")
        grp = cm.s_t.n
        seg.wait("scalar", cm.s_t, grp)
        if par in ps_read:
            seg.wait("scalar", cm.s_v, ps_read[par])
        seg.op("scalar", "activation", out=eo_sb[par][:], in_=o_ps[par][:],
               func=AF.Identity, bias=b_sb[:, bia1 + oc: bia1 + oc + 1])
        seg.inc(cm.s_s, 1, eng="scalar")
        seg.wait("vector", cm.s_s)
        tsl = slice(oc * T + tc * 512, oc * T + (tc + 1) * 512)
        pe_t, pe_sl = pe_out
        po_t, po_sl = po_out
        seg.op("vector", "scalar_tensor_tensor", out=pe_t[:, pe_sl], in0=e_ps[par][:],
               scalar=b_sb[:, bia0 + oc: bia0 + oc + 1], in1=eo_sb[par][:],
               op0=OP.add, op1=OP.subtract)
        seg.op("vector", "scalar_tensor_tensor", out=po_t[:, po_sl], in0=e_ps[par][:],
               scalar=b_sb[:, bia0 + oc: bia0 + oc + 1], in1=eo_sb[par][:],
               op0=OP.add, op1=OP.add)
        seg.chain("vector", cm.s_v)
        seg.op("vector", "tensor_tensor", out=pe_t[:, pe_sl], in0=pe_t[:, pe_sl],
               in1=t_e_sb[:, tsl], op=OP.mult)
        seg.op("vector", "tensor_tensor", out=po_t[:, po_sl], in0=po_t[:, po_sl],
               in1=t_o_sb[:, tsl], op=OP.mult)
        seg.inc(cm.s_v, 1, eng="vector")
        ps_read[par] = cm.s_v.n

    for oc in range(4):
        for tc in range(NQC):
            par = (oc * NQC + tc) % 2
            sl = slice(oc * T + tc * 512, oc * T + (tc + 1) * 512)
            qk_tile("q", oc, tc, par, (qp_e, sl), (qp_o, sl), None)
    q_done_v = cm.s_v.n

    sqr = [seg.sbuf(name + f"sqr{i}", [128, 512], BF16, stack=es2t) for i in range(2)]
    sq_read_t = [0, 0]
    nsum_read_v = 0
    for tc in range(NQC):
        for oc in range(4):
            for half, src in ((0, qp_e), (1, qp_o)):
                par = (oc * 2 + half) % 2
                seg.wait("scalar", cm.s_v, q_done_v)
                if sq_read_t[par]:
                    seg.wait("scalar", cm.s_t, sq_read_t[par])
                seg.op("scalar", "activation", out=sqr[par][:],
                       in_=src[:, oc * T + tc * 512: oc * T + (tc + 1) * 512], func=AF.Square)
                seg.inc(cm.s_s, 1, eng="scalar")
                seg.wait("tensor", cm.s_s)
                if oc == 0 and half == 0 and nsum_read_v:
                    seg.wait("tensor", cm.s_v, nsum_read_v)
                seg.op("tensor", "matmul", nsum_ps[:], cm.ones_col[:], sqr[par][:],
                       start=(oc == 0 and half == 0), stop=(oc == 3 and half == 1))
                seg.inc(cm.s_t, 1, eng="tensor")
                sq_read_t[par] = cm.s_t.n
        seg.wait("vector", cm.s_t)
        seg.op("vector", "tensor_copy", out=rows[0:1, tc * 512:(tc + 1) * 512], in_=nsum_ps[:])
        seg.inc(cm.s_v, 1, eng="vector")
        nsum_read_v = cm.s_v.n

    s_ex = seg.sem(name + "sex")
    s_exk = [seg.sem(name + "sexk0"), seg.sem(name + "sexk1")]
    s_exv = [seg.sem(name + "sexv0"), seg.sem(name + "sexv1")]
    kb = [seg.sbuf(name + f"kb{i}", [128, 512], BF16, stack=es2t) for i in range(4)]
    kb_dma_done = {}
    for tc in range(NQC):
        for oc in range(4):
            par = oc % 2
            if par in kb_dma_done:
                seg.wait("vector", s_exk[par], kb_dma_done[par][0])
                seg.wait("vector", cm.s_s, kb_dma_done[par][1])
            qk_tile("k", oc, tc, par, (kb[par * 2], slice(0, 512)),
                    (kb[par * 2 + 1], slice(0, 512)), None)
            built_v = cm.s_v.n
            seg.wait("sync", cm.s_v, built_v)
            seg.dma("sync", ck_in[oc * 128:(oc + 1) * 128, tc * 512:(tc + 1) * 512], kb[par * 2][:], s_exk[par])
            seg.dma("sync", ck_in[512 + oc * 128: 512 + (oc + 1) * 128, tc * 512:(tc + 1) * 512],
                    kb[par * 2 + 1][:], s_exk[par])
            kb_dma_before = s_exk[par].n
            for half in range(2):
                sp = (oc * 2 + half) % 2
                seg.wait("scalar", cm.s_v, built_v)
                if sq_read_t[sp]:
                    seg.wait("scalar", cm.s_t, sq_read_t[sp])
                seg.op("scalar", "activation", out=sqr[sp][:], in_=kb[par * 2 + half][:], func=AF.Square)
                seg.inc(cm.s_s, 1, eng="scalar")
                seg.wait("tensor", cm.s_s)
                if oc == 0 and half == 0 and nsum_read_v:
                    seg.wait("tensor", cm.s_v, nsum_read_v)
                seg.op("tensor", "matmul", nsum_ps[:], cm.ones_col[:], sqr[sp][:],
                       start=(oc == 0 and half == 0), stop=(oc == 3 and half == 1))
                seg.inc(cm.s_t, 1, eng="tensor")
                sq_read_t[sp] = cm.s_t.n
            kb_dma_done[par] = (kb_dma_before, cm.s_s.n)
        seg.wait("vector", cm.s_t)
        seg.op("vector", "tensor_copy", out=rows[0:1, T + tc * 512: T + (tc + 1) * 512], in_=nsum_ps[:])
        seg.inc(cm.s_v, 1, eng="vector")
        nsum_read_v = cm.s_v.n

    seg.wait("scalar", cm.s_v, nsum_read_v)
    seg.op("scalar", "activation", out=rows[0:1, 0:2 * T], in_=rows[0:1, 0:2 * T], func=AF.Sqrt)
    seg.inc(cm.s_s, 1, eng="scalar")
    sqrt_done_s = cm.s_s.n
    seg.wait("gpsimd", cm.s_s, sqrt_done_s)
    seg.dma("gpsimd", cn_in, rows[0:1, T:2 * T], s_ex)

    s_cc = seg.sem(name + "scc")
    s_cn = seg.sem(name + "scn")
    seg.wait("gpsimd", s_ex, s_ex.n)
    seg.op("gpsimd", "collective_compute", "AllGather", OP.bypass, replica_groups=rg,
           ins=[cc_in[2 * 1024 * T:2 * 1024 * T + 2048][None, :].opt()],
           outs=[ccn_out[None, :].opt()])
    seg.inc(s_cn, None, eng="gpsimd")
    seg.wait("gpsimd", s_exk[0], s_exk[0].n)
    seg.wait("gpsimd", s_exk[1], s_exk[1].n)
    seg.op("gpsimd", "collective_compute", "AllGather", OP.bypass, replica_groups=rg,
           ins=[cc_in[0:1024 * T].rearrange("(a b) -> a b", b=1024).opt()],
           outs=[cck_out.rearrange("(a b) -> a b", b=1024).opt()])
    seg.inc(s_cc, None, eng="gpsimd")

    es2t.close()
    wv_sb = seg.sbuf(name + "wv", [128, NC * 1024], BF16, stack=es2)
    seg.wait("gpsimd", cm.s_t)
    seg.wait("gpsimd", cm.s_v)
    seg.dma("gpsimd", wv_sb[:].rearrange("p (c m) -> p c m", c=NC),
            wv.rearrange("(c p) m -> p c m", p=128), s_wq)
    wv_loaded = s_wq.n
    seg.wait("tensor", s_wq, wv_loaded)
    vb = [seg.sbuf(name + f"vb{i}", [128, 512], BF16, stack=es2) for i in range(2)]
    vb_dma = [0, 0]
    v_grp_read = [ps_read.get(0, 0), ps_read.get(1, 0)]
    for tt in range(NKT_OWN):
        for vc in range(2):
            par = vc
            if v_grp_read[par]:
                seg.wait("tensor", cm.s_v, v_grp_read[par])
            for c in range(NC):
                seg.op("tensor", "matmul", e_ps[par][:],
                       h[:, c * T + tt * 128: c * T + (tt + 1) * 128],
                       wv_sb[:, c * 1024 + vc * 512: c * 1024 + (vc + 1) * 512],
                       start=(c == 0), stop=(c == NC - 1))
            seg.inc(cm.s_t, 1, eng="tensor")
            seg.wait("vector", cm.s_t)
            if vb_dma[par]:
                seg.wait("vector", s_exv[par], vb_dma[par])
            seg.op("vector", "tensor_copy", out=vb[par][:], in_=e_ps[par][:])
            seg.inc(cm.s_v, 1, eng="vector")
            v_grp_read[par] = cm.s_v.n
            seg.wait("sync", cm.s_v, v_grp_read[par])
            seg.dma("sync", cv_in[tt * 128:(tt + 1) * 128, vc * 512:(vc + 1) * 512], vb[par][:], s_exv[par])
            vb_dma[par] = s_exv[par].n

    s_ccv = seg.sem(name + "sccv")
    seg.wait("gpsimd", s_exv[0], s_exv[0].n)
    seg.wait("gpsimd", s_exv[1], s_exv[1].n)
    seg.op("gpsimd", "collective_compute", "AllGather", OP.bypass, replica_groups=rg,
           ins=[cc_in[1024 * T:2 * 1024 * T].rearrange("(a b) -> a b", b=1024).opt()],
           outs=[ccv_out.rearrange("(a b) -> a b", b=1024).opt()])
    seg.inc(s_ccv, None, eng="gpsimd")

    kn_all = seg.sbuf(name + "knall", [1, 2 * T], BF16, stack=es2)
    mrow = seg.sbuf(name + "mrow", [1, 16], F32, stack=es2)
    mrow_bf = seg.sbuf(name + "mrowbf", [1, 16], BF16, stack=es2)
    seg.wait("sync", s_cn)
    seg.dma("sync", kn_all[0:1, 0:T], cn_out[0], cm.s_ld)
    seg.dma("sync", kn_all[0:1, T:2 * T], cn_out[1], cm.s_ld)
    seg.wait("vector", cm.s_ld)
    seg.op("vector", "reduce_max", out=mrow[0:1, 0:1], in_=kn_all[:], axis=AX.X)
    seg.chain("vector", cm.s_v)
    seg.op("vector", "tensor_scalar", out=rows[0:1, T:2 * T], in0=rows[0:1, 0:T],
           scalar1=mrow[0:1, 0:1], scalar2=float(SC), op0=OP.mult, op1=OP.mult)
    seg.chain("vector", cm.s_v)
    for qc in range(NQC):
        seg.op("vector", "reduce_max", out=mrow[0:1, 1 + qc: 2 + qc],
               in_=rows[0:1, T + qc * 512: T + (qc + 1) * 512], axis=AX.X)
    seg.chain("vector", cm.s_v)
    seg.op("vector", "tensor_scalar_mul", out=mrow_bf[0:1, 0:NQC + 1],
           in0=mrow[0:1, 0:NQC + 1], scalar1=-1.0)
    seg.inc(cm.s_v, 1, eng="vector")
    mneg_v = cm.s_v.n
    seg.wait("tensor", cm.s_v, mneg_v)
    for qc in range(NQC):
        seg.op("tensor", "matmul", ebias_ps[:, qc:qc + 1], cm.ones_row[:],
               mrow_bf[0:1, 1 + qc:2 + qc], start=True, stop=True)
    seg.inc(cm.s_t, 1, eng="tensor")
    seg.wait("scalar", cm.s_t)
    seg.op("scalar", "copy", out=ebias[:, 0:NQC], in_=ebias_ps[:, 0:NQC])
    seg.inc(cm.s_s, 1, eng="scalar")
    ebias_s = cm.s_s.n

    es2p.close()
    es2.close()

    es3 = ExitStack()
    P_sb = seg.sbuf(name + "P", [128, NKT * 512], BF16, stack=es3)
    kst = [seg.sbuf(name + f"kst{i}", [128, 1024], BF16, stack=es3) for i in range(3)]
    vst = [seg.sbuf(name + f"vst{i}", [128, 512], BF16, stack=es3) for i in range(3)]
    rb_sb = seg.sbuf(name + "rbsb", [128, 512], F32, stack=es3)
    rcp_bf = seg.sbuf(name + "rcpbf", [1, 512], BF16, stack=es3)
    st_ps = [seg.psum(name + f"stps{i}", [128, 512], F32, stack=es3) for i in range(2)]
    av_ps = [seg.psum(name + f"avps{i}", [128, 512], F32, stack=es3) for i in range(4)]
    sum_ps = seg.psum(name + "sums", [1, 512], F32, stack=es3)
    s_kk = [seg.sem(name + "skk0"), seg.sem(name + "skk1"), seg.sem(name + "skk2")]
    s_vv = [seg.sem(name + "svv0"), seg.sem(name + "svv1"), seg.sem(name + "svv2")]
    kst_cons = [0, 0, 0]
    vst_cons = [0, 0, 0]
    st_read = [0, 0]
    av_read_v = 0
    sum_read_v = 0
    rb_read_v = 0
    rcp_read_t = 0
    rb_read_s = 0

    gate_t = cm.s_t.n
    gate_v = cm.s_v.n
    gate_s = cm.s_s.n

    def load_k(kt, par):
        if kst_cons[par]:
            seg.wait("gpsimd", cm.s_s, kst_cons[par])
        else:
            seg.wait("gpsimd", cm.s_t, gate_t)
            seg.wait("gpsimd", cm.s_s, gate_s)
            seg.wait("gpsimd", cm.s_v, gate_v)
            seg.wait("gpsimd", s_cc, 1)
        r, t0 = (0, kt * 128) if kt < NKT_OWN else (1, (kt - NKT_OWN) * 128)
        seg.dma("gpsimd", kst[par][:].rearrange("p (c k) -> p c k", c=NC),
                ck_out[r][:, t0:t0 + 128].rearrange("(c p) k -> p c k", p=128),
                s_kk[par])

    def load_v(kt, dcg, par):
        if vst_cons[par]:
            seg.wait("gpsimd", cm.s_t, vst_cons[par])
        else:
            seg.wait("gpsimd", cm.s_t, gate_t)
            seg.wait("gpsimd", s_ccv, 1)
        r, t0 = (0, kt * 128) if kt < NKT_OWN else (1, (kt - NKT_OWN) * 128)
        seg.dma("gpsimd", vst[par][:], cv_out[r][t0:t0 + 128, dcg * 512:(dcg + 1) * 512], s_vv[par])

    for qc in range(NQC):
        q0 = qc * 512
        kv_cnt = {}
        for pre in range(min(3, NKT)):
            load_k(pre, pre)
            kv_cnt[pre] = s_kk[pre].n
        for kt in range(NKT):
            par = kt % 3
            spar = kt % 2
            seg.wait("tensor", s_kk[par], kv_cnt[kt])
            if st_read[spar]:
                seg.wait("tensor", cm.s_s, st_read[spar])
            for c in range(4):
                seg.op("tensor", "matmul", st_ps[spar][:], kst[par][:, c * 128:(c + 1) * 128],
                       qp_e[:, c * T + q0: c * T + q0 + 512], start=(c == 0), stop=False)
            for c in range(4):
                seg.op("tensor", "matmul", st_ps[spar][:], kst[par][:, (4 + c) * 128:(5 + c) * 128],
                       qp_o[:, c * T + q0: c * T + q0 + 512], start=False, stop=(c == 3))
            seg.inc(cm.s_t, 1, eng="tensor")
            seg.wait("scalar", cm.s_t)
            seg.wait("scalar", cm.s_s, ebias_s)
            seg.op("scalar", "activation", out=P_sb[:, kt * 512:(kt + 1) * 512], in_=st_ps[spar][:],
                   func=AF.Exp, scale=sc_ap, bias=ebias[:, qc:qc + 1])
            seg.inc(cm.s_s, 1, eng="scalar")
            st_read[spar] = cm.s_s.n
            kst_cons[par] = cm.s_s.n
            if kt + 3 < NKT:
                load_k(kt + 3, par)
                kv_cnt[kt + 3] = s_kk[par].n
        p_done_s = cm.s_s.n

        seg.wait("tensor", cm.s_s, p_done_s)
        if sum_read_v:
            seg.wait("tensor", cm.s_v, sum_read_v)
        for kt in range(NKT):
            seg.op("tensor", "matmul", sum_ps[:], cm.ones_col[:], P_sb[:, kt * 512:(kt + 1) * 512],
                   start=(kt == 0), stop=(kt == NKT - 1))
        seg.inc(cm.s_t, 1, eng="tensor")
        seg.wait("vector", cm.s_t)
        seg.op("vector", "reciprocal", out=rows[0:1, q0: q0 + 512], in_=sum_ps[:])
        seg.chain("vector", cm.s_v)
        if rcp_read_t:
            seg.wait("vector", cm.s_t, rcp_read_t)
        seg.op("vector", "tensor_copy", out=rcp_bf[:], in_=rows[0:1, q0: q0 + 512])
        seg.inc(cm.s_v, 1, eng="vector")
        sum_read_v = cm.s_v.n
        seg.wait("tensor", cm.s_v, sum_read_v)
        seg.op("tensor", "matmul", st_ps[0][:], cm.ones_row[:], rcp_bf[:], start=True, stop=True)
        seg.inc(cm.s_t, 1, eng="tensor")
        rcp_read_t = cm.s_t.n
        seg.wait("scalar", cm.s_t)
        if rb_read_v:
            seg.wait("scalar", cm.s_v, rb_read_v)
        seg.op("scalar", "copy", out=rb_sb[:], in_=st_ps[0][:])
        seg.inc(cm.s_s, 1, eng="scalar")
        rb_read_s = cm.s_s.n
        st_read[0] = rb_read_s

        for dcg in range(2):
            vv_cnt = {}
            for pre in range(min(3, NKT)):
                load_v(pre, dcg, pre)
                vv_cnt[pre] = s_vv[pre].n
            for kt in range(NKT):
                par = kt % 3
                seg.wait("tensor", s_vv[par], vv_cnt[kt])
                if kt < 2 and av_read_v:
                    seg.wait("tensor", cm.s_v, av_read_v)
                for dd in range(4):
                    seg.op("tensor", "matmul", av_ps[dd][:],
                           vst[par][:, dd * 128:(dd + 1) * 128],
                           P_sb[:, kt * 512:(kt + 1) * 512],
                           start=(kt == 0), stop=(kt == NKT - 1))
                seg.inc(cm.s_t, 1, eng="tensor")
                vst_cons[par] = cm.s_t.n
                if kt + 3 < NKT:
                    load_v(kt + 3, dcg, par)
                    vv_cnt[kt + 3] = s_vv[par].n
            av_done_t = cm.s_t.n
            seg.wait("vector", cm.s_t, av_done_t)
            seg.wait("vector", cm.s_s, rb_read_s)
            for dd in range(4):
                dc = dcg * 4 + dd
                tmp = eo_sb[dd % 2]
                seg.op("vector", "tensor_tensor", out=tmp[:], in0=av_ps[dd][:], in1=rb_sb[:], op=OP.mult)
                seg.chain("vector", cm.s_v)
                xsl = slice(dc * T + q0, dc * T + q0 + 512)
                seg.op("vector", "scalar_tensor_tensor", out=xs[:, xsl], in0=tmp[:],
                       scalar=bv_sb[:, dc:dc + 1], in1=xs[:, xsl], op0=OP.add, op1=OP.add)
                seg.chain("vector", cm.s_v)
            av_read_v = cm.s_v.n
        rb_read_v = cm.s_v.n

    es3.close()

    if standalone:
        seg.wait("sync", cm.s_v)
        for c in range(NC):
            seg.dma("sync", x_out[c * 128:(c + 1) * 128, :], xs[:, c * T:(c + 1) * T], cm.s_st)
        seg.wait("sync", cm.s_st)
        seg.emit()
        return seg
    return None


def host_prep_s1(inputs):
    wq = inputs["L0_attn_a_Wq"]; wk = inputs["L0_attn_a_Wk"]; wv = inputs["L0_attn_a_Wv"]
    bq = inputs["L0_attn_a_bq"]; bk = inputs["L0_attn_a_bk"]; bv = inputs["L0_attn_a_bv"]
    wqT = wq.T; wkT = wk.T
    b_eo = np.concatenate([bq[0::2], bq[1::2], bk[0::2], bk[1::2]]).astype(np.float32)
    d = {
        "wqT_e": np.ascontiguousarray(wqT[:, 0::2]),
        "wqT_o": np.ascontiguousarray(wqT[:, 1::2]),
        "wkT_e": np.ascontiguousarray(wkT[:, 0::2]),
        "wkT_o": np.ascontiguousarray(wkT[:, 1::2]),
        "wvT": np.ascontiguousarray(wv.T),
        "b_eo": np.ascontiguousarray(b_eo.reshape(16, 128).T),
        "bv2": np.ascontiguousarray(bv.astype(np.float32).reshape(8, 128).T),
    }
    HEAD_SIZE = 512
    S = 4096
    pos = np.arange(S, dtype=np.float32)
    inv = 10000.0 ** (np.arange(0, HEAD_SIZE, 2, dtype=np.float32) / HEAD_SIZE)
    radians = pos[:, None] / inv
    sincos = np.stack([np.sin(radians), np.cos(radians)], axis=-1).reshape(S, -1)
    t = np.repeat(sincos, 2, axis=-1)
    d["t_full_e"] = np.ascontiguousarray(t[:, 0::2].T).astype(ml_dtypes.bfloat16)
    d["t_full_o"] = np.ascontiguousarray(t[:, 1::2].T).astype(ml_dtypes.bfloat16)
    return d



def build_s2(nc, standalone=True, xs=None, cm=None, seg=None, T=2048, name="s2",
             dram_prefix="", io_dram=None):
    NC = 8
    HIDC = 24
    TH = T // 2
    S = 1.0 / 127.0

    if standalone:
        seg = Seg(nc)
        x_in = nc.dram_tensor("x_in", [1024, T], F32, kind="ExternalInput").ap()
        x_out = nc.dram_tensor("x_out", [1024, T], F32, kind="ExternalOutput").ap()
    wgT = nc.dram_tensor(dram_prefix + "wgT", [1024, 3072], I32, kind="ExternalInput").ap()
    wuT = nc.dram_tensor(dram_prefix + "wuT", [1024, 3072], I32, kind="ExternalInput").ap()
    wdT = nc.dram_tensor(dram_prefix + "wdT", [3072, 1024], I32, kind="ExternalInput").ap()

    if standalone:
        cm = Common(seg)
        xs = seg.sbuf("xs", [128, NC * T], F32)
        for c in range(NC):
            seg.dma("sync", xs[:, c * T:(c + 1) * T], x_in[c * 128:(c + 1) * 128, :], cm.s_ld)

    es1 = ExitStack()
    h = seg.sbuf(name + "h", [128, NC * T], BF16)

    ln = emit_ln_fm(seg, cm, xs, T, h, es1, out_scale=S, name=name + "ln",
                    xs_ready=[(cm.s_ld, cm.s_ld.n)])
    h_ready = ln["h_v"]
    es1.close()

    es2 = ExitStack()
    mid = seg.sbuf(name + "mid", [128, HIDC * TH], BF16, stack=es2)
    wg_sb = [seg.sbuf(f"{name}wg{i}", [128, NC * 128], BF16, stack=es2) for i in range(2)]
    wu_sb = [seg.sbuf(f"{name}wu{i}", [128, NC * 128], BF16, stack=es2) for i in range(2)]
    wd_sb = [seg.sbuf(f"{name}wd{i}", [128, HIDC * 128], BF16, stack=es2) for i in range(2)]
    sg_sb = [seg.sbuf(f"{name}sg{i}", [128, 512], BF16, stack=es2) for i in range(2)]
    gate_ps = [seg.psum(f"{name}gps{i}", [128, 512], F32, stack=es2) for i in range(2)]
    up_ps = [seg.psum(f"{name}ups{i}", [128, 512], F32, stack=es2) for i in range(2)]
    out_ps = [seg.psum(f"{name}ops{i}", [128, 512], F32, stack=es2) for i in range(2)]

    seg.wait("tensor", cm.s_v, h_ready)
    s_gu = [seg.sem(name + "sgu0"), seg.sem(name + "sgu1")]
    s_wd = [seg.sem(name + "swd0"), seg.sem(name + "swd1")]

    wg_consumed = [ln["t_done"], ln["t_done"]]
    wd_consumed = [ln["t_done"], ln["t_done"]]
    sg_done_v = [0, 0]
    gate_read_s = [0, 0]
    up_read_v = [0, 0]
    out_read_v = [0, 0]

    def load_guw(mc, par):
        if wg_consumed[par]:
            seg.wait("gpsimd", cm.s_t, wg_consumed[par])
        seg.dma("gpsimd", wg_sb[par][:],
                wgT[:, mc * 128:(mc + 1) * 128].rearrange("(c p) m -> p c m", p=128), s_gu[par])
        seg.dma("gpsimd", wu_sb[par][:],
                wuT[:, mc * 128:(mc + 1) * 128].rearrange("(c p) m -> p c m", p=128), s_gu[par])

    def load_wd(dc, par):
        if wd_consumed[par]:
            seg.wait("gpsimd", cm.s_t, wd_consumed[par])
        seg.dma("gpsimd", wd_sb[par][:],
                wdT[:, dc * 128:(dc + 1) * 128].rearrange("(c p) m -> p c m", p=128), s_wd[par])

    for th in range(2):
        tb = th * TH
        load_guw(0, 0)
        w_cnt_after = [0] * HIDC
        w_cnt_after[0] = s_gu[0].n
        load_guw(1, 1)
        w_cnt_after[1] = s_gu[1].n
        for mc in range(HIDC):
            par = mc % 2
            seg.wait("tensor", s_gu[par], w_cnt_after[mc])
            for tc in range(TH // 512):
                t0 = tb + tc * 512
                hsl = lambda c: slice(c * T + t0, c * T + t0 + 512)
                if gate_read_s[tc]:
                    seg.wait("tensor", cm.s_s, gate_read_s[tc])
                for c in range(NC):
                    seg.op("tensor", "matmul", gate_ps[tc][:], wg_sb[par][:, c * 128:(c + 1) * 128],
                           h[:, hsl(c)], start=(c == 0), stop=(c == NC - 1))
                seg.inc(cm.s_t, 1, eng="tensor")
                gate_grp = cm.s_t.n
                if up_read_v[tc]:
                    seg.wait("tensor", cm.s_v, up_read_v[tc])
                for c in range(NC):
                    seg.op("tensor", "matmul", up_ps[tc][:], wu_sb[par][:, c * 128:(c + 1) * 128],
                           h[:, hsl(c)], start=(c == 0), stop=(c == NC - 1))
                seg.inc(cm.s_t, 1, eng="tensor")
                up_grp = cm.s_t.n
                seg.wait("scalar", cm.s_t, gate_grp)
                if sg_done_v[tc]:
                    seg.wait("scalar", cm.s_v, sg_done_v[tc])
                seg.op("scalar", "activation", out=sg_sb[tc][:], in_=gate_ps[tc][:], func=AF.Silu)
                seg.inc(cm.s_s, 1, eng="scalar")
                gate_read_s[tc] = cm.s_s.n
                seg.wait("vector", cm.s_s)
                seg.wait("vector", cm.s_t, up_grp)
                seg.op("vector", "tensor_tensor", out=mid[:, mc * TH + tc * 512: mc * TH + tc * 512 + 512],
                       in0=up_ps[tc][:], in1=sg_sb[tc][:], op=OP.mult)
                seg.inc(cm.s_v, 1, eng="vector")
                sg_done_v[tc] = cm.s_v.n
                up_read_v[tc] = cm.s_v.n
            wg_consumed[par] = cm.s_t.n
            if mc + 2 < HIDC:
                load_guw(mc + 2, par)
                w_cnt_after[mc + 2] = s_gu[par].n
        mid_ready = cm.s_v.n

        load_wd(0, 0)
        wd_cnt = [0] * NC
        wd_cnt[0] = s_wd[0].n
        load_wd(1, 1)
        wd_cnt[1] = s_wd[1].n
        seg.wait("tensor", cm.s_v, mid_ready)
        for dc in range(NC):
            par = dc % 2
            seg.wait("tensor", s_wd[par], wd_cnt[dc])
            for tc in range(TH // 512):
                if out_read_v[tc]:
                    seg.wait("tensor", cm.s_v, out_read_v[tc])
                for kc in range(HIDC):
                    seg.op("tensor", "matmul", out_ps[tc][:], wd_sb[par][:, kc * 128:(kc + 1) * 128],
                           mid[:, kc * TH + tc * 512: kc * TH + tc * 512 + 512],
                           start=(kc == 0), stop=(kc == HIDC - 1))
                seg.inc(cm.s_t, 1, eng="tensor")
                seg.wait("vector", cm.s_t)
                xsl = slice(dc * T + tb + tc * 512, dc * T + tb + tc * 512 + 512)
                seg.op("vector", "scalar_tensor_tensor", out=xs[:, xsl], in0=out_ps[tc][:],
                       scalar=S, in1=xs[:, xsl], op0=OP.mult, op1=OP.add)
                seg.inc(cm.s_v, 1, eng="vector")
                out_read_v[tc] = cm.s_v.n
            wd_consumed[par] = cm.s_t.n
            if dc + 2 < NC:
                load_wd(dc + 2, par)
                wd_cnt[dc + 2] = s_wd[par].n

    if standalone:
        x_done = cm.s_v.n
        seg.wait("sync", cm.s_v, x_done)
        for c in range(NC):
            seg.dma("sync", x_out[c * 128:(c + 1) * 128, :], xs[:, c * T:(c + 1) * T], cm.s_st)
        seg.wait("sync", cm.s_st)
        seg.emit()
        return seg
    return None


def host_prep_s2(inputs):
    wgT = np.ascontiguousarray(inputs["L0_ffn_w_gate"].T)
    wuT = np.ascontiguousarray(inputs["L0_ffn_w_up"].T)
    wdT = np.ascontiguousarray(inputs["L0_ffn_w_down"].T)
    return wgT, wuT, wdT




def sched(qc):
    i0 = 4 * qc
    out = []
    for r in range(2):
        for j in range(i0 + 4):
            n0 = max(0, j - i0) * 128
            out.append((r, j, n0))
    return out


def build_s3(nc, standalone=True, xs=None, cm=None, seg=None, T=2048, name="s3",
             rg=RG_PAIRS, dram_prefix="", io_dram=None):
    NC = 8
    NQC = T // 512
    NKT_OWN = T // 128

    if standalone:
        seg = Seg(nc)
        x_in = nc.dram_tensor("x_in", [1024, T], F32, kind="ExternalInput").ap()
        x_out = nc.dram_tensor("x_out", [1024, T], F32, kind="ExternalOutput").ap()
    wq_d = nc.dram_tensor(dram_prefix + "wq3", [1024, 128], F32, kind="ExternalInput").ap()
    wk_d = nc.dram_tensor(dram_prefix + "wk3", [1024, 128], F32, kind="ExternalInput").ap()
    wv_d = nc.dram_tensor(dram_prefix + "wv3", [1024, 128], F32, kind="ExternalInput").ap()
    w1_d = nc.dram_tensor(dram_prefix + "w13", [1024, 1024], F32, kind="ExternalInput").ap()
    w2_d = nc.dram_tensor(dram_prefix + "w23", [1024, 1024], F32, kind="ExternalInput").ap()
    gnw_d = nc.dram_tensor(dram_prefix + "gnw2", [128, 8], F32, kind="ExternalInput").ap()
    gnb_d = nc.dram_tensor(dram_prefix + "gnb2", [128, 8], F32, kind="ExternalInput").ap()
    dec_d = nc.dram_tensor(dram_prefix + "dec", [128, 1024], F32, kind="ExternalInput").ap()
    dmask_d = nc.dram_tensor(dram_prefix + "dmask", [128, 256], BF16, kind="ExternalInput").ap()
    rfac_d = nc.dram_tensor(dram_prefix + "rfac", [8, T], BF16, kind="ExternalInput").ap()
    CCN = 2 * 1024 * T
    cc_in = nc.dram_tensor(name + "cc_in", [CCN], BF16, kind="Internal").ap()
    cc_out = nc.dram_tensor(name + "cc_out", [2 * CCN], BF16, kind="Internal").ap()
    ck_in = cc_in[0:1024 * T].rearrange("(a b) -> a b", b=T)
    cv_in = cc_in[1024 * T:2 * 1024 * T].rearrange("(a b) -> a b", b=1024)
    ck_out = [cc_out[r * CCN: r * CCN + 1024 * T].rearrange("(a b) -> a b", b=T) for r in range(2)]
    cv_out = [cc_out[r * CCN + 1024 * T: (r + 1) * CCN].rearrange("(a b) -> a b", b=1024) for r in range(2)]

    if standalone:
        cm = Common(seg)
        xs = seg.sbuf("xs", [128, NC * T], F32)
        for c in range(NC):
            seg.dma("sync", xs[:, c * T:(c + 1) * T], x_in[c * 128:(c + 1) * 128, :], cm.s_ld)

    q_sb = seg.sbuf(name + "q", [128, NC * T], BF16)
    g_sb = seg.sbuf(name + "g", [128, NC * T], BF16)
    gnw_sb = seg.sbuf(name + "gnw", [128, 8], F32)
    gnb_sb = seg.sbuf(name + "gnb", [128, 8], BF16)
    gbw2_sb = seg.sbuf(name + "gbw2", [128, 8], F32)
    dec_sb = seg.sbuf(name + "dec", [128, 1024], F32)
    dmask_sb = seg.sbuf(name + "dmask", [128, 256], BF16)

    es_ln = ExitStack()
    h = seg.sbuf(name + "h", [128, NC * T], BF16)
    ln = emit_ln_fm(seg, cm, xs, T, h, es_ln, name=name + "ln",
                    xs_ready=[(cm.s_ld, cm.s_ld.n)])
    es_ln.close()

    es_w1 = ExitStack()
    wq_sb = seg.sbuf(name + "wq", [128, 8 * 128], BF16)
    wk_sb = seg.sbuf(name + "wk", [128, 8 * 128], BF16)
    wv_sb = seg.sbuf(name + "wv", [128, 8 * 128], BF16)
    w1_sb = seg.sbuf(name + "w1", [128, 8 * 1024], BF16, stack=es_w1)
    s_w = seg.sem(name + "sw")
    seg.wait("gpsimd", cm.s_t, ln["t_done"])
    seg.wait("gpsimd", cm.s_v, ln["v_done"])
    for sb, d in ((wq_sb, wq_d), (wk_sb, wk_d), (wv_sb, wv_d)):
        seg.dma("gpsimd", sb[:].rearrange("p (c e) -> p c e", c=8),
                d.rearrange("(c p) e -> p c e", p=128), s_w)
    seg.dma("gpsimd", w1_sb[:].rearrange("p (c m) -> p c m", c=8),
            w1_d.rearrange("(c p) m -> p c m", p=128), s_w)
    seg.dma("gpsimd", gnw_sb[:], gnw_d[:], s_w)
    seg.dma("gpsimd", gnb_sb[:], gnb_d[:], s_w)
    seg.dma("gpsimd", dec_sb[:], dec_d[:], s_w)
    seg.dma("gpsimd", dmask_sb[:], dmask_d[:], s_w)
    w_loaded = s_w.n

    es_p = ExitStack()
    p_ps = [seg.psum(name + f"pps{i}", [128, 512], F32, stack=es_p) for i in range(2)]
    seg.wait("tensor", s_w, w_loaded)
    seg.wait("tensor", cm.s_v, ln["h_v"])
    seg.wait("tensor", cm.s_init)
    ps_read = [0, 0]

    for hh in range(NC):
        for tc in range(NQC):
            par = (hh * NQC + tc) % 2
            if ps_read[par]:
                seg.wait("tensor", cm.s_s, ps_read[par])
            seg.op("tensor", "matmul", p_ps[par][:], wq_sb[:, hh * 128:(hh + 1) * 128],
                   h[:, hh * T + tc * 512: hh * T + (tc + 1) * 512], start=True, stop=True)
            seg.inc(cm.s_t, 1, eng="tensor")
            seg.wait("scalar", cm.s_t)
            seg.op("scalar", "copy", out=q_sb[:, hh * T + tc * 512: hh * T + (tc + 1) * 512],
                   in_=p_ps[par][:])
            seg.inc(cm.s_s, 1, eng="scalar")
            ps_read[par] = cm.s_s.n

    s_exk = [seg.sem(name + "sexk0"), seg.sem(name + "sexk1")]
    kb = [seg.sbuf(name + f"kb{i}", [128, 512], BF16, stack=es_w1) for i in range(2)]
    kb_dma = [0, 0]
    for hh in range(NC):
        for tc in range(NQC):
            par = (hh * NQC + tc) % 2
            if ps_read[par]:
                seg.wait("tensor", cm.s_s, ps_read[par])
            seg.op("tensor", "matmul", p_ps[par][:], wk_sb[:, hh * 128:(hh + 1) * 128],
                   h[:, hh * T + tc * 512: hh * T + (tc + 1) * 512], start=True, stop=True)
            seg.inc(cm.s_t, 1, eng="tensor")
            seg.wait("scalar", cm.s_t)
            if kb_dma[par]:
                seg.wait("scalar", s_exk[par], kb_dma[par])
            seg.op("scalar", "copy", out=kb[par][:], in_=p_ps[par][:])
            seg.inc(cm.s_s, 1, eng="scalar")
            ps_read[par] = cm.s_s.n
            seg.wait("sync", cm.s_s, ps_read[par])
            seg.dma("sync", ck_in[hh * 128:(hh + 1) * 128, tc * 512:(tc + 1) * 512], kb[par][:], s_exk[par])
            kb_dma[par] = s_exk[par].n

    s_exv = [seg.sem(name + "sexv0"), seg.sem(name + "sexv1")]
    vb = [seg.sbuf(name + f"vb{i}", [128, 1024], BF16, stack=es_w1) for i in range(2)]
    vb_dma = [0, 0]
    for tt in range(NKT_OWN):
        par = tt % 2
        for hh in range(NC):
            spar = (tt * NC + hh) % 2
            if ps_read[spar]:
                seg.wait("tensor", cm.s_s, ps_read[spar])
            seg.op("tensor", "matmul", p_ps[spar][:, 0:128],
                   h[:, hh * T + tt * 128: hh * T + (tt + 1) * 128],
                   wv_sb[:, hh * 128:(hh + 1) * 128], start=True, stop=True)
            seg.inc(cm.s_t, 1, eng="tensor")
            seg.wait("scalar", cm.s_t)
            if hh == 0 and vb_dma[par]:
                seg.wait("scalar", s_exv[par], vb_dma[par])
            seg.op("scalar", "copy", out=vb[par][:, hh * 128:(hh + 1) * 128], in_=p_ps[spar][:, 0:128])
            seg.inc(cm.s_s, 1, eng="scalar")
            ps_read[spar] = cm.s_s.n
        seg.wait("sync", cm.s_s)
        seg.dma("sync", cv_in[tt * 128:(tt + 1) * 128, :], vb[par][:], s_exv[par])
        vb_dma[par] = s_exv[par].n

    for oc in range(NC):
        for tc in range(NQC):
            par = (oc * NQC + tc) % 2
            if ps_read[par]:
                seg.wait("tensor", cm.s_s, ps_read[par])
            for c in range(NC):
                seg.op("tensor", "matmul", p_ps[par][:],
                       w1_sb[:, c * 1024 + oc * 128: c * 1024 + oc * 128 + 128],
                       h[:, c * T + tc * 512: c * T + (tc + 1) * 512],
                       start=(c == 0), stop=(c == NC - 1))
            seg.inc(cm.s_t, 1, eng="tensor")
            seg.wait("scalar", cm.s_t)
            seg.op("scalar", "activation", out=g_sb[:, oc * T + tc * 512: oc * T + (tc + 1) * 512],
                   in_=p_ps[par][:], func=AF.Silu)
            seg.inc(cm.s_s, 1, eng="scalar")
            ps_read[par] = cm.s_s.n
    gated_done_s = cm.s_s.n

    s_cc = seg.sem(name + "scc")
    for s in (s_exk[0], s_exk[1], s_exv[0], s_exv[1]):
        seg.wait("gpsimd", s, s.n)
    seg.op("gpsimd", "collective_compute", "AllGather", OP.bypass, replica_groups=rg,
           ins=[cc_in.rearrange("(a b) -> a b", b=1024).opt()],
           outs=[cc_out.rearrange("(a b) -> a b", b=1024).opt()])
    seg.inc(s_cc, None, eng="gpsimd")

    es_w1.close()
    es_w2 = ExitStack()
    w2_sb = seg.sbuf(name + "w2", [128, 8 * 1024], BF16, stack=es_w2)
    seg.wait("gpsimd", cm.s_t)
    seg.wait("gpsimd", cm.s_s)
    seg.dma("gpsimd", w2_sb[:].rearrange("p (c m) -> p c m", c=8),
            w2_d.rearrange("(c p) m -> p c m", p=128), s_w)
    w2_loaded = s_w.n

    es_p.close()
    es_pa = ExitStack()
    st_ps = [seg.psum(name + f"stps{i}", [128, 512], F32, stack=es_pa) for i in range(2)]
    av_ps = [seg.psum(name + f"avps{i}", [128, 512], F32, stack=es_pa) for i in range(4)]
    bc_ps = [seg.psum(name + f"bcps{i}", [128, 512], F32, stack=es_pa) for i in range(2)]
    seg.wait("tensor", s_w, w2_loaded)
    for dc in range(NC):
        for kc in range(NC):
            seg.op("tensor", "matmul", bc_ps[0][:, dc:dc + 1],
                   w2_sb[:, kc * 1024 + dc * 128: kc * 1024 + dc * 128 + 128],
                   gnb_sb[:, kc:kc + 1], start=(kc == 0), stop=(kc == NC - 1))
    seg.inc(cm.s_t, 1, eng="tensor")
    seg.wait("scalar", cm.s_t)
    seg.op("scalar", "copy", out=gbw2_sb[:], in_=bc_ps[0][:, 0:8])
    seg.inc(cm.s_s, 1, eng="scalar")
    gbw2_s = cm.s_s.n

    k4 = [seg.sbuf(name + f"k4_{i}", [128, 4 * 128], BF16, stack=es_w2) for i in range(2)]
    v4 = [seg.sbuf(name + f"v4_{i}", [128, 512], BF16, stack=es_w2) for i in range(2)]
    P = [seg.sbuf(name + f"P{i}", [128, 512], BF16, stack=es_w2) for i in range(2)]
    ret_sb = seg.sbuf(name + "ret", [128, 512], BF16, stack=es_w2)
    sq_sb = seg.sbuf(name + "sq", [128, 512], BF16, stack=es_w2)
    tmp_sb = seg.sbuf(name + "tmp", [128, 512], BF16, stack=es_w2)
    rfb_sb = seg.sbuf(name + "rfb", [128, 512], F32, stack=es_w2)
    rrow = seg.sbuf(name + "rrow", [1, 512], BF16, stack=es_w2)
    srow = seg.sbuf(name + "srow", [1, 3 * 512], F32, stack=es_w2)
    srow_bf = seg.sbuf(name + "srowbf", [1, 2 * 512], BF16, stack=es_w2)
    s_kv = [seg.sem(name + "skv0"), seg.sem(name + "skv1"), seg.sem(name + "skv2")]
    kv_cons = [0, 0, 0]
    st_read = [0, 0]
    P_read = [0, 0]
    av_read = [0, 0, 0, 0]
    acc = h

    def load_kv(r, j, hg, par):
        if kv_cons[par]:
            seg.wait("gpsimd", cm.s_t, kv_cons[par])
        else:
            seg.wait("gpsimd", cm.s_s, gated_done_s)
        seg.wait("gpsimd", s_cc, 1)
        seg.dma("gpsimd", k4[par][:].rearrange("p (c k) -> p c k", c=4),
                ck_out[r][hg * 512:(hg + 1) * 512, j * 128:(j + 1) * 128].rearrange("(c p) k -> p c k", p=128),
                s_kv[par])
        seg.dma("gpsimd", v4[par][:], cv_out[r][j * 128:(j + 1) * 128, hg * 512:(hg + 1) * 512], s_kv[par])

    rrow_t = 0
    post_v = 0
    post_s = 0
    for qc in range(NQC):
        i0 = 4 * qc
        q0 = qc * 512
        sl = sched(qc)
        for hg in range(2):
            kv_cnt = {}
            for pre in range(min(2, len(sl))):
                load_kv(sl[pre][0], sl[pre][1], hg, pre)
                kv_cnt[pre] = s_kv[pre].n
            pend = []

            def emit_av(e):
                m_, par_, ppar_, n0_, si_, gate_eng, gate_cnt = e
                if gate_eng == 'v':
                    seg.wait("tensor", cm.s_v, gate_cnt)
                else:
                    seg.wait("tensor", cm.s_s, gate_cnt)
                if si_ == 0 and av_read[m_]:
                    seg.wait("tensor", cm.s_v, av_read[m_])
                seg.op("tensor", "matmul", av_ps[m_][:, n0_:512],
                       v4[par_][:, m_ * 128:(m_ + 1) * 128],
                       P[ppar_][:, n0_:512], start=(si_ == 0), stop=(si_ == len(sl) - 1))
                seg.inc(cm.s_t, 1, eng="tensor")
                P_read[ppar_] = cm.s_t.n
                kv_cons[par_] = cm.s_t.n

            for si, (r, j, n0) in enumerate(sl):
                par = si % 2
                seg.wait("tensor", s_kv[par], kv_cnt[si])
                for m in range(4):
                    hh = hg * 4 + m
                    spar = m % 2
                    if st_read[spar]:
                        seg.wait("tensor", cm.s_s, st_read[spar])
                    seg.op("tensor", "matmul", st_ps[spar][:, n0:512],
                           k4[par][:, m * 128:(m + 1) * 128],
                           q_sb[:, hh * T + q0 + n0: hh * T + q0 + 512], start=True, stop=True)
                    seg.inc(cm.s_t, 1, eng="tensor")
                    while len(pend) > 1:
                        emit_av(pend.pop(0))
                    seg.wait("scalar", cm.s_t)
                    ppar = m % 2
                    if P_read[ppar]:
                        seg.wait("scalar", cm.s_t, P_read[ppar])
                    dcol = ((hh * NQC + qc) * 2 + r) * 16 + j
                    seg.op("scalar", "activation", out=P[ppar][:, n0:512], in_=st_ps[spar][:, n0:512],
                           func=AF.Identity, scale=dec_sb[:, dcol:dcol + 1])
                    seg.inc(cm.s_s, 1, eng="scalar")
                    st_read[spar] = cm.s_s.n
                    pm_done = cm.s_s.n
                    if j >= i0:
                        c0 = (j - i0) * 128
                        seg.wait("vector", cm.s_s, pm_done)
                        seg.op("vector", "tensor_tensor", out=P[ppar][:, c0:c0 + 128],
                               in0=P[ppar][:, c0:c0 + 128],
                               in1=dmask_sb[:, r * 128:(r + 1) * 128], op=OP.mult)
                        seg.inc(cm.s_v, 1, eng="vector")
                        pend.append((m, par, ppar, n0, si, 'v', cm.s_v.n))
                    else:
                        pend.append((m, par, ppar, n0, si, 's', pm_done))
                if si + 2 < len(sl):
                    while pend:
                        emit_av(pend.pop(0))
                    load_kv(sl[si + 2][0], sl[si + 2][1], hg, par)
                    kv_cnt[si + 2] = s_kv[par].n
            while pend:
                emit_av(pend.pop(0))
            av_done_t = cm.s_t.n

            for m in range(4):
                hh = hg * 4 + m
                if rrow_t:
                    seg.wait("sync", cm.s_t, rrow_t)
                seg.dma("sync", rrow[:], rfac_d[hh:hh + 1, q0:q0 + 512], cm.s_ld)
                seg.wait("tensor", cm.s_ld)
                if post_s:
                    seg.wait("tensor", cm.s_s, post_s)
                if post_v:
                    seg.wait("tensor", cm.s_v, post_v)
                seg.op("tensor", "matmul", bc_ps[0][:], cm.ones_row[:], rrow[:], start=True, stop=True)
                seg.inc(cm.s_t, 1, eng="tensor")
                rrow_t = cm.s_t.n
                seg.wait("scalar", cm.s_t)
                if post_v:
                    seg.wait("scalar", cm.s_v, post_v)
                seg.op("scalar", "copy", out=rfb_sb[:], in_=bc_ps[0][:])
                seg.inc(cm.s_s, 1, eng="scalar")
                seg.wait("vector", cm.s_s)
                seg.wait("vector", cm.s_t, av_done_t)
                seg.op("vector", "tensor_tensor", out=ret_sb[:], in0=av_ps[m][:], in1=rfb_sb[:], op=OP.mult)
                seg.chain("vector", cm.s_v)
                seg.wait("scalar", cm.s_v)
                seg.op("scalar", "activation", out=sq_sb[:], in_=ret_sb[:], func=AF.Square)
                seg.inc(cm.s_s, 1, eng="scalar")
                seg.wait("tensor", cm.s_s)
                seg.op("tensor", "matmul", bc_ps[1][0:1, :], cm.ones_col[:], ret_sb[:], start=True, stop=True)
                seg.op("tensor", "matmul", av_ps[m][0:1, :], cm.ones_col[:], sq_sb[:], start=True, stop=True)
                seg.inc(cm.s_t, 1, eng="tensor")
                mu = srow[0:1, 0:512]; rstd = srow[0:1, 512:1024]
                var = srow[0:1, 1024:1536]; msq = srow[0:1, 1024:1536]
                seg.wait("vector", cm.s_t)
                seg.op("vector", "tensor_scalar_mul", out=mu, in0=bc_ps[1][0:1, :], scalar1=1.0 / 128)
                seg.chain("vector", cm.s_v)
                seg.op("vector", "tensor_tensor", out=msq, in0=mu, in1=mu, op=OP.mult)
                seg.chain("vector", cm.s_v)
                seg.op("vector", "scalar_tensor_tensor", out=var, in0=av_ps[m][0:1, :], scalar=1.0 / 128,
                       in1=msq, op0=OP.mult, op1=OP.subtract)
                seg.inc(cm.s_v, 1, eng="vector")
                seg.wait("scalar", cm.s_v)
                seg.op("scalar", "activation", out=rstd, in_=var, func=AF.Ln, bias=cm.eps1)
                seg.chain("scalar", cm.s_s)
                seg.op("scalar", "activation", out=rstd, in_=rstd, func=AF.Exp, scale=cm.mhalf1)
                seg.inc(cm.s_s, 1, eng="scalar")
                seg.wait("vector", cm.s_s)
                seg.op("vector", "tensor_copy", out=srow_bf[0:1, 0:512], in_=mu)
                seg.op("vector", "tensor_copy", out=srow_bf[0:1, 512:1024], in_=rstd)
                seg.inc(cm.s_v, 1, eng="vector")
                seg.wait("tensor", cm.s_v)
                seg.op("tensor", "matmul", bc_ps[0][:], cm.ones_row[:], srow_bf[0:1, 0:512], start=True, stop=True)
                seg.op("tensor", "matmul", bc_ps[1][:], cm.ones_row[:], srow_bf[0:1, 512:1024], start=True, stop=True)
                seg.inc(cm.s_t, 1, eng="tensor")
                seg.wait("vector", cm.s_t)
                seg.op("vector", "tensor_tensor", out=tmp_sb[:], in0=ret_sb[:], in1=bc_ps[0][:], op=OP.subtract)
                seg.chain("vector", cm.s_v)
                seg.op("vector", "tensor_tensor", out=tmp_sb[:], in0=tmp_sb[:], in1=bc_ps[1][:], op=OP.mult)
                seg.chain("vector", cm.s_v)
                seg.op("vector", "scalar_tensor_tensor", out=acc[:, hh * T + q0: hh * T + q0 + 512],
                       in0=tmp_sb[:], scalar=gnw_sb[:, hh:hh + 1],
                       in1=g_sb[:, hh * T + q0: hh * T + q0 + 512], op0=OP.mult, op1=OP.add)
                seg.inc(cm.s_v, 1, eng="vector")
                post_v = cm.s_v.n
                post_s = cm.s_s.n
                av_read[m] = cm.s_v.n
    acc_done_v = cm.s_v.n

    out_read = [0, 0]
    seg.wait("tensor", cm.s_v, acc_done_v)
    seg.wait("scalar", cm.s_s, gbw2_s)
    for dc in range(NC):
        for tc in range(NQC):
            par = (dc * NQC + tc) % 2
            if out_read[par]:
                seg.wait("tensor", cm.s_v, out_read[par])
            for kc in range(NC):
                seg.op("tensor", "matmul", st_ps[par][:],
                       w2_sb[:, kc * 1024 + dc * 128: kc * 1024 + dc * 128 + 128],
                       acc[:, kc * T + tc * 512: kc * T + (tc + 1) * 512],
                       start=(kc == 0), stop=(kc == NC - 1))
            seg.inc(cm.s_t, 1, eng="tensor")
            seg.wait("vector", cm.s_t)
            xsl = slice(dc * T + tc * 512, dc * T + (tc + 1) * 512)
            seg.op("vector", "scalar_tensor_tensor", out=xs[:, xsl], in0=st_ps[par][:],
                   scalar=gbw2_sb[:, dc:dc + 1], in1=xs[:, xsl], op0=OP.add, op1=OP.add)
            seg.inc(cm.s_v, 1, eng="vector")
            out_read[par] = cm.s_v.n

    es_pa.close()
    es_w2.close()

    if standalone:
        seg.wait("sync", cm.s_v)
        for c in range(NC):
            seg.dma("sync", x_out[c * 128:(c + 1) * 128, :], xs[:, c * T:(c + 1) * T], cm.s_st)
        seg.wait("sync", cm.s_st)
        seg.emit()
        return seg
    return None


def host_prep_s3(inputs, core, T=2048):
    p = core % 2
    wq = inputs["L0_attn_b_Wq"]; wk = inputs["L0_attn_b_Wk"]; wv = inputs["L0_attn_b_Wv"]
    d = {
        "wq3": np.ascontiguousarray(wq.reshape(1024, 128)),
        "wk3": np.ascontiguousarray(wk.reshape(1024, 128)),
        "wv3": np.ascontiguousarray(wv.reshape(1024, 128)),
        "w13": np.ascontiguousarray(inputs["L0_attn_b_W1"]),
        "w23": np.ascontiguousarray(inputs["L0_attn_b_W2"]),
        "gnw2": np.ascontiguousarray(inputs["L0_attn_b_gn_w"].reshape(8, 128).T),
        "gnb2": np.ascontiguousarray(inputs["L0_attn_b_gn_b"].reshape(8, 128).T),
    }
    lo = np.log(1.0 / 512.0); hi = np.log(1.0 / 32.0)
    gam = 1.0 - np.exp(np.linspace(lo, hi, 8))
    own = own_token_index(core)
    NQC = T // 512
    q0_abs = np.array([own[qc * 512] for qc in range(NQC)])
    dec = np.zeros((128, 1024), dtype=np.float32)
    for hh in range(8):
        for qc in range(NQC):
            for r in range(2):
                for j in range(16):
                    col = ((hh * NQC + qc) * 2 + r) * 16 + j
                    if j > 4 * qc + 3:
                        continue
                    k_abs = 128 * (2 * j + r) + np.arange(128)
                    expo = (q0_abs[qc] - k_abs).astype(np.float64)
                    dec[:, col] = (gam[hh] ** expo).astype(np.float32)
    d["dec"] = dec
    tri = (np.arange(128)[:, None] <= np.arange(128)[None, :]).astype(np.float32)
    m0 = tri if p == 0 else np.ones((128, 128), np.float32)
    m1 = np.zeros((128, 128), np.float32) if p == 0 else tri
    d["dmask"] = np.concatenate([m0, m1], axis=1).astype(ml_dtypes.bfloat16)
    rfac = np.zeros((8, T), dtype=np.float64)
    for hh in range(8):
        for qc in range(NQC):
            rfac[hh, qc * 512:(qc + 1) * 512] = gam[hh] ** (own[qc * 512:(qc + 1) * 512] - q0_abs[qc])
    d["rfac"] = rfac.astype(ml_dtypes.bfloat16)
    return d


L_STEEP = [2, 3, 6, 10]


def sched5(qc, steep):
    i0 = 4 * qc
    jmin = max(0, i0 - L_STEEP[3]) if steep else 0
    out = []
    for r in range(2):
        for j in range(jmin, i0 + 4):
            out.append((r, j, max(0, j - i0) * 128))
    return out


def build_s5(nc, standalone=True, xs=None, cm=None, seg=None, T=2048, name="s5",
             rg=RG_PAIRS, dram_prefix="", io_dram=None):
    NC = 8
    NQC = T // 512
    NKT_OWN = T // 128
    SC = 1.0 / 8.0

    if standalone:
        seg = Seg(nc)
        x_in = nc.dram_tensor("x_in", [1024, T], F32, kind="ExternalInput").ap()
        x_out = nc.dram_tensor("x_out", [1024, T], F32, kind="ExternalOutput").ap()
    wattn_d = nc.dram_tensor(dram_prefix + "wattn", [1024, 3072], F32, kind="ExternalInput").ap()
    battn_d = nc.dram_tensor(dram_prefix + "battn", [128, 16], F32, kind="ExternalInput").ap()
    bvrow_d = nc.dram_tensor(dram_prefix + "bvrow", [1, 1024], F32, kind="ExternalInput").ap()
    bprojf_d = nc.dram_tensor(dram_prefix + "bprojf", [128, 8], F32, kind="ExternalInput").ap()
    wproj_d = nc.dram_tensor(dram_prefix + "wproj", [1024, 1024], F32, kind="ExternalInput").ap()
    alibk_d = nc.dram_tensor(dram_prefix + "alibk", [128, 512], F32, kind="ExternalInput").ap()
    qabs_d = nc.dram_tensor(dram_prefix + "qabs", [1, T], F32, kind="ExternalInput").ap()
    slopeq_d = nc.dram_tensor(dram_prefix + "slopeq", [1, 64], F32, kind="ExternalInput").ap()
    dmask_d = nc.dram_tensor(dram_prefix + "dmask5", [128, 256], F32, kind="ExternalInput").ap()

    CCN = 2 * 1024 * T + 16 * T
    cc_in = nc.dram_tensor(name + "cc_in", [CCN], BF16, kind="Internal").ap()
    cc_out = nc.dram_tensor(name + "cc_out", [2 * CCN], BF16, kind="Internal").ap()
    ck_in = cc_in[0:1024 * T].rearrange("(a b) -> a b", b=T)
    cv_in = cc_in[1024 * T:2 * 1024 * T].rearrange("(a b) -> a b", b=1024)
    cn_in = cc_in[2 * 1024 * T:].rearrange("(a b) -> a b", b=T)
    ck_out = [cc_out[r * CCN: r * CCN + 1024 * T].rearrange("(a b) -> a b", b=T) for r in range(2)]
    cv_out = [cc_out[r * CCN + 1024 * T: r * CCN + 2 * 1024 * T].rearrange("(a b) -> a b", b=1024) for r in range(2)]
    cn_out = [cc_out[r * CCN + 2 * 1024 * T: (r + 1) * CCN].rearrange("(a b) -> a b", b=T) for r in range(2)]

    if standalone:
        cm = Common(seg)
    sc_ap = cm.constf(SC)

    battn_sb = seg.sbuf(name + "battn", [128, 16], F32)
    bproj_sb = seg.sbuf(name + "bproj", [128, 8], F32)
    btot_sb = seg.sbuf(name + "btot", [128, 8], F32)
    dmask_sb = seg.sbuf(name + "dmask", [128, 256], F32)
    ones64 = seg.sbuf(name + "ones64", [128, 64], BF16)
    qn_bf = seg.sbuf(name + "qnbf", [1, 4 * T], BF16)
    strow = seg.sbuf(name + "strow", [1, 160], F32)
    mrow_bf = seg.sbuf(name + "mrowbf", [1, 64], BF16)
    rowz = seg.sbuf(name + "rowz", [128, 512], F32)
    rz_bf = seg.sbuf(name + "rzbf", [128, 512], BF16)
    s_par = seg.sem(name + "spar")
    seg.op("vector", "memset", ones64[:], 1.0)
    seg.inc(cm.s_init, 1, eng="vector")
    for sb, d in ((battn_sb, battn_d), (bproj_sb, bprojf_d), (dmask_sb, dmask_d)):
        seg.dma("gpsimd", sb[:], d[:], s_par)
    par_n = s_par.n
    seg.wait("vector", s_par, par_n)
    seg.wait("scalar", s_par, par_n)
    seg.wait("tensor", s_par, par_n)

    if standalone:
        xs = seg.sbuf("xs", [128, NC * T], F32)
        for c in range(NC):
            seg.dma("sync", xs[:, c * T:(c + 1) * T], x_in[c * 128:(c + 1) * 128, :], cm.s_ld)

    q_sb = seg.sbuf(name + "q", [128, NC * T], BF16)

    es_h = ExitStack()
    h = seg.sbuf(name + "h", [128, NC * T], BF16, stack=es_h)
    es_ln = ExitStack()
    ln = emit_ln_fm(seg, cm, xs, T, h, es_ln, name=name + "ln",
                    xs_ready=[(cm.s_ld, cm.s_ld.n)])
    es_ln.close()

    es_w = ExitStack()
    wbuf = [seg.sbuf(name + f"wbuf{i}", [128, NC * 1024], BF16, stack=es_w) for i in range(2)]
    kb = [seg.sbuf(name + f"kb{i}", [128, 512], BF16, stack=es_w) for i in range(2)]
    vb = [seg.sbuf(name + f"vb{i}", [128, 1024], BF16, stack=es_w) for i in range(2)]
    sqb = [seg.sbuf(name + f"sqb{i}", [128, 512], BF16, stack=es_w) for i in range(2)]
    s_w3 = [seg.sem(name + "sw3a"), seg.sem(name + "sw3b")]
    w3_cons = [0, 0]

    def load_w5(idx, par):
        if w3_cons[par]:
            seg.wait("gpsimd", cm.s_t, w3_cons[par])
        else:
            seg.wait("gpsimd", cm.s_t, ln["t_done"])
            seg.wait("gpsimd", cm.s_v, ln["v_done"])
        seg.dma("gpsimd", wbuf[par][:].rearrange("p (c m) -> p c m", c=NC),
                wattn_d[:, idx * 1024:(idx + 1) * 1024].rearrange("(c p) m -> p c m", p=128), s_w3[par])
        return s_w3[par].n

    wq_n = load_w5(0, 0)
    wk_n = load_w5(1, 1)
    wq_sb, wk_sb = wbuf[0], wbuf[1]

    es_p = ExitStack()
    p_ps = [seg.psum(name + f"pps{i}", [128, 512], F32, stack=es_p) for i in range(2)]
    n_ps = seg.psum(name + "nps", [1, 512], F32, stack=es_p)
    seg.wait("tensor", s_w3[0], wq_n)
    seg.wait("tensor", cm.s_v, ln["h_v"])
    seg.wait("tensor", cm.s_init)
    guard = [None, None]
    s_exk = [seg.sem(name + "sexk0"), seg.sem(name + "sexk1")]
    s_exv = [seg.sem(name + "sexv0"), seg.sem(name + "sexv1")]
    s_exn = seg.sem(name + "sexn")
    kb_dma = [0, 0]
    vb_dma = [0, 0]
    nrow_guard = None
    nps_guard = None

    for oc in range(NC):
        for tc in range(NQC):
            par = (oc * NQC + tc) % 2
            if guard[par]:
                seg.wait("tensor", guard[par][0], guard[par][1])
            for c in range(NC):
                seg.op("tensor", "matmul", p_ps[par][:],
                       wq_sb[:, c * 1024 + oc * 128: c * 1024 + oc * 128 + 128],
                       h[:, c * T + tc * 512: c * T + (tc + 1) * 512],
                       start=(c == 0), stop=(c == NC - 1))
            seg.inc(cm.s_t, 1, eng="tensor")
            seg.wait("scalar", cm.s_t)
            seg.op("scalar", "activation", out=q_sb[:, oc * T + tc * 512: oc * T + (tc + 1) * 512],
                   in_=p_ps[par][:], func=AF.Identity, bias=battn_sb[:, oc:oc + 1])
            seg.chain("scalar", cm.s_s)
            guard[par] = (cm.s_s, cm.s_s.n)
            seg.op("scalar", "activation", out=sqb[par][:],
                   in_=q_sb[:, oc * T + tc * 512: oc * T + (tc + 1) * 512], func=AF.Square)
            seg.inc(cm.s_s, 1, eng="scalar")
            sq_s = cm.s_s.n
            for half in range(2):
                hh = 2 * oc + half
                seg.wait("tensor", cm.s_s, sq_s)
                if nps_guard:
                    seg.wait("tensor", nps_guard[0], nps_guard[1])
                seg.op("tensor", "matmul", n_ps[:], ones64[half * 64:half * 64 + 64, 0:1],
                       sqb[par][half * 64:half * 64 + 64, :], start=True, stop=True)
                seg.inc(cm.s_t, 1, eng="tensor")
                seg.wait("vector", cm.s_t)
                if nrow_guard:
                    seg.wait("vector", nrow_guard[0], nrow_guard[1])
                seg.op("vector", "tensor_copy", out=rowz[0:1, 0:512], in_=n_ps[:])
                seg.chain("vector", cm.s_v)
                nps_guard = (cm.s_v, cm.s_v.n)
                if tc % 1 == 0:
                    seg.op("vector", "reduce_max", out=strow[0:1, 144:145], in_=rowz[0:1, 0:512], axis=AX.X)
                    seg.chain("vector", cm.s_v)
                    if tc == 0:
                        seg.op("vector", "tensor_copy", out=strow[0:1, hh * 4 + tc // 1 * 0 + 0:hh * 4 + 1]
                               if False else strow[0:1, hh * 4 + tc:hh * 4 + tc + 1], in_=strow[0:1, 144:145])
                    else:
                        seg.op("vector", "tensor_copy", out=strow[0:1, hh * 4 + tc:hh * 4 + tc + 1],
                               in_=strow[0:1, 144:145])
                    seg.chain("vector", cm.s_v)
                if hh < 4:
                    seg.wait("scalar", cm.s_v)
                    seg.op("scalar", "activation", out=rowz[0:1, 0:512], in_=rowz[0:1, 0:512], func=AF.Sqrt)
                    seg.inc(cm.s_s, 1, eng="scalar")
                    seg.wait("vector", cm.s_s)
                    seg.op("vector", "tensor_copy",
                           out=qn_bf[0:1, hh * T + tc * 512: hh * T + (tc + 1) * 512],
                           in_=rowz[0:1, 0:512])
                    seg.chain("vector", cm.s_v)
                nrow_guard = (cm.s_v, cm.s_v.n)

    seg.wait("tensor", s_w3[1], wk_n)
    for oc in range(NC):
        for tc in range(NQC):
            par = (oc * NQC + tc) % 2
            if guard[par]:
                seg.wait("tensor", guard[par][0], guard[par][1])
            for c in range(NC):
                seg.op("tensor", "matmul", p_ps[par][:],
                       wk_sb[:, c * 1024 + oc * 128: c * 1024 + oc * 128 + 128],
                       h[:, c * T + tc * 512: c * T + (tc + 1) * 512],
                       start=(c == 0), stop=(c == NC - 1))
            seg.inc(cm.s_t, 1, eng="tensor")
            seg.wait("scalar", cm.s_t)
            if kb_dma[par]:
                seg.wait("scalar", s_exk[par], kb_dma[par])
            seg.op("scalar", "activation", out=kb[par][:], in_=p_ps[par][:],
                   func=AF.Identity, bias=battn_sb[:, 8 + oc: 8 + oc + 1])
            seg.chain("scalar", cm.s_s)
            guard[par] = (cm.s_s, cm.s_s.n)
            seg.wait("sync", cm.s_s)
            seg.dma("sync", ck_in[oc * 128:(oc + 1) * 128, tc * 512:(tc + 1) * 512], kb[par][:], s_exk[par])
            kb_dma[par] = s_exk[par].n
            seg.op("scalar", "activation", out=sqb[par][:], in_=kb[par][:], func=AF.Square)
            seg.inc(cm.s_s, 1, eng="scalar")
            sq_s = cm.s_s.n
            for half in range(2):
                hh = 2 * oc + half
                seg.wait("tensor", cm.s_s, sq_s)
                if nps_guard:
                    seg.wait("tensor", nps_guard[0], nps_guard[1])
                seg.op("tensor", "matmul", n_ps[:], ones64[half * 64:half * 64 + 64, 0:1],
                       sqb[par][half * 64:half * 64 + 64, :], start=True, stop=True)
                seg.inc(cm.s_t, 1, eng="tensor")
                seg.wait("scalar", cm.s_t)
                if nrow_guard:
                    seg.wait("scalar", nrow_guard[0], nrow_guard[1])
                seg.op("scalar", "activation", out=rowz[0:1, 0:512], in_=n_ps[:], func=AF.Sqrt)
                seg.inc(cm.s_s, 1, eng="scalar")
                nps_guard = (cm.s_s, cm.s_s.n)
                seg.wait("vector", cm.s_s)
                seg.wait("vector", s_exn, s_exn.n)
                seg.op("vector", "tensor_copy", out=rz_bf[0:1, 0:512], in_=rowz[0:1, 0:512])
                seg.inc(cm.s_v, 1, eng="vector")
                nrow_guard = (cm.s_v, cm.s_v.n)
                seg.wait("sync", cm.s_v)
                seg.dma("sync", cn_in[hh:hh + 1, tc * 512:(tc + 1) * 512], rz_bf[0:1, 0:512], s_exn)

    w3_cons[0] = cm.s_t.n
    wv_n = load_w5(2, 0)
    wv_sb = wbuf[0]
    seg.wait("tensor", s_w3[0], wv_n)
    for tt in range(NKT_OWN):
        vpar = tt % 2
        for vc in range(2):
            par = vc
            if guard[par]:
                seg.wait("tensor", guard[par][0], guard[par][1])
            for c in range(NC):
                seg.op("tensor", "matmul", p_ps[par][:],
                       h[:, c * T + tt * 128: c * T + (tt + 1) * 128],
                       wv_sb[:, c * 1024 + vc * 512: c * 1024 + (vc + 1) * 512],
                       start=(c == 0), stop=(c == NC - 1))
            seg.inc(cm.s_t, 1, eng="tensor")
            seg.wait("scalar", cm.s_t)
            if vc == 0 and vb_dma[vpar]:
                seg.wait("scalar", s_exv[vpar], vb_dma[vpar])
            seg.op("scalar", "copy", out=vb[vpar][:, vc * 512:(vc + 1) * 512], in_=p_ps[par][:])
            seg.inc(cm.s_s, 1, eng="scalar")
            guard[par] = (cm.s_s, cm.s_s.n)
        seg.wait("sync", cm.s_s)
        seg.dma("sync", cv_in[tt * 128:(tt + 1) * 128, :], vb[vpar][:], s_exv[vpar])
        vb_dma[vpar] = s_exv[vpar].n

    s_cc = seg.sem(name + "scc")
    for s in (s_exk[0], s_exk[1], s_exv[0], s_exv[1], s_exn):
        seg.wait("gpsimd", s, s.n)
    seg.op("gpsimd", "collective_compute", "AllGather", OP.bypass, replica_groups=rg,
           ins=[cc_in[None, :].opt()], outs=[cc_out[None, :].opt()])
    seg.inc(s_cc, None, eng="gpsimd")

    es_p.close()
    es_w.close()
    es_h.close()
    y_sb = seg.sbuf(name + "y", [128, NC * T], BF16)
    es_w2 = ExitStack()
    alibk_sb = seg.sbuf(name + "alibk", [128, 512], F32, stack=es_w2)
    btbl_sb = seg.sbuf(name + "btbl", [128, 12 * 4 * 32], F32, stack=es_w2)
    c_bf = seg.sbuf(name + "cbf", [1, 4 * T], BF16, stack=es_w2)
    qabs_sb = seg.sbuf(name + "qabs", [1, T], F32, stack=es_w2)
    slopeq_sb = seg.sbuf(name + "slopeq", [1, 64], F32, stack=es_w2)
    knall = seg.sbuf(name + "knall", [1, 2 * T], BF16, stack=es_w2)
    es_p2 = ExitStack()
    mrep_ps = seg.psum(name + "mrep", [128, 64], F32, stack=es_p2)
    seg.wait("gpsimd", cm.s_t)
    seg.wait("gpsimd", cm.s_s)
    seg.wait("gpsimd", cm.s_v)
    seg.dma("gpsimd", alibk_sb[:], alibk_d[:], s_par)
    seg.dma("gpsimd", qabs_sb[:], qabs_d[:], s_par)
    seg.dma("gpsimd", slopeq_sb[:], slopeq_d[:], s_par)
    tbl_n = s_par.n
    seg.wait("vector", s_par, tbl_n)
    seg.wait("scalar", s_par, tbl_n)
    seg.wait("tensor", s_par, tbl_n)

    seg.wait("sync", s_cc)
    knp = 0
    for hh in range(16):
        if knp:
            seg.wait("sync", cm.s_v, knp)
        seg.dma("sync", knall[0:1, 0:T], cn_out[0][hh:hh + 1, :], cm.s_ld)
        seg.dma("sync", knall[0:1, T:2 * T], cn_out[1][hh:hh + 1, :], cm.s_ld)
        seg.wait("vector", cm.s_ld)
        seg.op("vector", "reduce_max", out=strow[0:1, 64 + hh:65 + hh], in_=knall[:], axis=AX.X)
        seg.inc(cm.s_v, 1, eng="vector")
        knp = cm.s_v.n
    seg.wait("scalar", cm.s_v, knp)
    seg.op("scalar", "activation", out=strow[0:1, 0:64], in_=strow[0:1, 0:64], func=AF.Sqrt)
    seg.inc(cm.s_s, 1, eng="scalar")
    seg.wait("vector", cm.s_s)
    for qc in range(4):
        seg.op("vector", "tensor_copy", out=strow[0:1, 80 + qc:144 + qc:4][0:1, 0:16]
               if False else strow[0:1, 80 + qc:80 + qc + 61:4], in_=strow[0:1, 64:80])
    seg.chain("vector", cm.s_v)
    seg.op("vector", "tensor_tensor", out=strow[0:1, 80:144], in0=strow[0:1, 80:144],
           in1=strow[0:1, 0:64], op=OP.mult)
    seg.chain("vector", cm.s_v)
    seg.op("vector", "tensor_scalar_mul", out=strow[0:1, 80:144], in0=strow[0:1, 80:144], scalar1=SC)
    seg.chain("vector", cm.s_v)
    seg.op("vector", "tensor_add", out=strow[0:1, 80:144], in0=strow[0:1, 80:144], in1=slopeq_sb[:])
    seg.chain("vector", cm.s_v)
    seg.op("vector", "tensor_copy", out=mrow_bf[0:1, 0:64], in_=strow[0:1, 80:144])
    seg.inc(cm.s_v, 1, eng="vector")
    m_v = cm.s_v.n
    for hh in range(4):
        slope = 2.0 ** (-(hh + 1))
        seg.wait("vector", cm.s_v)
        seg.op("vector", "tensor_scalar", out=rowz[0:1, 0:512] if False else c_bf[0:1, hh * T:(hh + 1) * T],
               in0=qn_bf[0:1, hh * T:(hh + 1) * T],
               scalar1=strow[0:1, 64 + hh:65 + hh], scalar2=SC, op0=OP.mult, op1=OP.mult)
        seg.chain("vector", cm.s_v)
        seg.op("vector", "scalar_tensor_tensor", out=c_bf[0:1, hh * T:(hh + 1) * T], in0=qabs_sb[:],
               scalar=slope, in1=c_bf[0:1, hh * T:(hh + 1) * T], op0=OP.mult, op1=OP.add)
        seg.chain("vector", cm.s_v)
        seg.op("vector", "tensor_scalar_mul", out=c_bf[0:1, hh * T:(hh + 1) * T],
               in0=c_bf[0:1, hh * T:(hh + 1) * T], scalar1=-1.0 / SC)
        seg.chain("vector", cm.s_v)
    c_v = cm.s_v.n
    seg.wait("tensor", cm.s_v, m_v)
    for hh in range(4, 16):
        for qc in range(4):
            idx = hh * 4 + qc
            seg.op("tensor", "matmul", mrep_ps[:, idx - 16:idx - 16 + 1] if False else mrep_ps[:, (idx % 64):(idx % 64) + 1],
                   cm.ones_row[:], mrow_bf[0:1, idx:idx + 1], start=True, stop=True)
    seg.inc(cm.s_t, 1, eng="tensor")
    seg.wait("vector", cm.s_t)
    for hh in range(4, 16):
        for qc in range(4):
            idx = hh * 4 + qc
            b0 = ((hh - 4) * 4 + qc) * 32
            seg.op("vector", "scalar_tensor_tensor", out=btbl_sb[:, b0:b0 + 32],
                   in0=alibk_sb[:, hh * 32:(hh + 1) * 32], scalar=mrep_ps[:, (idx % 64):(idx % 64) + 1],
                   in1=alibk_sb[:, hh * 32:(hh + 1) * 32], op0=OP.mult, op1=OP.subtract) if False else None
            seg.op("vector", "tensor_scalar_sub", out=btbl_sb[:, b0:b0 + 32],
                   in0=alibk_sb[:, hh * 32:(hh + 1) * 32], scalar1=mrep_ps[:, (idx % 64):(idx % 64) + 1])
    seg.inc(cm.s_v, 1, eng="vector")
    btbl_v = cm.s_v.n

    es_p2.close()

    es_a = ExitStack()
    kst = [seg.sbuf(name + f"kst{i}", [128, 2 * 128], BF16, stack=es_a) for i in range(3)]
    vst = [seg.sbuf(name + f"vst{i}", [128, 4 * 65], BF16, stack=es_a) for i in range(3)]
    P = [seg.sbuf(name + f"P{i}", [128, 512], BF16, stack=es_a) for i in range(2)]
    tmp_sb = seg.sbuf(name + "tmp", [128, 512], BF16, stack=es_a)
    bcs_sb = seg.sbuf(name + "bcs", [128, 512], F32, stack=es_a)
    st_ps = [seg.psum(name + f"st{i}", [128, 512], F32, stack=es_a) for i in range(2)]
    av_ps = [seg.psum(name + f"av{i}", [128, 512], F32, stack=es_a) for i in range(4)]
    bc_ps = seg.psum(name + "bc", [128, 512], F32, stack=es_a)
    s_kk = [seg.sem(name + "skk0"), seg.sem(name + "skk1"), seg.sem(name + "skk2")]
    s_vv = [seg.sem(name + "svv0"), seg.sem(name + "svv1"), seg.sem(name + "svv2")]
    s_ysh = seg.sem(name + "sysh")
    kst_cons = [0, 0, 0]
    vst_cons = [0, 0, 0]
    st_read = [0, 0]
    P_read = [0, 0]
    av_evac = [0, 0, 0, 0]
    rz_free = 0
    gate_t = cm.s_t.n
    gate_s = cm.s_s.n
    gate_v = cm.s_v.n
    for i in range(3):
        seg.wait("vector", cm.s_v, gate_v)
        for m in range(4):
            seg.op("vector", "memset", vst[i][:, m * 65 + 64:m * 65 + 65], 1.0)
    seg.inc(cm.s_v, 1, eng="vector")
    vst_init_v = cm.s_v.n

    def load_kv5(r, j, hg, par):
        if kst_cons[par]:
            seg.wait("gpsimd", cm.s_t, kst_cons[par])
        else:
            seg.wait("gpsimd", cm.s_t, gate_t)
            seg.wait("gpsimd", cm.s_s, gate_s)
            seg.wait("gpsimd", cm.s_v, vst_init_v)
        if vst_cons[par]:
            seg.wait("gpsimd", cm.s_t, vst_cons[par])
        seg.wait("gpsimd", s_cc, 1)
        seg.dma("gpsimd", kst[par][:].rearrange("p (c k) -> p c k", c=2),
                ck_out[r][hg * 256:(hg + 1) * 256, j * 128:(j + 1) * 128].rearrange("(c p) k -> p c k", p=128),
                s_kk[par])
        seg.dma("gpsimd", vst[par][:].rearrange("p (m c) -> p m c", m=4)[:, :, 0:64],
                cv_out[r][j * 128:(j + 1) * 128, hg * 256:(hg + 1) * 256].rearrange("k (m c) -> k m c", m=4),
                s_vv[par])

    for qc in range(NQC):
        q0 = qc * 512
        i0 = 4 * qc
        for hg in range(4):
            sl = sched5(qc, hg == 0)
            kv_cnt = {}
            first_av = [True] * 4
            pend = []

            def emit_av(e):
                m_, par_, ppar_, n0_, si_, exp_ = e
                seg.wait("tensor", cm.s_s, exp_)
                if first_av[m_] and av_evac[m_]:
                    seg.wait("tensor", cm.s_v, av_evac[m_])
                seg.op("tensor", "matmul", av_ps[m_][0:65, n0_:512],
                       vst[par_][:, m_ * 65:(m_ + 1) * 65],
                       P[ppar_][:, n0_:512], start=first_av[m_],
                       stop=(si_ == len(sl) - 1))
                seg.inc(cm.s_t, 1, eng="tensor")
                first_av[m_] = False
                P_read[ppar_] = cm.s_t.n
                vst_cons[par_] = cm.s_t.n

            for pre in range(min(3, len(sl))):
                load_kv5(sl[pre][0], sl[pre][1], hg, pre)
                kv_cnt[pre] = (s_kk[pre].n, s_vv[pre].n)
            for si, (r, j, n0) in enumerate(sl):
                par = si % 3
                seg.wait("tensor", s_kk[par], kv_cnt[si][0])
                seg.wait("tensor", s_vv[par], kv_cnt[si][1])
                for m in range(4):
                    hh = 4 * hg + m
                    if hg == 0 and j < i0 - L_STEEP[hh]:
                        continue
                    base = (m % 2) * 64
                    cc_ = m // 2
                    spar = m % 2
                    if st_read[spar]:
                        seg.wait("tensor", cm.s_s, st_read[spar])
                    seg.op("tensor", "matmul", st_ps[spar][:, n0:512],
                           kst[par][base:base + 64, cc_ * 128:(cc_ + 1) * 128],
                           q_sb[base:base + 64, (hh // 2) * T + q0 + n0: (hh // 2) * T + q0 + 512],
                           start=True, stop=(hg != 0))
                    if hg == 0:
                        seg.wait("tensor", cm.s_v, c_v)
                        seg.op("tensor", "matmul", st_ps[spar][:, n0:512],
                               cm.ones_row[:], c_bf[0:1, hh * T + q0 + n0: hh * T + q0 + 512],
                               start=False, stop=True)
                    seg.inc(cm.s_t, 1, eng="tensor")
                    while len(pend) > 1:
                        emit_av(pend.pop(0))
                    if j >= i0:
                        c0 = (j - i0) * 128
                        seg.wait("vector", cm.s_t)
                        seg.op("vector", "tensor_tensor", out=st_ps[spar][:, c0:c0 + 128],
                               in0=st_ps[spar][:, c0:c0 + 128],
                               in1=dmask_sb[:, r * 128:(r + 1) * 128], op=OP.add)
                        seg.inc(cm.s_v, 1, eng="vector")
                        seg.wait("scalar", cm.s_v)
                    seg.wait("scalar", cm.s_t)
                    if hg != 0:
                        seg.wait("scalar", cm.s_v, btbl_v)
                        bcol = btbl_sb[:, ((hh - 4) * 4 + qc) * 32 + 2 * j + r: ((hh - 4) * 4 + qc) * 32 + 2 * j + r + 1]
                    else:
                        bcol = alibk_sb[:, hh * 32 + 2 * j + r: hh * 32 + 2 * j + r + 1]
                    ppar = m % 2
                    if P_read[ppar]:
                        seg.wait("scalar", cm.s_t, P_read[ppar])
                    seg.op("scalar", "activation", out=P[ppar][:, n0:512], in_=st_ps[spar][:, n0:512],
                           func=AF.Exp, scale=sc_ap, bias=bcol)
                    seg.inc(cm.s_s, 1, eng="scalar")
                    st_read[spar] = cm.s_s.n
                    pend.append((m, par, ppar, n0, si, cm.s_s.n))
                kst_cons[par] = cm.s_t.n
                if si + 3 < len(sl):
                    while pend:
                        emit_av(pend.pop(0))
                    load_kv5(sl[si + 3][0], sl[si + 3][1], hg, par)
                    kv_cnt[si + 3] = (s_kk[par].n, s_vv[par].n)
            while pend:
                emit_av(pend.pop(0))
            av_done_t = cm.s_t.n
            cm.constf(-1.0)
            for m in range(4):
                hh = 4 * hg + m
                seg.wait("scalar", cm.s_t, av_done_t)
                if rz_free:
                    seg.wait("scalar", cm.s_v, rz_free)
                seg.op("scalar", "activation", out=rowz[64:65, 0:512],
                       in_=av_ps[m][64:65, 0:512], func=AF.Ln)
                seg.chain("scalar", cm.s_s)
                seg.op("scalar", "activation", out=rowz[64:65, 0:512],
                       in_=rowz[64:65, 0:512], func=AF.Exp,
                       scale=cm.cpool[64:65, cm._cvals[-1.0]:cm._cvals[-1.0] + 1])
                seg.inc(cm.s_s, 1, eng="scalar")
                seg.wait("vector", cm.s_s)
                seg.op("vector", "tensor_copy", out=rz_bf[64:65, 0:512], in_=rowz[64:65, 0:512])
                seg.inc(cm.s_v, 1, eng="vector")
                rb_v = cm.s_v.n
                rz_free = rb_v
                seg.wait("tensor", cm.s_v, rb_v)
                seg.op("tensor", "matmul", bc_ps[0:64, :], ones64[64:65, :],
                       rz_bf[64:65, 0:512], start=True, stop=True)
                seg.inc(cm.s_t, 1, eng="tensor")
                seg.wait("scalar", cm.s_t)
                seg.op("scalar", "copy", out=bcs_sb[0:64, :], in_=bc_ps[0:64, :])
                seg.inc(cm.s_s, 1, eng="scalar")
                seg.wait("vector", cm.s_s)
                dst_half = hh % 2
                ychunk = hh // 2
                if dst_half == 0:
                    seg.op("vector", "tensor_tensor", out=y_sb[0:64, ychunk * T + q0: ychunk * T + q0 + 512],
                           in0=av_ps[m][0:64, :], in1=bcs_sb[0:64, :], op=OP.mult)
                    seg.inc(cm.s_v, 1, eng="vector")
                else:
                    seg.wait("vector", s_ysh, s_ysh.n)
                    seg.op("vector", "tensor_tensor", out=tmp_sb[0:64, :],
                           in0=av_ps[m][0:64, :], in1=bcs_sb[0:64, :], op=OP.mult)
                    seg.inc(cm.s_v, 1, eng="vector")
                    seg.wait("sync", cm.s_v)
                    seg.dma("sync", y_sb[64:128, ychunk * T + q0: ychunk * T + q0 + 512],
                            tmp_sb[0:64, :], s_ysh)
                av_evac[m] = cm.s_v.n
    y_done_v = cm.s_v.n
    y_dma_n = s_ysh.n
    es_a.close()
    es_w2.close()

    es_o = ExitStack()
    wp_sb = seg.sbuf(name + "wp", [128, NC * 1024], BF16, stack=es_o)
    bv_sb = seg.sbuf(name + "bvrow", [1, 1024], BF16, stack=es_o)
    bvf_sb = seg.sbuf(name + "bvf", [128, 8], BF16, stack=es_o)
    o_ps = [seg.psum(name + f"ops{i}", [128, 512], F32, stack=es_o) for i in range(2)]
    bvf_ps = seg.psum(name + "bvfps", [128, 8], F32, stack=es_o)
    seg.wait("gpsimd", cm.s_t)
    seg.wait("gpsimd", cm.s_s)
    seg.wait("gpsimd", cm.s_v, y_done_v)
    seg.dma("gpsimd", wp_sb[:].rearrange("p (c m) -> p c m", c=NC),
            wproj_d.rearrange("(c p) m -> p c m", p=128), s_par)
    seg.dma("gpsimd", bv_sb[:], bvrow_d[:], s_par)
    wp_n = s_par.n
    seg.wait("tensor", s_par, wp_n)
    seg.wait("tensor", cm.s_init)
    for c in range(NC):
        seg.op("tensor", "matmul", bvf_ps[:, c:c + 1], bv_sb[0:1, c * 128:(c + 1) * 128],
               ones64[0:1, 0:1], start=True, stop=True)
    seg.inc(cm.s_t, 1, eng="tensor")
    seg.wait("vector", cm.s_t)
    seg.op("vector", "tensor_copy", out=bvf_sb[:], in_=bvf_ps[:])
    seg.inc(cm.s_v, 1, eng="vector")
    bvf_v = cm.s_v.n
    seg.wait("tensor", cm.s_v, bvf_v)
    for dc in range(NC):
        for kc in range(NC):
            seg.op("tensor", "matmul", bvf_ps[:, dc:dc + 1],
                   wp_sb[:, kc * 1024 + dc * 128: kc * 1024 + dc * 128 + 128],
                   bvf_sb[:, kc:kc + 1], start=(kc == 0), stop=(kc == NC - 1))
    seg.inc(cm.s_t, 1, eng="tensor")
    seg.wait("vector", cm.s_t)
    seg.op("vector", "tensor_add", out=btot_sb[:], in0=bvf_ps[:, 0:8], in1=bproj_sb[:])
    seg.inc(cm.s_v, 1, eng="vector")
    btot_v = cm.s_v.n
    seg.wait("tensor", cm.s_v, y_done_v)
    seg.wait("tensor", s_ysh, y_dma_n)
    seg.wait("tensor", cm.s_v, btot_v)
    oguard = [None, None]
    for dc in range(NC):
        for tc in range(NQC):
            par = (dc * NQC + tc) % 2
            if oguard[par]:
                seg.wait("tensor", oguard[par][0], oguard[par][1])
            for kc in range(NC):
                seg.op("tensor", "matmul", o_ps[par][:],
                       wp_sb[:, kc * 1024 + dc * 128: kc * 1024 + dc * 128 + 128],
                       y_sb[:, kc * T + tc * 512: kc * T + (tc + 1) * 512],
                       start=(kc == 0), stop=(kc == NC - 1))
            seg.inc(cm.s_t, 1, eng="tensor")
            seg.wait("vector", cm.s_t)
            xsl = slice(dc * T + tc * 512, dc * T + (tc + 1) * 512)
            seg.op("vector", "scalar_tensor_tensor", out=xs[:, xsl], in0=o_ps[par][:],
                   scalar=btot_sb[:, dc:dc + 1], in1=xs[:, xsl], op0=OP.add, op1=OP.add)
            seg.inc(cm.s_v, 1, eng="vector")
            oguard[par] = (cm.s_v, cm.s_v.n)
    es_o.close()

    if standalone:
        seg.wait("sync", cm.s_v)
        for c in range(NC):
            seg.dma("sync", x_out[c * 128:(c + 1) * 128, :], xs[:, c * T:(c + 1) * T], cm.s_st)
        seg.wait("sync", cm.s_st)
        seg.emit()
        return seg
    return None


def host_prep_s5(inputs, core, T=2048):
    p = core % 2
    W = inputs["L1_attn_a_W_attn"]
    b = inputs["L1_attn_a_b_attn"]
    d = {
        "wattn": np.ascontiguousarray(W),
        "battn": np.ascontiguousarray(np.concatenate([b[0:1024], b[1024:2048]]).reshape(16, 128).T),
        "bvrow": np.ascontiguousarray(b[2048:3072])[None, :],
        "bprojf": np.ascontiguousarray(inputs["L1_attn_a_b_proj"].reshape(8, 128).T),
        "wproj": np.ascontiguousarray(inputs["L1_attn_a_W_proj"]),
    }
    alib = np.zeros((128, 512), dtype=np.float32)
    for hh in range(16):
        slope = 2.0 ** (-(hh + 1))
        for gk in range(32):
            alib[:, hh * 32 + gk] = slope * (128 * gk + np.arange(128))
    d["alibk"] = alib
    NEG = np.float32(-1e30)
    tri_add = np.where(np.arange(128)[:, None] <= np.arange(128)[None, :], 0.0, NEG).astype(np.float32)
    m0 = tri_add if p == 0 else np.zeros((128, 128), np.float32)
    m1 = np.full((128, 128), NEG, np.float32) if p == 0 else tri_add
    d["dmask5"] = np.concatenate([m0, m1], axis=1).astype(np.float32)
    d["qabs"] = own_token_index(core).astype(np.float32)[None, :]
    own = own_token_index(core)
    sq = np.zeros((1, 64), dtype=np.float32)
    for hh in range(16):
        slope = 2.0 ** (-(hh + 1))
        for qc in range(4):
            sq[0, hh * 4 + qc] = slope * own[qc * 512:(qc + 1) * 512].max()
    d["slopeq"] = sq
    return d




def build_s6(nc, standalone=True, xs=None, cm=None, seg=None, T=2048, name="s6",
             rg=RG_PAIRS, dram_prefix="", io_dram=None, wtag="6"):
    NC = 8
    NQC = T // 512
    NT = T // 128
    NL = 2
    HIDC = 22

    if standalone:
        seg = Seg(nc)
    if standalone:
        x_in = nc.dram_tensor("x_in", [1024, T], F32, kind="ExternalInput").ap()
        x_out = nc.dram_tensor("x_out", [1024, T], F32, kind="ExternalOutput").ap()
    elif io_dram is not None:
        x_in, x_out = io_dram
    wqkv_d = nc.dram_tensor(dram_prefix + "wqkv" + wtag, [NL, 1024, 1536], F32, kind="ExternalInput").ap()
    wo_d = nc.dram_tensor(dram_prefix + "wo" + wtag, [NL, 512, 1024], F32, kind="ExternalInput").ap()
    qs_d = nc.dram_tensor(dram_prefix + "qs" + wtag, [128, NL, 2], F32, kind="ExternalInput").ap()
    ks_d = nc.dram_tensor(dram_prefix + "ks" + wtag, [128, NL, 2], F32, kind="ExternalInput").ap()
    qsr_d = nc.dram_tensor(dram_prefix + "qsr" + wtag, [1, NL * 64], F32, kind="ExternalInput").ap()
    ksr_d = nc.dram_tensor(dram_prefix + "ksr" + wtag, [1, NL * 64], F32, kind="ExternalInput").ap()
    rotq_d = nc.dram_tensor(dram_prefix + "rotq" + wtag, [128, NL, 2, T], BF16, kind="ExternalInput").ap()
    rotk_d = nc.dram_tensor(dram_prefix + "rotk" + wtag, [128, NL, 2, T], BF16, kind="ExternalInput").ap()
    lng_d = nc.dram_tensor(dram_prefix + "lng" + wtag, [128, NL, 8], F32, kind="ExternalInput").ap()
    lnb_d = nc.dram_tensor(dram_prefix + "lnb" + wtag, [128, NL, 8], F32, kind="ExternalInput").ap()
    w1_d = nc.dram_tensor(dram_prefix + "w1p" + wtag, [NL, 1024, 5632], F32, kind="ExternalInput").ap()
    b1_d = nc.dram_tensor(dram_prefix + "b1p" + wtag, [128, NL, 44], F32, kind="ExternalInput").ap()
    w2_d = nc.dram_tensor(dram_prefix + "w2p" + wtag, [NL, 2816, 1024], F32, kind="ExternalInput").ap()
    b2_d = nc.dram_tensor(dram_prefix + "b2f" + wtag, [128, NL, 8], F32, kind="ExternalInput").ap()
    amask_d = nc.dram_tensor(dram_prefix + "amask" + wtag, [128, 12 * 512], F32, kind="ExternalInput").ap()
    hsel_d = nc.dram_tensor(dram_prefix + "hsel" + wtag, [128, 2], F32, kind="ExternalInput").ap()
    msc_d = nc.dram_tensor(dram_prefix + "msc" + wtag, [1, 16], F32, kind="ExternalInput").ap()

    HN = 512 * 16
    halo_in = nc.dram_tensor(name + "hin", [HN], F32, kind="Internal").ap()
    halo_out = nc.dram_tensor(name + "hout", [2 * HN], F32, kind="Internal").ap()
    CCN = 512 * T + T * 512
    cc_in = nc.dram_tensor(name + "cc_in", [CCN], BF16, kind="Internal").ap()
    cc_out = nc.dram_tensor(name + "cc_out", [2 * CCN], BF16, kind="Internal").ap()
    ck_in = cc_in[0:512 * T].rearrange("(a b) -> a b", b=T)
    cv_in = cc_in[512 * T:].rearrange("(a b) -> a b", b=512)
    ck_out = [cc_out[r * CCN: r * CCN + 512 * T].rearrange("(a b) -> a b", b=T) for r in range(2)]
    cv_out = [cc_out[r * CCN + 512 * T: (r + 1) * CCN].rearrange("(a b) -> a b", b=512) for r in range(2)]

    if standalone:
        cm = Common(seg)
    e8 = cm.constf(8.0)

    qs_sb = seg.sbuf(name + "qs", [128, NL * 2], F32)
    ks_sb = seg.sbuf(name + "ks", [128, NL * 2], F32)
    lng_sb = seg.sbuf(name + "lng", [128, NL * 8], F32)
    lnb_sb = seg.sbuf(name + "lnb", [128, NL * 8], F32)
    b1_sb = seg.sbuf(name + "b1", [128, NL * 44], F32)
    b2_sb = seg.sbuf(name + "b2", [128, NL * 8], F32)
    amask_sb = seg.sbuf(name + "amask", [128, 12 * 512], F32)
    hsel_sb = seg.sbuf(name + "hsel", [128, 2], F32)
    qsr_sb = seg.sbuf(name + "qsr", [1, NL * 64], F32)
    ksr_sb = seg.sbuf(name + "ksr", [1, NL * 64], F32)
    msc_sb = seg.sbuf(name + "msc", [1, 16], F32)
    ones32 = seg.sbuf(name + "ones32", [128, 32], BF16)
    onesr32 = seg.sbuf(name + "onesr32", [1, 32], BF16)
    ones64r = seg.sbuf(name + "ones64r", [128, 64], BF16)
    strow = seg.sbuf(name + "strow", [1, 64], F32)
    mrow_bf = seg.sbuf(name + "mrowbf", [1, 16], BF16)
    ebias_sb = seg.sbuf(name + "ebias", [128, 16], F32)
    s_par = seg.sem(name + "spar")
    seg.op("vector", "memset", ones32[:], 1.0)
    seg.op("vector", "memset", onesr32[:], 1.0)
    seg.op("vector", "memset", ones64r[:], 1.0)
    seg.inc(cm.s_init, 1, eng="vector")
    for sb, d in ((qs_sb, qs_d), (ks_sb, ks_d), (lng_sb, lng_d), (lnb_sb, lnb_d),
                  (b1_sb, b1_d), (b2_sb, b2_d)):
        seg.dma("gpsimd", sb[:].rearrange("p (l c) -> p l c", l=NL), d[:], s_par)
    seg.dma("gpsimd", amask_sb[:], amask_d[:], s_par)
    seg.dma("gpsimd", hsel_sb[:], hsel_d[:], s_par)
    seg.dma("gpsimd", qsr_sb[:], qsr_d[:], s_par)
    seg.dma("gpsimd", ksr_sb[:], ksr_d[:], s_par)
    seg.dma("gpsimd", msc_sb[:], msc_d[:], s_par)
    par_n = s_par.n
    for e in ("vector", "scalar", "tensor"):
        seg.wait(e, s_par, par_n)

    hcur = seg.sbuf(name + "hcur", [128, NC * T], F32)

    es_xs = ExitStack()
    if standalone or io_dram is not None:
        xs = seg.sbuf(name + "xsl", [128, NC * T], F32, stack=es_xs)
        for c in range(NC):
            seg.dma("sync", xs[:, c * T:(c + 1) * T], x_in[c * 128:(c + 1) * 128, :], cm.s_ld)
    es_ln = ExitStack()
    ln = emit_ln_fm(seg, cm, xs, T, hcur, es_ln, name=name + "ln0",
                    xs_ready=[(cm.s_ld, cm.s_ld.n)])
    es_ln.close()
    if standalone or io_dram is not None:
        es_xs.close()
    hcur_v = ln["h_v"]

    s_hx = seg.sem(name + "shx")
    s_hcc = seg.sem(name + "shcc")
    s_cc = seg.sem(name + "scc")
    s_exk = [seg.sem(name + "sexk0"), seg.sem(name + "sexk1")]
    s_exv = [seg.sem(name + "sexv0"), seg.sem(name + "sexv1")]
    s_w = seg.sem(name + "sw")
    s_ysh = seg.sem(name + "sysh")
    s_kk = [seg.sem(name + "skk0"), seg.sem(name + "skk1"), seg.sem(name + "skk2")]
    s_vv = [seg.sem(name + "svv0"), seg.sem(name + "svv1"), seg.sem(name + "svv2")]
    halo_num = [0]

    def build_shift(dst_bf, scope, src_v):
        hb = seg.sbuf(name + f"hb{halo_num[0]}", [128, 4 * 16], F32, stack=scope)
        hu = seg.sbuf(name + f"hu{halo_num[0]}", [128, 4 * 16], F32, stack=scope)
        h2 = seg.sbuf(name + f"h2{halo_num[0]}", [128, 4 * 16], F32, stack=scope)
        halo_num[0] += 1
        seg.wait("vector", cm.s_v, src_v)
        seg.wait("vector", s_hx, s_hx.n)
        for c in range(4):
            seg.op("vector", "tensor_copy", out=hb[:, c * 16:(c + 1) * 16],
                   in_=hcur[:, (4 + c) * T + 127: (4 + c + 1) * T: 128])
        seg.inc(cm.s_v, 1, eng="vector")
        seg.wait("sync", cm.s_v)
        seg.dma("sync", halo_in.rearrange("(c p t) -> p c t", p=128, c=4),
                hb[:].rearrange("p (c t) -> p c t", c=4), s_hx)
        seg.wait("gpsimd", s_hx, s_hx.n)
        seg.wait("gpsimd", s_hcc, s_hcc.n)
        seg.op("gpsimd", "collective_compute", "AllGather", OP.bypass, replica_groups=rg,
               ins=[halo_in[None, :].opt()], outs=[halo_out[None, :].opt()])
        seg.inc(s_hcc, None, eng="gpsimd")
        seg.wait("sync", s_hcc, s_hcc.n)
        seg.dma("sync", hu[:].rearrange("p (c t) -> p c t", c=4),
                halo_out[0:HN].rearrange("(c p t) -> p c t", p=128, c=4), s_hx)
        seg.wait("vector", s_hx, s_hx.n)
        seg.op("vector", "memset", h2[:], 0.0)
        seg.inc(cm.s_v, 1, eng="vector")
        seg.wait("sync", cm.s_v)
        seg.dma("sync", h2[:].rearrange("p (c t) -> p c t", c=4)[:, :, 1:16],
                halo_out[HN:2 * HN].rearrange("(c p t) -> p c t", p=128, c=4)[:, :, 0:15], s_hx)
        seg.wait("vector", s_hx, s_hx.n)
        seg.op("vector", "tensor_scalar_mul", out=hu[:], in0=hu[:], scalar1=hsel_sb[:, 0:1])
        seg.chain("vector", cm.s_v)
        seg.op("vector", "scalar_tensor_tensor", out=hu[:], in0=h2[:],
               scalar=hsel_sb[:, 1:2], in1=hu[:], op0=OP.mult, op1=OP.add)
        seg.chain("vector", cm.s_v)
        seg.wait("vector", cm.s_v, src_v)
        for c in range(4):
            seg.op("vector", "tensor_copy", out=dst_bf[:, c * T:(c + 1) * T],
                   in_=hcur[:, c * T:(c + 1) * T])
        for c in range(4):
            seg.op("vector", "tensor_copy", out=dst_bf[:, (4 + c) * T + 1:(4 + c + 1) * T],
                   in_=hcur[:, (4 + c) * T: (4 + c + 1) * T - 1])
        seg.chain("vector", cm.s_v)
        for c in range(4):
            seg.op("vector", "tensor_copy", out=dst_bf[:, (4 + c) * T:(4 + c + 1) * T:128],
                   in_=hu[:, c * 16:(c + 1) * 16])
        seg.inc(cm.s_v, 1, eng="vector")
        return cm.s_v.n

    for l in range(NL):
        lname = f"{name}l{l}"
        es_b = ExitStack()
        hs_bf = seg.sbuf(lname + "hs", [128, NC * T], BF16, stack=es_b)
        hs_v = build_shift(hs_bf, es_b, hcur_v)
        wq_sb = seg.sbuf(lname + "wq", [128, NC * 512], BF16, stack=es_b)
        wk_sb = seg.sbuf(lname + "wk", [128, NC * 512], BF16, stack=es_b)
        wv_sb = seg.sbuf(lname + "wv", [128, NC * 512], BF16, stack=es_b)
        kb = [seg.sbuf(lname + f"kb{i}", [128, 512], BF16, stack=es_b) for i in range(2)]
        vb = [seg.sbuf(lname + f"vb{i}", [128, 512], BF16, stack=es_b) for i in range(2)]
        sqb = [seg.sbuf(lname + f"sqb{i}", [128, 512], BF16, stack=es_b) for i in range(2)]
        rn_row = seg.sbuf(lname + "rnr", [1, 512], F32, stack=es_b)
        rn_bf = seg.sbuf(lname + "rnbf", [1, 4 * 512], BF16, stack=es_b)
        rotq_sb = seg.sbuf(lname + "rotq", [128, 2 * T], BF16, stack=es_b)
        rotk_sb = seg.sbuf(lname + "rotk", [128, 2 * T], BF16, stack=es_b)
        seg.wait("gpsimd", cm.s_t)
        seg.wait("gpsimd", cm.s_v)
        seg.wait("gpsimd", cm.s_s)
        for i, sb in enumerate((wq_sb, wk_sb, wv_sb)):
            seg.dma("gpsimd", sb[:].rearrange("p (c m) -> p c m", c=NC),
                    wqkv_d[l, :, i * 512:(i + 1) * 512].rearrange("(c p) m -> p c m", p=128), s_w)
        seg.dma("gpsimd", rotq_sb[:].rearrange("p (a t) -> p a t", a=2), rotq_d[:, l], s_w)
        seg.dma("gpsimd", rotk_sb[:].rearrange("p (a t) -> p a t", a=2), rotk_d[:, l], s_w)
        w_n = s_w.n

        es_q = ExitStack()
        q_sb = seg.sbuf(lname + "q", [128, 4 * T], BF16, stack=es_q, side="right")
        o_sb = seg.sbuf(lname + "o", [128, 4 * T], BF16, stack=es_q, side="right")
        es_p = ExitStack()
        p_ps = [seg.psum(lname + f"pps{i}", [128, 512], F32, stack=es_p) for i in range(2)]
        n_ps = seg.psum(lname + "nps", [1, 512], F32, stack=es_p)
        bcn_ps = seg.psum(lname + "bcn", [128, 512], F32, stack=es_p)
        guard = [None, None]
        seg.wait("tensor", s_w, w_n)
        seg.wait("tensor", cm.s_v, hs_v)
        seg.wait("tensor", cm.s_init)

        def qk_build(w_sb, sc_col, rot_sb, dst, to_ck):
            nonlocal guard
            for oc in range(4):
                for tc in range(NQC):
                    par = (oc * NQC + tc) % 2
                    if guard[par]:
                        seg.wait("tensor", guard[par][0], guard[par][1])
                    for c in range(NC):
                        seg.op("tensor", "matmul", p_ps[par][:],
                               w_sb[:, c * 512 + oc * 128: c * 512 + oc * 128 + 128],
                               hs_bf[:, c * T + tc * 512: c * T + (tc + 1) * 512],
                               start=(c == 0), stop=(c == NC - 1))
                    seg.inc(cm.s_t, 1, eng="tensor")
                    seg.wait("scalar", cm.s_t)
                    seg.op("scalar", "copy", out=dst[:, oc * T + tc * 512: oc * T + (tc + 1) * 512],
                           in_=p_ps[par][:])
                    seg.inc(cm.s_s, 1, eng="scalar")
                    guard[par] = (cm.s_s, cm.s_s.n)
            raw_s = cm.s_s.n
            dma_par = 0
            for hg in range(2):
                for tc in range(NQC):
                    spar = tc % 2
                    seg.wait("scalar", cm.s_s, raw_s)
                    seg.wait("scalar", cm.s_t)
                    seg.wait("scalar", cm.s_v)
                    seg.op("scalar", "activation", out=sqb[0][:],
                           in_=dst[:, hg * T + tc * 512: hg * T + (tc + 1) * 512], func=AF.Square)
                    seg.chain("scalar", cm.s_s)
                    seg.op("scalar", "activation", out=sqb[1][:],
                           in_=dst[:, (2 + hg) * T + tc * 512: (2 + hg) * T + (tc + 1) * 512], func=AF.Square)
                    seg.inc(cm.s_s, 1, eng="scalar")
                    sq_s = cm.s_s.n
                    for m in range(4):
                        b32 = 32 * m
                        kw = {"tile_position": (b32, 0)} if b32 >= 96 else {}
                        seg.wait("tensor", cm.s_s)
                        if m == 0:
                            seg.wait("tensor", cm.s_v)
                        seg.op("tensor", "matmul", n_ps[:], ones32[b32:b32 + 32, 0:1],
                               sqb[0][b32:b32 + 32, :], start=True, stop=False, **kw)
                        seg.op("tensor", "matmul", n_ps[:], ones32[b32:b32 + 32, 0:1],
                               sqb[1][b32:b32 + 32, :], start=False, stop=True, **kw)
                        seg.inc(cm.s_t, 1, eng="tensor")
                        seg.wait("scalar", cm.s_t)
                        seg.wait("scalar", cm.s_v)
                        seg.op("scalar", "activation", out=rn_row[:], in_=n_ps[:],
                               func=AF.Ln, bias=cm.constf1(1e-24))
                        seg.chain("scalar", cm.s_s)
                        seg.op("scalar", "activation", out=rn_row[:], in_=rn_row[:],
                               func=AF.Exp, scale=cm.mhalf1)
                        seg.inc(cm.s_s, 1, eng="scalar")
                        seg.wait("vector", cm.s_s)
                        seg.wait("vector", cm.s_t)
                        seg.op("vector", "tensor_copy", out=rn_bf[0:1, m * 512:(m + 1) * 512], in_=rn_row[:])
                        seg.inc(cm.s_v, 1, eng="vector")
                    rn_v = cm.s_v.n
                    seg.wait("tensor", cm.s_v, rn_v)
                    for m in range(4):
                        seg.op("tensor", "matmul", bcn_ps[32 * m:32 * m + 32, :],
                               onesr32[:], rn_bf[0:1, m * 512:(m + 1) * 512],
                               start=True, stop=True, tile_position=(0, 32 * m))
                    seg.inc(cm.s_t, 1, eng="tensor")
                    seg.wait("vector", cm.s_t)
                    h1sl = slice(hg * T + tc * 512, hg * T + (tc + 1) * 512)
                    h2sl = slice((2 + hg) * T + tc * 512, (2 + hg) * T + (tc + 1) * 512)
                    seg.op("vector", "scalar_tensor_tensor", out=sqb[0][:], in0=dst[:, h1sl],
                           scalar=sc_col[0], in1=bcn_ps[:], op0=OP.mult, op1=OP.mult)
                    seg.op("vector", "scalar_tensor_tensor", out=sqb[1][:], in0=dst[:, h2sl],
                           scalar=sc_col[1], in1=bcn_ps[:], op0=OP.mult, op1=OP.mult)
                    seg.chain("vector", cm.s_v)
                    tsl = slice(tc * 512, (tc + 1) * 512)
                    seg.op("vector", "tensor_tensor", out=dst[:, h1sl], in0=sqb[0][:],
                           in1=rot_sb[:, tsl], op=OP.mult)
                    seg.op("vector", "tensor_tensor", out=dst[:, h2sl], in0=sqb[1][:],
                           in1=rot_sb[:, tsl], op=OP.mult)
                    seg.chain("vector", cm.s_v)
                    seg.op("vector", "tensor_tensor", out=sqb[0][:], in0=sqb[0][:],
                           in1=rot_sb[:, T + tc * 512: T + (tc + 1) * 512], op=OP.mult)
                    seg.op("vector", "tensor_tensor", out=sqb[1][:], in0=sqb[1][:],
                           in1=rot_sb[:, T + tc * 512: T + (tc + 1) * 512], op=OP.mult)
                    seg.chain("vector", cm.s_v)
                    seg.op("vector", "tensor_sub", out=dst[:, h1sl], in0=dst[:, h1sl], in1=sqb[1][:])
                    seg.op("vector", "tensor_add", out=dst[:, h2sl], in0=dst[:, h2sl], in1=sqb[0][:])
                    seg.inc(cm.s_v, 1, eng="vector")
                    done_v = cm.s_v.n
                    if to_ck:
                        seg.wait("sync", cm.s_v, done_v)
                        seg.dma("sync", ck_in[hg * 128 + 0 * 256:(hg + 1) * 128, tc * 512:(tc + 1) * 512]
                                if False else ck_in[hg * 128: (hg + 1) * 128, tc * 512:(tc + 1) * 512],
                                dst[:, h1sl], s_exk[dma_par])
                        seg.dma("sync", ck_in[256 + hg * 128: 256 + (hg + 1) * 128, tc * 512:(tc + 1) * 512],
                                dst[:, h2sl], s_exk[dma_par])
                        dma_par = (dma_par + 1) % 2
            return cm.s_v.n

        q_done = qk_build(wq_sb, (qs_sb[:, l * 2:l * 2 + 1], qs_sb[:, l * 2 + 1:l * 2 + 2]),
                          rotq_sb, q_sb, False)
        k_done = qk_build(wk_sb, (ks_sb[:, l * 2:l * 2 + 1], ks_sb[:, l * 2 + 1:l * 2 + 2]),
                          rotk_sb, o_sb, True)
        vb_dma = [0, 0]
        for tt in range(NT):
            vpar = tt % 2
            par = tt % 2
            if guard[par]:
                seg.wait("tensor", guard[par][0], guard[par][1])
            for c in range(NC):
                seg.op("tensor", "matmul", p_ps[par][:],
                       hs_bf[:, c * T + tt * 128: c * T + (tt + 1) * 128],
                       wv_sb[:, c * 512:(c + 1) * 512],
                       start=(c == 0), stop=(c == NC - 1))
            seg.inc(cm.s_t, 1, eng="tensor")
            seg.wait("scalar", cm.s_t)
            if vb_dma[vpar]:
                seg.wait("scalar", s_exv[vpar], vb_dma[vpar])
            seg.op("scalar", "copy", out=vb[vpar][:], in_=p_ps[par][:])
            seg.inc(cm.s_s, 1, eng="scalar")
            guard[par] = (cm.s_s, cm.s_s.n)
            seg.wait("sync", cm.s_s)
            seg.dma("sync", cv_in[tt * 128:(tt + 1) * 128, :], vb[vpar][:], s_exv[vpar])
            vb_dma[vpar] = s_exv[vpar].n

        for s in (s_exk[0], s_exk[1], s_exv[0], s_exv[1]):
            seg.wait("gpsimd", s, s.n)
        seg.wait("gpsimd", s_cc, s_cc.n)
        seg.op("gpsimd", "collective_compute", "AllGather", OP.bypass, replica_groups=rg,
               ins=[cc_in[None, :].opt()], outs=[cc_out[None, :].opt()])
        seg.inc(s_cc, None, eng="gpsimd")
        cc_n = s_cc.n

        seg.wait("vector", s_par, par_n)
        seg.op("vector", "reduce_max", out=strow[0:1, 0:1], in_=qsr_sb[0:1, l * 64:(l + 1) * 64],
               axis=AX.X, apply_absolute_value=True)
        seg.chain("vector", cm.s_v)
        seg.op("vector", "reduce_max", out=strow[0:1, 1:2], in_=ksr_sb[0:1, l * 64:(l + 1) * 64],
               axis=AX.X, apply_absolute_value=True)
        seg.chain("vector", cm.s_v)
        seg.op("vector", "tensor_scalar", out=strow[0:1, 16:32], in0=msc_sb[:],
               scalar1=strow[0:1, 0:1], scalar2=strow[0:1, 1:2], op0=OP.mult, op1=OP.mult)
        seg.chain("vector", cm.s_v)
        seg.op("vector", "tensor_scalar_mul", out=mrow_bf[0:1, 0:16], in0=strow[0:1, 16:32], scalar1=-1.0)
        seg.inc(cm.s_v, 1, eng="vector")
        m_v = cm.s_v.n
        seg.wait("tensor", cm.s_v, m_v)
        seg.wait("tensor", cm.s_s)
        for i in range(16):
            seg.op("tensor", "matmul", bcn_ps[:, i:i + 1], cm.ones_row[:], mrow_bf[0:1, i:i + 1],
                   start=True, stop=True)
        seg.inc(cm.s_t, 1, eng="tensor")
        seg.wait("scalar", cm.s_t)
        seg.op("scalar", "copy", out=ebias_sb[:], in_=bcn_ps[:, 0:16])
        seg.inc(cm.s_s, 1, eng="scalar")
        ebias_s = cm.s_s.n

        es_p.close()
        es_b.close()
        es_a = ExitStack()
        NKT = 12
        kstb = [seg.sbuf(lname + f"kst{i}", [128, 2 * 128], BF16, stack=es_a) for i in range(NKT)]
        vstb = [seg.sbuf(lname + f"vst{i}", [128, 4 * 65], BF16, stack=es_a) for i in range(NKT)]
        P2 = [seg.sbuf(lname + f"P2_{i}", [128, 1024], BF16, stack=es_a) for i in range(2)]
        tmp_sb = seg.sbuf(lname + "tmp", [128, 512], BF16, stack=es_a)
        bcs_sb = seg.sbuf(lname + "bcs", [128, 1024], F32, stack=es_a)
        rzrow = seg.sbuf(lname + "rz", [128, 1024], F32, stack=es_a)
        rzb = seg.sbuf(lname + "rzb", [128, 1024], BF16, stack=es_a)
        stp = [seg.psum(lname + f"st{i}", [128, 1024], F32, stack=es_a) for i in range(2)]
        av2_ps = seg.psum(lname + "av2", [128, 1024], F32, stack=es_a)
        bc2_ps = seg.psum(lname + "bc2", [128, 512], F32, stack=es_a)
        kst_cons = [0] * NKT
        vst_cons = [0] * NKT
        st_read = [0, 0]
        P_read = [0, 0]
        av_evac = 0
        rz_free = 0
        prev_copy_s = 0
        tmp_dma = 0
        gate_t = cm.s_t.n
        gate_v = cm.s_v.n
        gate_s = cm.s_s.n
        seg.wait("vector", cm.s_t, gate_t)
        for b in range(NKT):
            seg.wait("vector", cm.s_v, gate_v)
            for m in range(4):
                seg.op("vector", "memset", vstb[b][:, m * 65 + 64:m * 65 + 65], 1.0)
        seg.inc(cm.s_v, 1, eng="vector")
        vst_init = cm.s_v.n

        def load_kvq(r, j, hg, b):
            if kst_cons[b]:
                seg.wait("gpsimd", cm.s_t, kst_cons[b])
            else:
                seg.wait("gpsimd", cm.s_t, gate_t)
                seg.wait("gpsimd", cm.s_s, gate_s)
                seg.wait("gpsimd", cm.s_v, vst_init)
            if vst_cons[b]:
                seg.wait("gpsimd", cm.s_t, vst_cons[b])
            seg.wait("gpsimd", s_cc, cc_n)
            seg.dma("gpsimd", kstb[b][:, 0:128],
                    ck_out[r][hg * 128:(hg + 1) * 128, j * 128:(j + 1) * 128], s_kk[0])
            seg.dma("gpsimd", kstb[b][:, 128:256],
                    ck_out[r][256 + hg * 128: 256 + (hg + 1) * 128, j * 128:(j + 1) * 128], s_kk[0])
            seg.dma("gpsimd", vstb[b][:].rearrange("p (m c) -> p m c", m=4)[:, :, 0:64],
                    cv_out[r][j * 128:(j + 1) * 128, hg * 256:(hg + 1) * 256].rearrange("k (m c) -> k m c", m=4),
                    s_vv[0])

        for I in range(NT // 4):
            q0 = I * 512
            kts = [(r, j) for j in range(max(0, 4 * I - 2), 4 * I + 4) for r in range(2)]
            for hg in range(2):
                kk_base = s_kk[0].n
                vv_base = s_vv[0].n
                for b, (r, j) in enumerate(kts):
                    load_kvq(r, j, hg, b)
                for mh in range(2):
                    exp_cnt = {}

                    def emit_av(kt):
                        kp = kt % 2
                        seg.wait("tensor", cm.s_s, exp_cnt[kt])
                        if kt == 0 and av_evac:
                            seg.wait("tensor", cm.s_v, av_evac)
                        for h in range(2):
                            m = 2 * mh + h
                            seg.op("tensor", "matmul", av2_ps[0:65, h * 512:(h + 1) * 512],
                                   vstb[kt][:, m * 65:(m + 1) * 65],
                                   P2[kp][:, h * 512:(h + 1) * 512],
                                   start=(kt == 0), stop=(kt == len(kts) - 1))
                        seg.inc(cm.s_t, 1, eng="tensor")
                        P_read[kp] = cm.s_t.n
                        vst_cons[kt] = cm.s_t.n

                    for ktidx, (r, j) in enumerate(kts):
                        kpar = ktidx % 2
                        jd = j - 4 * I
                        pidx = r * 6 + (jd + 2)
                        seg.wait("tensor", s_kk[0], kk_base + 32 * (ktidx + 1))
                        seg.wait("tensor", s_vv[0], vv_base + 16 * (ktidx + 1))
                        seg.wait("tensor", cm.s_s, max(st_read[kpar], ebias_s))
                        for h in range(2):
                            m = 2 * mh + h
                            b32 = 32 * m
                            kw = {"tile_position": (b32, 0)} if b32 >= 96 else {}
                            seg.op("tensor", "matmul", stp[kpar][:, h * 512:(h + 1) * 512],
                                   kstb[ktidx][b32:b32 + 32, 0:128],
                                   q_sb[b32:b32 + 32, hg * T + q0: hg * T + q0 + 512],
                                   start=True, stop=False, **kw)
                            seg.op("tensor", "matmul", stp[kpar][:, h * 512:(h + 1) * 512],
                                   kstb[ktidx][b32:b32 + 32, 128:256],
                                   q_sb[b32:b32 + 32, (2 + hg) * T + q0: (2 + hg) * T + q0 + 512],
                                   start=False, stop=True, **kw)
                        seg.inc(cm.s_t, 1, eng="tensor")
                        kst_cons[ktidx] = cm.s_t.n
                        if ktidx > 0:
                            emit_av(ktidx - 1)
                        seg.wait("vector", cm.s_t, kst_cons[ktidx])
                        for h in range(2):
                            seg.op("vector", "tensor_tensor", out=stp[kpar][:, h * 512:(h + 1) * 512],
                                   in0=stp[kpar][:, h * 512:(h + 1) * 512],
                                   in1=amask_sb[:, pidx * 512:(pidx + 1) * 512], op=OP.add)
                        seg.inc(cm.s_v, 1, eng="vector")
                        seg.wait("scalar", cm.s_v)
                        seg.wait("scalar", cm.s_s, ebias_s)
                        if P_read[kpar]:
                            seg.wait("scalar", cm.s_t, P_read[kpar])
                        for h in range(2):
                            seg.op("scalar", "activation", out=P2[kpar][:, h * 512:(h + 1) * 512],
                                   in_=stp[kpar][:, h * 512:(h + 1) * 512],
                                   func=AF.Exp, scale=e8, bias=ebias_sb[:, 0:1])
                        seg.inc(cm.s_s, 1, eng="scalar")
                        st_read[kpar] = cm.s_s.n
                        exp_cnt[ktidx] = cm.s_s.n
                    emit_av(len(kts) - 1)
                    av_t = cm.s_t.n
                    seg.wait("scalar", cm.s_t, av_t)
                    if rz_free:
                        seg.wait("scalar", cm.s_v, rz_free)
                    for h in range(2):
                        seg.op("scalar", "activation", out=rzrow[64:65, h * 512:(h + 1) * 512],
                               in_=av2_ps[64:65, h * 512:(h + 1) * 512], func=AF.Ln)
                    seg.chain("scalar", cm.s_s)
                    cm.constf(-1.0)
                    seg.op("scalar", "activation", out=rzrow[64:65, 0:1024],
                           in_=rzrow[64:65, 0:1024], func=AF.Exp,
                           scale=cm.cpool[64:65, cm._cvals[-1.0]:cm._cvals[-1.0] + 1])
                    seg.inc(cm.s_s, 1, eng="scalar")
                    seg.wait("vector", cm.s_s)
                    seg.wait("vector", cm.s_t, av_t)
                    seg.op("vector", "tensor_copy", out=rzb[64:65, 0:1024], in_=rzrow[64:65, 0:1024])
                    seg.inc(cm.s_v, 1, eng="vector")
                    rb_v = cm.s_v.n
                    rz_free = rb_v
                    for h in range(2):
                        seg.wait("tensor", cm.s_v, rb_v)
                        if prev_copy_s:
                            seg.wait("tensor", cm.s_s, prev_copy_s)
                        seg.op("tensor", "matmul", bc2_ps[0:64, 0:512], ones64r[64:65, :],
                               rzb[64:65, h * 512:(h + 1) * 512], start=True, stop=True)
                        seg.inc(cm.s_t, 1, eng="tensor")
                        seg.wait("scalar", cm.s_t)
                        if h == 0 and av_evac:
                            seg.wait("scalar", cm.s_v, av_evac)
                        seg.op("scalar", "copy", out=bcs_sb[0:64, h * 512:(h + 1) * 512],
                               in_=bc2_ps[0:64, 0:512])
                        seg.inc(cm.s_s, 1, eng="scalar")
                        prev_copy_s = cm.s_s.n
                    seg.wait("vector", cm.s_s)
                    if tmp_dma:
                        seg.wait("vector", s_ysh, tmp_dma)
                    ychunk = 2 * hg + mh
                    seg.op("vector", "tensor_tensor",
                           out=o_sb[0:64, ychunk * T + q0: ychunk * T + q0 + 512],
                           in0=av2_ps[0:64, 0:512], in1=bcs_sb[0:64, 0:512], op=OP.mult)
                    seg.op("vector", "tensor_tensor", out=tmp_sb[0:64, 0:512],
                           in0=av2_ps[0:64, 512:1024], in1=bcs_sb[0:64, 512:1024], op=OP.mult)
                    seg.inc(cm.s_v, 1, eng="vector")
                    av_evac = cm.s_v.n
                    seg.wait("sync", cm.s_v, av_evac)
                    seg.dma("sync", o_sb[64:128, ychunk * T + q0: ychunk * T + q0 + 512],
                            tmp_sb[0:64, 0:512], s_ysh)
                    tmp_dma = s_ysh.n
        o_done_v = cm.s_v.n
        o_dma = s_ysh.n

        es_a.close()
        es_o = ExitStack()
        wo_sb = seg.sbuf(lname + "wo", [128, 4 * 1024], BF16, stack=es_o)
        o_ps2 = [seg.psum(lname + f"o2_{i}", [128, 512], F32, stack=es_o) for i in range(2)]
        seg.wait("gpsimd", cm.s_t)
        seg.wait("gpsimd", cm.s_s)
        seg.wait("gpsimd", cm.s_v, o_done_v)
        seg.wait("gpsimd", s_ysh, s_ysh.n)
        seg.dma("gpsimd", wo_sb[:].rearrange("p (c m) -> p c m", c=4),
                wo_d[l].rearrange("(c p) m -> p c m", p=128), s_w)
        wo_n = s_w.n
        seg.wait("tensor", s_w, wo_n)
        seg.wait("tensor", cm.s_v, o_done_v)
        seg.wait("tensor", s_ysh, o_dma)
        oguard = [None, None]
        for dc in range(NC):
            for tc in range(NQC):
                par = (dc * NQC + tc) % 2
                if oguard[par]:
                    seg.wait("tensor", oguard[par][0], oguard[par][1])
                for kc in range(4):
                    seg.op("tensor", "matmul", o_ps2[par][:],
                           wo_sb[:, kc * 1024 + dc * 128: kc * 1024 + dc * 128 + 128],
                           o_sb[:, kc * T + tc * 512: kc * T + (tc + 1) * 512],
                           start=(kc == 0), stop=(kc == 3))
                seg.inc(cm.s_t, 1, eng="tensor")
                seg.wait("vector", cm.s_t)
                xsl = slice(dc * T + tc * 512, dc * T + (tc + 1) * 512)
                seg.op("vector", "tensor_tensor", out=hcur[:, xsl], in0=o_ps2[par][:],
                       in1=hcur[:, xsl], op=OP.add)
                seg.inc(cm.s_v, 1, eng="vector")
                oguard[par] = (cm.s_v, cm.s_v.n)
        hcur_v = cm.s_v.n
        es_o.close()
        es_q.close()

        es_f = ExitStack()
        hs2 = seg.sbuf(lname + "hs2", [128, NC * T], BF16, stack=es_f)
        sh_v = build_shift(hs2, es_f, hcur_v)
        es_ln2 = ExitStack()
        lnf = hs2
        lnr = emit_ln_fm(seg, cm, hs2, T, lnf, es_ln2, name=lname + "ln2",
                         gb=(lng_sb[:, l * 8:(l + 1) * 8], lnb_sb[:, l * 8:(l + 1) * 8]),
                         gb_ready=(s_par, par_n), xs_ready=[(cm.s_v, sh_v)])
        es_ln2.close()
        lnf_v = lnr["h_v"]
        w1b = [seg.sbuf(lname + f"w1b{i}", [128, NC * 256], BF16, stack=es_f) for i in range(2)]
        w2b = [seg.sbuf(lname + f"w2b{i}", [128, HIDC * 128], BF16, stack=es_f) for i in range(2)]
        mid = seg.sbuf(lname + "mid", [128, HIDC * 1024], BF16, stack=es_f)
        sg2 = seg.sbuf(lname + "sg2", [128, 512], BF16, stack=es_f)
        f_ps = [seg.psum(lname + f"f{i}", [128, 512], F32, stack=es_f) for i in range(2)]
        s_w1 = [seg.sem(lname + "sw1a"), seg.sem(lname + "sw1b")]
        s_w2 = [seg.sem(lname + "sw2a"), seg.sem(lname + "sw2b")]
        w1_cons = [0, 0]
        w2_cons = [0, 0]

        def load_w1(oc, par):
            if w1_cons[par]:
                seg.wait("gpsimd", cm.s_t, w1_cons[par])
            else:
                seg.wait("gpsimd", cm.s_t)
                seg.wait("gpsimd", cm.s_s)
                seg.wait("gpsimd", cm.s_v, lnf_v)
            seg.dma("gpsimd", w1b[par][:].rearrange("p (c m) -> p c m", c=NC),
                    w1_d[l, :, oc * 256:(oc + 1) * 256].rearrange("(c p) m -> p c m", p=128),
                    s_w1[par])
            return s_w1[par].n

        def load_w2(dc, par):
            if w2_cons[par]:
                seg.wait("gpsimd", cm.s_t, w2_cons[par])
            else:
                seg.wait("gpsimd", cm.s_t)
            seg.dma("gpsimd", w2b[par][:].rearrange("p (c m) -> p c m", c=HIDC),
                    w2_d[l, :, dc * 128:(dc + 1) * 128].rearrange("(c p) m -> p c m", p=128),
                    s_w2[par])
            return s_w2[par].n

        fguard = [None, None]
        for th in range(2):
            w1cnt = {}
            w1cnt[0] = load_w1(0, 0)
            w1cnt[1] = load_w1(1, 1)
            for oc in range(HIDC):
                par = oc % 2
                seg.wait("tensor", s_w1[par], w1cnt[oc])
                if th == 0 and oc == 0:
                    seg.wait("tensor", cm.s_v, lnf_v)
                for tc in range(2):
                    tco = th * 2 + tc
                    spar = tc
                    if fguard[spar]:
                        seg.wait("tensor", fguard[spar][0], fguard[spar][1])
                    for c in range(NC):
                        seg.op("tensor", "matmul", f_ps[spar][:],
                               w1b[par][:, c * 256 + 128: c * 256 + 256],
                               lnf[:, c * T + tco * 512: c * T + (tco + 1) * 512],
                               start=(c == 0), stop=(c == NC - 1))
                    seg.inc(cm.s_t, 1, eng="tensor")
                    g_grp = cm.s_t.n
                    opar = (spar + 1) % 2
                    if fguard[opar]:
                        seg.wait("tensor", fguard[opar][0], fguard[opar][1])
                    for c in range(NC):
                        seg.op("tensor", "matmul", f_ps[opar][:],
                               w1b[par][:, c * 256: c * 256 + 128],
                               lnf[:, c * T + tco * 512: c * T + (tco + 1) * 512],
                               start=(c == 0), stop=(c == NC - 1))
                    seg.inc(cm.s_t, 1, eng="tensor")
                    a_grp = cm.s_t.n
                    seg.wait("scalar", cm.s_t, g_grp)
                    seg.wait("scalar", cm.s_v)
                    seg.op("scalar", "activation", out=sg2[:], in_=f_ps[spar][:], func=AF.Gelu,
                           bias=b1_sb[:, l * 44 + 2 * oc + 1: l * 44 + 2 * oc + 2])
                    seg.inc(cm.s_s, 1, eng="scalar")
                    fguard[spar] = (cm.s_s, cm.s_s.n)
                    seg.wait("vector", cm.s_s)
                    seg.wait("vector", cm.s_t, a_grp)
                    seg.op("vector", "scalar_tensor_tensor",
                           out=mid[:, oc * 1024 + tc * 512: oc * 1024 + (tc + 1) * 512],
                           in0=f_ps[opar][:], scalar=b1_sb[:, l * 44 + 2 * oc: l * 44 + 2 * oc + 1],
                           in1=sg2[:], op0=OP.add, op1=OP.mult)
                    seg.inc(cm.s_v, 1, eng="vector")
                    fguard[opar] = (cm.s_v, cm.s_v.n)
                w1_cons[par] = cm.s_t.n
                if oc + 2 < HIDC:
                    w1cnt[oc + 2] = load_w1(oc + 2, par)
            mid_v = cm.s_v.n
            w2cnt = {}
            w2cnt[0] = load_w2(0, 0)
            w2cnt[1] = load_w2(1, 1)
            seg.wait("tensor", cm.s_v, mid_v)
            for dc in range(NC):
                par = dc % 2
                seg.wait("tensor", s_w2[par], w2cnt[dc])
                for tc in range(2):
                    tco = th * 2 + tc
                    spar = tc
                    if fguard[spar]:
                        seg.wait("tensor", fguard[spar][0], fguard[spar][1])
                    for kc in range(HIDC):
                        seg.op("tensor", "matmul", f_ps[spar][:],
                               w2b[par][:, kc * 128:(kc + 1) * 128],
                               mid[:, kc * 1024 + tc * 512: kc * 1024 + (tc + 1) * 512],
                               start=(kc == 0), stop=(kc == HIDC - 1))
                    seg.inc(cm.s_t, 1, eng="tensor")
                    seg.wait("vector", cm.s_t)
                    xsl = slice(dc * T + tco * 512, dc * T + (tco + 1) * 512)
                    seg.op("vector", "scalar_tensor_tensor", out=hcur[:, xsl], in0=f_ps[spar][:],
                           scalar=b2_sb[:, l * 8 + dc: l * 8 + dc + 1], in1=hcur[:, xsl],
                           op0=OP.add, op1=OP.add)
                    seg.inc(cm.s_v, 1, eng="vector")
                    fguard[spar] = (cm.s_v, cm.s_v.n)
                w2_cons[par] = cm.s_t.n
                if dc + 2 < NC:
                    w2cnt[dc + 2] = load_w2(dc + 2, par)
        hcur_v = cm.s_v.n
        es_f.close()

    es_fin = ExitStack()
    xt = [seg.sbuf(name + f"xt{i}", [128, 2048], F32, stack=es_fin) for i in range(2)]
    s_fin = [seg.sem(name + "sfin0"), seg.sem(name + "sfin1")]
    s_st2 = [seg.sem(name + "sst0"), seg.sem(name + "sst1")]
    seg.wait("sync", cm.s_v, hcur_v)
    for c in range(NC):
        par = c % 2
        if c >= 2:
            seg.wait("sync", s_st2[par], s_st2[par].n)
        seg.dma("sync", xt[par][:, 0:T], x_in[c * 128:(c + 1) * 128, :], s_fin[par])
        seg.wait("vector", s_fin[par], s_fin[par].n)
        seg.op("vector", "tensor_tensor", out=xt[par][:, 0:T], in0=xt[par][:, 0:T],
               in1=hcur[:, c * T:(c + 1) * T], op=OP.add)
        seg.inc(cm.s_v, 1, eng="vector")
        seg.wait("sync", cm.s_v)
        seg.dma("sync", x_out[c * 128:(c + 1) * 128, :], xt[par][:, 0:T], s_st2[par])
    seg.wait("sync", s_st2[0], s_st2[0].n)
    seg.wait("sync", s_st2[1], s_st2[1].n)
    es_fin.close()

    if standalone:
        seg.emit()
        return seg
    return None


def host_prep_s6(inputs, core, T=2048, wtag="6"):
    p = core % 2
    NL = 2
    wqkv = inputs["L1_ffn_wqkv"].copy()
    perm = np.concatenate([np.arange(8)[:, None] * 64 + np.arange(32)[None, :],
                           np.arange(8)[:, None] * 64 + 32 + np.arange(32)[None, :]], 0).reshape(-1)
    wq = wqkv[:, :, 0:512][:, :, perm]
    wk = wqkv[:, :, 512:1024][:, :, perm]
    wv = wqkv[:, :, 1024:1536]
    d = {"wqkv" + wtag: np.ascontiguousarray(np.concatenate([wq, wk, wv], axis=2)),
         "wo" + wtag: np.ascontiguousarray(inputs["L1_ffn_wo"])}
    qs = inputs["L1_ffn_q_scale"]; ks = inputs["L1_ffn_k_scale"]
    def pat(s):
        out = np.zeros((128, NL, 2), np.float32)
        for l in range(NL):
            for half in range(2):
                out[:, l, half] = np.tile(s[l, half * 32:(half + 1) * 32], 4)
        return out
    d["qs" + wtag] = pat(qs)
    d["ks" + wtag] = pat(ks)
    d["qsr" + wtag] = qs.reshape(1, -1).astype(np.float32)
    d["ksr" + wtag] = ks.reshape(1, -1).astype(np.float32)
    DIM_HEAD, WINDOW, n = 64, 512, 4096
    half_idx = np.arange(0, DIM_HEAD, 2, dtype=np.float64)
    inv_freq = 1.0 / 10000.0 ** (half_idx / DIM_HEAD)
    t = np.arange(n, dtype=np.float64)
    freqs = t[:, None] * inv_freq[None, :]
    base = (half_idx + 0.4 * DIM_HEAD) / (1.4 * DIM_HEAD)
    power = (t - n // 2) / (WINDOW // 2)
    scale = base[None, :] ** power[:, None]
    cos, sin = np.cos(freqs), np.sin(freqs)
    own = own_token_index(core)
    def rot_tab(sgn):
        out = np.zeros((128, NL, 2, T), np.float64)
        cs = cos[own] * (scale[own] ** sgn)
        sn = sin[own] * (scale[own] ** sgn)
        for l in range(NL):
            out[:, l, 0, :] = np.tile(cs.T, (4, 1))
            out[:, l, 1, :] = np.tile(sn.T, (4, 1))
        return out.astype(ml_dtypes.bfloat16)
    d["rotq" + wtag] = rot_tab(+1)
    d["rotk" + wtag] = rot_tab(-1)
    def fm(v, w):
        return np.ascontiguousarray(v.reshape(NL, w, 128).transpose(2, 0, 1))
    d["lng" + wtag] = fm(inputs["L1_ffn_ln_g"], 8)
    d["lnb" + wtag] = fm(inputs["L1_ffn_ln_b"], 8)
    w1 = inputs["L1_ffn_w1"]; b1 = inputs["L1_ffn_b1"]
    w1p = np.zeros((NL, 1024, 5632), np.float32)
    b1p = np.zeros((NL, 5632), np.float32)
    for oc in range(22):
        for half in range(2):
            src0 = half * 2730 + oc * 128
            nsz = max(0, min(128, 2730 - oc * 128))
            if nsz > 0:
                w1p[:, :, oc * 256 + half * 128: oc * 256 + half * 128 + nsz] = w1[:, :, src0:src0 + nsz]
                b1p[:, oc * 256 + half * 128: oc * 256 + half * 128 + nsz] = b1[:, src0:src0 + nsz]
    d["w1p" + wtag] = w1p
    b1f = np.zeros((128, NL, 44), np.float32)
    for l in range(NL):
        for oc in range(22):
            for half in range(2):
                b1f[:, l, 2 * oc + half] = b1p[l, oc * 256 + half * 128: oc * 256 + (half + 1) * 128]
    d["b1p" + wtag] = b1f
    w2 = inputs["L1_ffn_w2"]
    w2p = np.zeros((NL, 2816, 1024), np.float32)
    w2p[:, 0:2730] = w2
    d["w2p" + wtag] = w2p
    d["b2f" + wtag] = fm(inputs["L1_ffn_b2"], 8)
    NEG = np.float32(-1e30)
    am = np.zeros((128, 12 * 512), np.float32)
    pk = np.arange(128)[:, None]; fq = np.arange(128)[None, :]
    for r in range(2):
        for jd in range(-2, 4):
            pidx = r * 6 + (jd + 2)
            for isub in range(4):
                D = 2 * (isub - jd) + p - r
                qk = 128 * D + fq - pk
                keep = (qk >= 0) & (qk <= 512)
                am[:, pidx * 512 + isub * 128: pidx * 512 + (isub + 1) * 128] = \
                    np.where(keep, 0.0, NEG)
    d["amask" + wtag] = am
    hs = np.zeros((128, 2), np.float32)
    hs[:, 0] = 1.0 if p == 1 else 0.0
    hs[:, 1] = 1.0 if p == 0 else 0.0
    d["hsel" + wtag] = hs
    msc = np.full((1, 16), 16.0, np.float32)
    d["msc" + wtag] = msc
    return d




def sched(qc):
    i0 = 4 * qc
    out = []
    for r in range(2):
        for j in range(i0 + 4):
            out.append((r, j, max(0, j - i0) * 128))
    return out


def build_s7(nc, standalone=True, xs=None, cm=None, seg=None, T=2048, name="s7",
             rg=RG_PAIRS, dram_prefix="", io_dram=None):
    NC = 8
    NQC = T // 512
    NKT_OWN = T // 128
    NL = 4

    if standalone:
        seg = Seg(nc)
    if standalone:
        x_in = nc.dram_tensor("x_in", [1024, T], F32, kind="ExternalInput").ap()
        x_out = nc.dram_tensor("x_out", [1024, T], F32, kind="ExternalOutput").ap()
    elif io_dram is not None:
        x_in, x_out = io_dram
    wkqv_d = nc.dram_tensor(dram_prefix + "wkqv7", [NL, 1024, 3072], F32, kind="ExternalInput").ap()
    wm1_d = nc.dram_tensor(dram_prefix + "wm17", [NL, 1024, 2048], F32, kind="ExternalInput").ap()
    wm2_d = nc.dram_tensor(dram_prefix + "wm27", [NL, 1024, 1024], F32, kind="ExternalInput").ap()
    wf1_d = nc.dram_tensor(dram_prefix + "wf17", [NL, 1024, 2048], F32, kind="ExternalInput").ap()
    wf2_d = nc.dram_tensor(dram_prefix + "wf27", [NL, 2048, 1024], F32, kind="ExternalInput").ap()
    bm1_d = nc.dram_tensor(dram_prefix + "bm1f", [128, NL, 16], F32, kind="ExternalInput").ap()
    bm2_d = nc.dram_tensor(dram_prefix + "bm2f", [128, NL, 8], F32, kind="ExternalInput").ap()
    gng_d = nc.dram_tensor(dram_prefix + "gngf", [128, NL, 8], F32, kind="ExternalInput").ap()
    gnb_d = nc.dram_tensor(dram_prefix + "gnbf", [128, NL, 8], F32, kind="ExternalInput").ap()
    bf1_d = nc.dram_tensor(dram_prefix + "bf1f", [128, NL, 16], F32, kind="ExternalInput").ap()
    bf2_d = nc.dram_tensor(dram_prefix + "bf2f", [128, NL, 8], F32, kind="ExternalInput").ap()
    lng_d = nc.dram_tensor(dram_prefix + "lngf", [128, NL, 8], F32, kind="ExternalInput").ap()
    lnb_d = nc.dram_tensor(dram_prefix + "lnbf", [128, NL, 8], F32, kind="ExternalInput").ap()
    dmask_d = nc.dram_tensor(dram_prefix + "dmask7", [128, 256], BF16, kind="ExternalInput").ap()
    rn_d = nc.dram_tensor(dram_prefix + "rn32inv", [1, T], F32, kind="ExternalInput").ap()

    CCN = 2 * 1024 * T
    cc_in = nc.dram_tensor(name + "cc_in", [CCN], BF16, kind="Internal").ap()
    cck_out = nc.dram_tensor(name + "cck_out", [2 * 1024 * T], BF16, kind="Internal").ap()
    ccv_out = nc.dram_tensor(name + "ccv_out", [2 * T * 1024], BF16, kind="Internal").ap()
    ck_in = cc_in[0:1024 * T].rearrange("(a b) -> a b", b=T)
    cv_in = cc_in[1024 * T:2 * 1024 * T].rearrange("(a b) -> a b", b=1024)
    ck_out = [cck_out[r * 1024 * T: (r + 1) * 1024 * T].rearrange("(a b) -> a b", b=T) for r in range(2)]
    cv_out = [ccv_out[r * T * 1024: (r + 1) * T * 1024].rearrange("(a b) -> a b", b=1024) for r in range(2)]
    gn_in = nc.dram_tensor(name + "gn_in", [1, 2], F32, kind="Internal").ap()
    gn_out = nc.dram_tensor(name + "gn_out", [1, 2], F32, kind="Internal").ap()
    m2_dram = nc.dram_tensor(name + "m2d", [1024, T], F32, kind="Internal").ap()

    par_sb = seg_params = None
    if standalone:
        cm = Common(seg)
    bm1_sb = seg.sbuf(name + "bm1", [128, NL * 16], F32)
    bm2_sb = seg.sbuf(name + "bm2", [128, NL * 8], F32)
    gng_sb = seg.sbuf(name + "gng", [128, NL * 8], F32)
    gnb_sb = seg.sbuf(name + "gnb", [128, NL * 8], F32)
    bf1_sb = seg.sbuf(name + "bf1", [128, NL * 16], F32)
    bf2_sb = seg.sbuf(name + "bf2", [128, NL * 8], F32)
    lng_sb = seg.sbuf(name + "lng", [128, NL * 8], F32)
    lnb_sb = seg.sbuf(name + "lnb", [128, NL * 8], F32)
    dmask_sb = seg.sbuf(name + "dmask", [128, 256], BF16)
    rn_sb = seg.sbuf(name + "rn", [1, 512], F32)
    rows = seg.sbuf(name + "rows", [1, 1024], F32)
    rows_bf = seg.sbuf(name + "rowsbf", [1, 512], BF16)
    onesf = seg.sbuf(name + "onesf", [128, 1], F32)
    gn_sb = seg.sbuf(name + "gnsb", [1, 8], F32)
    grep = seg.sbuf(name + "grep", [128, 4], F32)
    coef_sb = seg.sbuf(name + "coef", [128, 8], F32)
    offc_sb = seg.sbuf(name + "offc", [128, 8], F32)
    s_w = seg.sem(name + "sw")
    s_par = seg.sem(name + "spar")
    seg.op("vector", "memset", onesf[:], 1.0)
    seg.inc(cm.s_init, 1, eng="vector")
    for sb, d in ((bm1_sb, bm1_d), (bm2_sb, bm2_d), (gng_sb, gng_d), (gnb_sb, gnb_d),
                  (bf1_sb, bf1_d), (bf2_sb, bf2_d), (lng_sb, lng_d), (lnb_sb, lnb_d)):
        seg.dma("gpsimd", sb[:].rearrange("p (l c) -> p l c", l=NL), d[:], s_par)
    seg.dma("gpsimd", dmask_sb[:], dmask_d[:], s_par)

    par_n = s_par.n

    hcur = seg.sbuf(name + "hcur", [128, NC * T], F32)

    es_xs = ExitStack()
    if standalone or io_dram is not None:
        xs = seg.sbuf(name + "xsl", [128, NC * T], F32, stack=es_xs)
        for c in range(NC):
            seg.dma("sync", xs[:, c * T:(c + 1) * T], x_in[c * 128:(c + 1) * 128, :], cm.s_ld)

    es_ln = ExitStack()
    ln = emit_ln_fm(seg, cm, xs, T, hcur, es_ln, name=name + "ln0",
                    xs_ready=[(cm.s_ld, cm.s_ld.n)])
    es_ln.close()
    if standalone or io_dram is not None:
        es_xs.close()

    s_exk = [seg.sem(name + "sexk0"), seg.sem(name + "sexk1")]
    s_exv = [seg.sem(name + "sexv0"), seg.sem(name + "sexv1")]
    s_kk = [seg.sem(name + "skk0"), seg.sem(name + "skk1"), seg.sem(name + "skk2")]
    s_vv = [seg.sem(name + "svv0"), seg.sem(name + "svv1"), seg.sem(name + "svv2")]
    s_cc = seg.sem(name + "scc")
    s_ccv = seg.sem(name + "sccv")
    s_m2 = [seg.sem(name + "sm20"), seg.sem(name + "sm21")]
    s_m2r = [seg.sem(name + "sm2r0"), seg.sem(name + "sm2r1")]
    s_gn = seg.sem(name + "sgn")
    s_rcc = seg.sem(name + "srcc")

    hcur_v = ln["h_v"]
    hcur_t_free = ln["t_done"]

    seg.wait("vector", s_par, par_n)
    seg.wait("scalar", s_par, par_n)
    seg.wait("tensor", s_par, par_n)
    for l in range(NL):
        lname = f"{name}l{l}"
        es_r = ExitStack()
        ret_sb = seg.sbuf(lname + "ret", [128, NC * T], BF16, stack=es_r)
        es_q = ExitStack()
        q_sb = seg.sbuf(lname + "q", [128, NC * T], BF16, stack=es_q)
        es_h = ExitStack()
        hb = seg.sbuf(lname + "hb", [128, NC * T], BF16, stack=es_h)
        w3 = [seg.sbuf(lname + f"w3_{i}", [128, NC * 1024], BF16, stack=es_h) for i in range(2)]
        kb = [seg.sbuf(lname + f"kb{i}", [128, 512], BF16, stack=es_h) for i in range(2)]
        vb = [seg.sbuf(lname + f"vb{i}", [128, 1024], BF16, stack=es_h) for i in range(2)]
        es_hp = ExitStack()
        p2 = [seg.psum(lname + f"p2_{i}", [128, 512], F32, stack=es_hp) for i in range(2)]

        seg.wait("vector", cm.s_v, hcur_v)
        seg.wait("vector", cm.s_t)
        seg.wait("vector", cm.s_s)
        for c in range(NC):
            seg.op("vector", "tensor_copy", out=hb[:, c * T:(c + 1) * T], in_=hcur[:, c * T:(c + 1) * T])
        seg.inc(cm.s_v, 1, eng="vector")
        hb_v = cm.s_v.n

        w3_cons = [0, 0]
        s_w3 = [seg.sem(lname + "sw3a"), seg.sem(lname + "sw3b")]

        def load_third(idx, par):
            if w3_cons[par]:
                seg.wait("gpsimd", cm.s_t, w3_cons[par])
            else:
                seg.wait("gpsimd", cm.s_t)
                seg.wait("gpsimd", cm.s_s)
                seg.wait("gpsimd", cm.s_v, hb_v)
            seg.dma("gpsimd", w3[par][:].rearrange("p (c m) -> p c m", c=NC),
                    wkqv_d[l, :, idx * 1024:(idx + 1) * 1024].rearrange("(c p) m -> p c m", p=128),
                    s_w3[par])
            return s_w3[par].n

        kw = load_third(0, 0)
        qw = load_third(1, 1)
        ps_read = [0, 0]
        kb_dma = [0, 0]
        seg.wait("tensor", s_w3[0], kw)
        seg.wait("tensor", cm.s_v, hb_v)
        evac = [("scalar", cm.s_s), ("vector", cm.s_v)]
        for oc in range(NC):
            for tc in range(NQC):
                par = (oc * NQC + tc) % 2
                een, esem = evac[par]
                if ps_read[par]:
                    seg.wait("tensor", esem, ps_read[par])
                for c in range(NC):
                    seg.op("tensor", "matmul", p2[par][:],
                           w3[0][:, c * 1024 + oc * 128: c * 1024 + oc * 128 + 128],
                           hb[:, c * T + tc * 512: c * T + (tc + 1) * 512],
                           start=(c == 0), stop=(c == NC - 1))
                seg.inc(cm.s_t, 1, eng="tensor")
                seg.wait(een, cm.s_t)
                spar = par
                if kb_dma[spar]:
                    seg.wait(een, s_exk[spar], kb_dma[spar])
                seg.op(een, "tensor_copy" if een == "vector" else "copy",
                       out=kb[spar][:], in_=p2[par][:])
                seg.inc(esem, 1, eng=een)
                ps_read[par] = esem.n
                seg.wait("sync", esem, ps_read[par])
                seg.dma("sync", ck_in[oc * 128:(oc + 1) * 128, tc * 512:(tc + 1) * 512], kb[spar][:], s_exk[spar])
                kb_dma[spar] = s_exk[spar].n
        k_done_t = cm.s_t.n
        w3_cons[0] = k_done_t
        vw = load_third(2, 0)
        for s in (s_exk[0], s_exk[1]):
            seg.wait("gpsimd", s, s.n)
        seg.op("gpsimd", "collective_compute", "AllGather", OP.bypass, replica_groups=rg,
               ins=[cc_in[0:1024 * T].rearrange("(a b) -> a b", b=1024).opt()],
               outs=[cck_out.rearrange("(a b) -> a b", b=1024).opt()])
        seg.inc(s_cc, None, eng="gpsimd")
        cc_n = s_cc.n
        seg.wait("tensor", s_w3[1], qw)
        for oc in range(NC):
            for tc in range(NQC):
                par = (oc * NQC + tc) % 2
                een, esem = evac[par]
                if ps_read[par]:
                    seg.wait("tensor", esem, ps_read[par])
                for c in range(NC):
                    seg.op("tensor", "matmul", p2[par][:],
                           w3[1][:, c * 1024 + oc * 128: c * 1024 + oc * 128 + 128],
                           hb[:, c * T + tc * 512: c * T + (tc + 1) * 512],
                           start=(c == 0), stop=(c == NC - 1))
                seg.inc(cm.s_t, 1, eng="tensor")
                seg.wait(een, cm.s_t)
                seg.op(een, "tensor_copy" if een == "vector" else "copy",
                       out=q_sb[:, oc * T + tc * 512: oc * T + (tc + 1) * 512],
                       in_=p2[par][:])
                seg.inc(esem, 1, eng=een)
                ps_read[par] = esem.n
        seg.wait("tensor", s_w3[0], vw)
        vb_dma = [0, 0]
        for tt in range(NKT_OWN):
            vpar = tt % 2
            for vc in range(2):
                par = vc
                een, esem = evac[par]
                if ps_read[par]:
                    seg.wait("tensor", esem, ps_read[par])
                for c in range(NC):
                    seg.op("tensor", "matmul", p2[par][:],
                           hb[:, c * T + tt * 128: c * T + (tt + 1) * 128],
                           w3[0][:, c * 1024 + vc * 512: c * 1024 + (vc + 1) * 512],
                           start=(c == 0), stop=(c == NC - 1))
                seg.inc(cm.s_t, 1, eng="tensor")
                seg.wait(een, cm.s_t)
                if vb_dma[vpar]:
                    seg.wait(een, s_exv[vpar], vb_dma[vpar])
                seg.op(een, "tensor_copy" if een == "vector" else "copy",
                       out=vb[vpar][:, vc * 512:(vc + 1) * 512], in_=p2[par][:])
                seg.inc(esem, 1, eng=een)
                ps_read[par] = esem.n
            seg.wait("sync", cm.s_s, ps_read[0])
            seg.wait("sync", cm.s_v, ps_read[1])
            seg.dma("sync", cv_in[tt * 128:(tt + 1) * 128, :], vb[vpar][:], s_exv[vpar])
            vb_dma[vpar] = s_exv[vpar].n
        qkv_done_s = cm.s_s.n
        qkv_done_t = cm.s_t.n

        for s in (s_exv[0], s_exv[1]):
            seg.wait("gpsimd", s, s.n)
        seg.op("gpsimd", "collective_compute", "AllGather", OP.bypass, replica_groups=rg,
               ins=[cc_in[1024 * T:2 * 1024 * T].rearrange("(a b) -> a b", b=1024).opt()],
               outs=[ccv_out.rearrange("(a b) -> a b", b=1024).opt()])
        seg.inc(s_ccv, None, eng="gpsimd")
        ccv_n = s_ccv.n

        es_hp.close()
        es_h.close()
        es_att = ExitStack()
        P_sb = seg.sbuf(lname + "P", [128, 32 * 512], BF16, stack=es_att)
        kst = [seg.sbuf(lname + f"kst{i}", [128, 1024], BF16, stack=es_att) for i in range(3)]
        vst = [seg.sbuf(lname + f"vst{i}", [128, 512], BF16, stack=es_att) for i in range(3)]
        rfb_sb = seg.sbuf(lname + "rfb", [128, 512], F32, stack=es_att)
        st_ps = [seg.psum(lname + f"st{i}", [128, 512], F32, stack=es_att) for i in range(2)]
        av_ps = [seg.psum(lname + f"av{i}", [128, 512], F32, stack=es_att) for i in range(4)]
        sum_ps = seg.psum(lname + "sum", [1, 512], F32, stack=es_att)
        bc_ps = seg.psum(lname + "bc", [128, 512], F32, stack=es_att)
        kst_cons = [0, 0, 0]
        vst_cons = [0, 0, 0]
        st_read = [0, 0]
        av_read_v = 0
        sum_read_v = 0
        rf_read_s = 0
        gate_t = cm.s_t.n
        gate_s = qkv_done_s
        gate_v = cm.s_v.n

        def load_k7(si_r, si_j, par):
            if kst_cons[par]:
                seg.wait("gpsimd", cm.s_s, kst_cons[par])
            else:
                seg.wait("gpsimd", cm.s_t, gate_t)
                seg.wait("gpsimd", cm.s_s, gate_s)
                seg.wait("gpsimd", cm.s_v, gate_v)
            seg.wait("gpsimd", s_cc, cc_n)
            seg.dma("gpsimd", kst[par][:].rearrange("p (c k) -> p c k", c=NC),
                    ck_out[si_r][:, si_j * 128:(si_j + 1) * 128].rearrange("(c p) k -> p c k", p=128),
                    s_kk[par])

        def load_v7(si_r, si_j, dcg, par):
            if vst_cons[par]:
                seg.wait("gpsimd", cm.s_t, vst_cons[par])
            else:
                seg.wait("gpsimd", cm.s_t, gate_t)
            seg.wait("gpsimd", s_ccv, ccv_n)
            seg.dma("gpsimd", vst[par][:],
                    cv_out[si_r][si_j * 128:(si_j + 1) * 128, dcg * 512:(dcg + 1) * 512], s_vv[par])

        rn_read_v = 0
        for qc in range(NQC):
            q0 = qc * 512
            i0 = 4 * qc
            sl = sched(qc)
            if rn_read_v:
                seg.wait("sync", cm.s_v, rn_read_v)
            seg.dma("sync", rn_sb[:], rn_d[0:1, q0:q0 + 512], s_gn)
            rn_n = s_gn.n
            kv_cnt = {}
            for pre in range(min(3, len(sl))):
                load_k7(sl[pre][0], sl[pre][1], pre)
                kv_cnt[pre] = s_kk[pre].n
            for si, (r, j, n0) in enumerate(sl):
                par = si % 3
                spar = si % 2
                seg.wait("tensor", s_kk[par], kv_cnt[si])
                if st_read[spar]:
                    seg.wait("tensor", cm.s_s, st_read[spar])
                for c in range(NC):
                    seg.op("tensor", "matmul", st_ps[spar][:, n0:512],
                           kst[par][:, c * 128:(c + 1) * 128],
                           q_sb[:, c * T + q0 + n0: c * T + q0 + 512],
                           start=(c == 0), stop=(c == NC - 1))
                seg.inc(cm.s_t, 1, eng="tensor")
                seg.wait("scalar", cm.s_t)
                seg.op("scalar", "copy", out=P_sb[:, si * 512 + n0: si * 512 + 512],
                       in_=st_ps[spar][:, n0:512])
                seg.inc(cm.s_s, 1, eng="scalar")
                st_read[spar] = cm.s_s.n
                kst_cons[par] = cm.s_s.n
                if j >= i0:
                    c0 = si * 512 + (j - i0) * 128
                    seg.wait("vector", cm.s_s)
                    seg.op("vector", "tensor_tensor", out=P_sb[:, c0:c0 + 128],
                           in0=P_sb[:, c0:c0 + 128],
                           in1=dmask_sb[:, r * 128:(r + 1) * 128], op=OP.mult)
                    seg.inc(cm.s_v, 1, eng="vector")
                if si + 3 < len(sl):
                    load_k7(sl[si + 3][0], sl[si + 3][1], par)
                    kv_cnt[si + 3] = s_kk[par].n
            pa_done_s = cm.s_s.n
            pa_done_v = cm.s_v.n

            seg.wait("tensor", cm.s_s, pa_done_s)
            seg.wait("tensor", cm.s_v, pa_done_v)
            if sum_read_v:
                seg.wait("tensor", cm.s_v, sum_read_v)
            for si, (r_, j_, n0_) in enumerate(sl):
                seg.op("tensor", "matmul", sum_ps[:, n0_:512], cm.ones_col[:],
                       P_sb[:, si * 512 + n0_:(si + 1) * 512],
                       start=(si == 0), stop=(si == len(sl) - 1))
            seg.inc(cm.s_t, 1, eng="tensor")
            seg.wait("vector", cm.s_t)
            seg.wait("vector", s_gn, rn_n)
            seg.op("vector", "tensor_tensor", out=rows[0:1, 0:512], in0=sum_ps[:],
                   in1=rn_sb[0:1, 0:512], op=OP.mult)
            seg.chain("vector", cm.s_v)
            sum_read_v = cm.s_v.n
            seg.wait("scalar", cm.s_v)
            seg.op("scalar", "activation", out=rows[0:1, 0:512], in_=rows[0:1, 0:512], func=AF.Abs)
            seg.inc(cm.s_s, 1, eng="scalar")
            seg.wait("vector", cm.s_s)
            seg.op("vector", "tensor_scalar_max", out=rows[0:1, 0:512], in0=rows[0:1, 0:512], scalar1=1.0)
            seg.chain("vector", cm.s_v)
            seg.op("vector", "reciprocal", out=rows[0:1, 512:1024], in_=rows[0:1, 0:512])
            seg.chain("vector", cm.s_v)
            seg.op("vector", "tensor_tensor", out=rows[0:1, 512:1024], in0=rows[0:1, 512:1024],
                   in1=rn_sb[0:1, 0:512], op=OP.mult)
            seg.chain("vector", cm.s_v)
            rn_read_v = cm.s_v.n
            seg.op("vector", "tensor_copy", out=rows_bf[0:1, 0:512], in_=rows[0:1, 512:1024])
            seg.inc(cm.s_v, 1, eng="vector")
            rf_v = cm.s_v.n
            seg.wait("tensor", cm.s_v, rf_v)
            if rf_read_s:
                seg.wait("tensor", cm.s_s, rf_read_s)
            seg.op("tensor", "matmul", bc_ps[:], cm.ones_row[:], rows_bf[0:1, 0:512], start=True, stop=True)
            seg.inc(cm.s_t, 1, eng="tensor")
            seg.wait("scalar", cm.s_t)
            if av_read_v:
                seg.wait("scalar", cm.s_v, av_read_v)
            seg.op("scalar", "copy", out=rfb_sb[:], in_=bc_ps[:])
            seg.inc(cm.s_s, 1, eng="scalar")
            rf_read_s = cm.s_s.n

            for dcg in range(2):
                vv_cnt = {}
                for pre in range(min(3, len(sl))):
                    load_v7(sl[pre][0], sl[pre][1], dcg, pre)
                    vv_cnt[pre] = s_vv[pre].n
                for si, (r, j, n0) in enumerate(sl):
                    par = si % 3
                    seg.wait("tensor", s_vv[par], vv_cnt[si])
                    if si < 2 and av_read_v:
                        seg.wait("tensor", cm.s_v, av_read_v)
                    for dd in range(4):
                        seg.op("tensor", "matmul", av_ps[dd][:, n0:512],
                               vst[par][:, dd * 128:(dd + 1) * 128],
                               P_sb[:, si * 512 + n0:(si + 1) * 512],
                               start=(si == 0), stop=(si == len(sl) - 1))
                    seg.inc(cm.s_t, 1, eng="tensor")
                    vst_cons[par] = cm.s_t.n
                    if si + 3 < len(sl):
                        load_v7(sl[si + 3][0], sl[si + 3][1], dcg, par)
                        vv_cnt[si + 3] = s_vv[par].n
                av_t = cm.s_t.n
                seg.wait("vector", cm.s_t, av_t)
                seg.wait("vector", cm.s_s, rf_read_s)
                for dd in range(4):
                    dc = dcg * 4 + dd
                    seg.op("vector", "tensor_tensor", out=ret_sb[:, dc * T + q0: dc * T + q0 + 512],
                           in0=av_ps[dd][:], in1=rfb_sb[:], op=OP.mult)
                seg.inc(cm.s_v, 1, eng="vector")
                av_read_v = cm.s_v.n
        ret_done_v = cm.s_v.n

        es_att.close()
        es_q.close()
        es_m = ExitStack()
        wm1_sb = seg.sbuf(lname + "wm1", [128, NC * 2048], BF16, stack=es_m)
        wm2_sb = seg.sbuf(lname + "wm2", [128, NC * 1024], BF16, stack=es_m)
        mid_sb = seg.sbuf(lname + "mid", [128, NC * 512], BF16, stack=es_m)
        sg_sb = seg.sbuf(lname + "sg", [128, 512], BF16, stack=es_m)
        m2t = [seg.sbuf(lname + f"m2t{i}", [128, 512], F32, stack=es_m) for i in range(2)]
        dump = seg.sbuf(lname + "dump", [128, 512], F32, stack=es_m)
        stats = seg.sbuf(lname + "stats", [128, 80], F32, stack=es_m)
        m_ps = [seg.psum(lname + f"mps{i}", [128, 512], F32, stack=es_m) for i in range(2)]
        seg.wait("gpsimd", cm.s_t)
        seg.wait("gpsimd", cm.s_v, ret_done_v)
        seg.wait("gpsimd", cm.s_s)
        seg.dma("gpsimd", wm1_sb[:].rearrange("p (c m) -> p c m", c=NC),
                wm1_d[l].rearrange("(c p) m -> p c m", p=128), s_w)
        seg.dma("gpsimd", wm2_sb[:].rearrange("p (c m) -> p c m", c=NC),
                wm2_d[l].rearrange("(c p) m -> p c m", p=128), s_w)
        wm_n = s_w.n
        seg.wait("tensor", s_w, wm_n)
        seg.wait("tensor", cm.s_v, ret_done_v)
        guard = [None, None]
        sg_guard = None
        m2_dma = [0, 0]
        nst = 0
        for tc in range(NQC):
            for oc in range(NC):
                par = oc % 2
                opar = (par + 1) % 2
                if guard[par]:
                    seg.wait("tensor", guard[par][0], guard[par][1])
                for c in range(NC):
                    seg.op("tensor", "matmul", m_ps[par][:],
                           wm1_sb[:, c * 2048 + 1024 + oc * 128: c * 2048 + 1024 + oc * 128 + 128],
                           ret_sb[:, c * T + tc * 512: c * T + (tc + 1) * 512],
                           start=(c == 0), stop=(c == NC - 1))
                seg.inc(cm.s_t, 1, eng="tensor")
                g_grp = cm.s_t.n
                if guard[opar]:
                    seg.wait("tensor", guard[opar][0], guard[opar][1])
                for c in range(NC):
                    seg.op("tensor", "matmul", m_ps[opar][:],
                           wm1_sb[:, c * 2048 + oc * 128: c * 2048 + oc * 128 + 128],
                           ret_sb[:, c * T + tc * 512: c * T + (tc + 1) * 512],
                           start=(c == 0), stop=(c == NC - 1))
                seg.inc(cm.s_t, 1, eng="tensor")
                a_grp = cm.s_t.n
                seg.wait("scalar", cm.s_t, g_grp)
                if sg_guard:
                    seg.wait("scalar", sg_guard[0], sg_guard[1])
                seg.op("scalar", "activation", out=sg_sb[:], in_=m_ps[par][:], func=AF.Silu,
                       bias=bm1_sb[:, l * 16 + 8 + oc: l * 16 + 8 + oc + 1])
                seg.inc(cm.s_s, 1, eng="scalar")
                guard[par] = (cm.s_s, cm.s_s.n)
                seg.wait("vector", cm.s_s)
                seg.wait("vector", cm.s_t, a_grp)
                seg.op("vector", "scalar_tensor_tensor", out=mid_sb[:, oc * 512:(oc + 1) * 512],
                       in0=m_ps[opar][:], scalar=bm1_sb[:, l * 16 + oc: l * 16 + oc + 1],
                       in1=sg_sb[:], op0=OP.add, op1=OP.mult)
                seg.inc(cm.s_v, 1, eng="vector")
                guard[opar] = (cm.s_v, cm.s_v.n)
                sg_guard = (cm.s_v, cm.s_v.n)
            mid_v = cm.s_v.n
            seg.wait("tensor", cm.s_v, mid_v)
            for dc in range(NC):
                par = dc % 2
                if guard[par]:
                    seg.wait("tensor", guard[par][0], guard[par][1])
                for kc in range(NC):
                    seg.op("tensor", "matmul", m_ps[par][:],
                           wm2_sb[:, kc * 1024 + dc * 128: kc * 1024 + dc * 128 + 128],
                           mid_sb[:, kc * 512:(kc + 1) * 512],
                           start=(kc == 0), stop=(kc == NC - 1))
                seg.inc(cm.s_t, 1, eng="tensor")
                seg.wait("scalar", cm.s_t)
                if m2_dma[par]:
                    seg.wait("scalar", s_m2[par], m2_dma[par])
                seg.op("scalar", "activation", out=m2t[par][:], in_=m_ps[par][:], func=AF.Identity,
                       bias=bm2_sb[:, l * 8 + dc: l * 8 + dc + 1],
                       accum_out=stats[:, nst:nst + 1])
                seg.chain("scalar", cm.s_s)
                seg.op("scalar", "activation", out=dump[:], in_=m2t[par][:], func=AF.Square,
                       accum_out=stats[:, 40 + nst: 40 + nst + 1])
                seg.inc(cm.s_s, 1, eng="scalar")
                nst += 1
                guard[par] = (cm.s_s, cm.s_s.n)
                seg.wait("sync", cm.s_s)
                seg.dma("sync", m2_dram[dc * 128:(dc + 1) * 128, tc * 512:(tc + 1) * 512],
                        m2t[par][:], s_m2[par])
                m2_dma[par] = s_m2[par].n
        m2_done_s = cm.s_s.n

        seg.wait("vector", cm.s_s, m2_done_s)
        seg.op("vector", "reduce_sum", out=stats[:, 76:77], in_=stats[:, 0:32], axis=AX.X)
        seg.op("vector", "reduce_sum", out=stats[:, 77:78], in_=stats[:, 40:72], axis=AX.X)
        seg.inc(cm.s_v, 1, eng="vector")
        seg.wait("tensor", cm.s_v)
        seg.wait("tensor", cm.s_init)
        es_g = ExitStack()
        gps = seg.psum(lname + "gps", [1, 2], F32, stack=es_g)
        seg.op("tensor", "matmul", gps[0:1, 0:1], onesf[:], stats[:, 76:77], start=True, stop=True)
        seg.op("tensor", "matmul", gps[0:1, 1:2], onesf[:], stats[:, 77:78], start=True, stop=True)
        seg.inc(cm.s_t, 1, eng="tensor")
        seg.wait("vector", cm.s_t)
        seg.op("vector", "tensor_copy", out=gn_sb[0:1, 0:2], in_=gps[0:1, 0:2])
        seg.inc(cm.s_v, 1, eng="vector")
        seg.wait("sync", cm.s_v)
        seg.dma("sync", gn_in[:], gn_sb[0:1, 0:2], s_gn)
        seg.wait("gpsimd", s_gn, s_gn.n)
        seg.op("gpsimd", "collective_compute", "AllReduce", OP.add, replica_groups=rg,
               ins=[gn_in.opt()], outs=[gn_out.opt()])
        seg.inc(s_rcc, None, eng="gpsimd")
        seg.wait("sync", s_rcc, s_rcc.n)
        seg.dma("sync", gn_sb[0:1, 2:4], gn_out[:], s_gn)
        NTOT = float(4096 * 1024)
        seg.wait("vector", s_gn)
        seg.op("vector", "tensor_scalar_mul", out=gn_sb[0:1, 4:6], in0=gn_sb[0:1, 2:4], scalar1=1.0 / NTOT)
        seg.chain("vector", cm.s_v)
        seg.op("vector", "tensor_tensor", out=gn_sb[0:1, 6:7], in0=gn_sb[0:1, 4:5], in1=gn_sb[0:1, 4:5], op=OP.mult)
        seg.chain("vector", cm.s_v)
        seg.op("vector", "tensor_sub", out=gn_sb[0:1, 5:6], in0=gn_sb[0:1, 5:6], in1=gn_sb[0:1, 6:7])
        seg.inc(cm.s_v, 1, eng="vector")
        seg.wait("scalar", cm.s_v)
        seg.op("scalar", "activation", out=gn_sb[0:1, 5:6], in_=gn_sb[0:1, 5:6], func=AF.Ln, bias=cm.eps1)
        seg.chain("scalar", cm.s_s)
        seg.op("scalar", "activation", out=gn_sb[0:1, 5:6], in_=gn_sb[0:1, 5:6], func=AF.Exp, scale=cm.mhalf1)
        seg.inc(cm.s_s, 1, eng="scalar")
        seg.wait("vector", cm.s_s)
        seg.op("vector", "tensor_copy", out=gn_sb[0:1, 6:8], in_=gn_sb[0:1, 4:6])
        seg.inc(cm.s_v, 1, eng="vector")
        seg.wait("tensor", cm.s_v)
        onesrow_f = seg.sbuf(lname + "onesrowf", [1, 128], F32, stack=es_g)
        seg.op("vector", "memset", onesrow_f[:], 1.0)
        seg.inc(cm.s_v, 1, eng="vector")
        seg.wait("tensor", cm.s_v)
        grep_ps = seg.psum(lname + "grep_ps", [128, 2], F32, stack=es_g)
        seg.op("tensor", "matmul", grep_ps[:], onesrow_f[:], gn_sb[0:1, 6:8], start=True, stop=True)
        seg.inc(cm.s_t, 1, eng="tensor")
        seg.wait("vector", cm.s_t)
        seg.op("vector", "tensor_copy", out=grep[:, 0:2], in_=grep_ps[:])
        seg.chain("vector", cm.s_v)
        seg.op("vector", "tensor_scalar_mul", out=coef_sb[:, 0:8], in0=gng_sb[:, l * 8:(l + 1) * 8],
               scalar1=grep[:, 1:2])
        seg.chain("vector", cm.s_v)
        seg.op("vector", "scalar_tensor_tensor", out=grep[:, 2:3], in0=grep[:, 0:1],
               scalar=-1.0, in1=grep[:, 1:2], op0=OP.mult, op1=OP.mult)
        seg.chain("vector", cm.s_v)
        seg.op("vector", "scalar_tensor_tensor", out=offc_sb[:, 0:8], in0=gng_sb[:, l * 8:(l + 1) * 8],
               scalar=grep[:, 2:3], in1=gnb_sb[:, l * 8:(l + 1) * 8], op0=OP.mult, op1=OP.add)
        seg.inc(cm.s_v, 1, eng="vector")
        gn_ready_v = cm.s_v.n
        es_g.close()

        m2r = [seg.sbuf(lname + f"m2r{i}", [128, 512], F32, stack=es_m) for i in range(2)]
        m2r_used = [0, 0]
        seg.wait("sync", s_m2[0], s_m2[0].n)
        seg.wait("sync", s_m2[1], s_m2[1].n)
        seg.wait("sync", cm.s_v, gn_ready_v)
        for dc in range(NC):
            for tc in range(NQC):
                par = (dc * NQC + tc) % 2
                if m2r_used[par]:
                    seg.wait("sync", cm.s_v, m2r_used[par])
                seg.dma("sync", m2r[par][:], m2_dram[dc * 128:(dc + 1) * 128, tc * 512:(tc + 1) * 512],
                        s_m2r[par])
                seg.wait("vector", s_m2r[par], s_m2r[par].n)
                seg.wait("vector", cm.s_v, gn_ready_v)
                xsl = slice(dc * T + tc * 512, dc * T + (tc + 1) * 512)
                seg.op("vector", "scalar_tensor_tensor", out=hcur[:, xsl], in0=m2r[par][:],
                       scalar=coef_sb[:, dc:dc + 1], in1=hcur[:, xsl], op0=OP.mult, op1=OP.add)
                seg.chain("vector", cm.s_v)
                seg.op("vector", "tensor_scalar_add", out=hcur[:, xsl], in0=hcur[:, xsl],
                       scalar1=offc_sb[:, dc:dc + 1])
                seg.inc(cm.s_v, 1, eng="vector")
                m2r_used[par] = cm.s_v.n
        y_v = cm.s_v.n
        es_m.close()
        es_r.close()

        es_f = ExitStack()
        yb = seg.sbuf(lname + "yb", [128, NC * T], BF16, stack=es_f)
        wf1_sb = seg.sbuf(lname + "wf1", [128, NC * 2048], BF16, stack=es_f)
        wf2_sb = seg.sbuf(lname + "wf2", [128, 16 * 1024], BF16, stack=es_f)
        fmid = seg.sbuf(lname + "fmid", [128, 16 * 512], BF16, stack=es_f)
        f_ps = [seg.psum(lname + f"fps{i}", [128, 512], F32, stack=es_f) for i in range(2)]
        seg.wait("gpsimd", cm.s_v, y_v)
        seg.wait("gpsimd", cm.s_t)
        seg.wait("gpsimd", cm.s_s)
        seg.dma("gpsimd", wf1_sb[:].rearrange("p (c m) -> p c m", c=NC),
                wf1_d[l].rearrange("(c p) m -> p c m", p=128), s_w)
        seg.dma("gpsimd", wf2_sb[:].rearrange("p (c m) -> p c m", c=16),
                wf2_d[l].rearrange("(c p) m -> p c m", p=128), s_w)
        wf_n = s_w.n
        seg.wait("vector", cm.s_v, y_v)
        for c in range(NC):
            seg.op("vector", "tensor_copy", out=yb[:, c * T:(c + 1) * T], in_=hcur[:, c * T:(c + 1) * T])
        seg.inc(cm.s_v, 1, eng="vector")
        yb_v = cm.s_v.n
        seg.wait("tensor", s_w, wf_n)
        seg.wait("tensor", cm.s_v, yb_v)
        guard = [None, None]
        for tc in range(NQC):
            for oc in range(16):
                par = oc % 2
                if guard[par]:
                    seg.wait("tensor", guard[par][0], guard[par][1])
                for c in range(NC):
                    seg.op("tensor", "matmul", f_ps[par][:],
                           wf1_sb[:, c * 2048 + oc * 128: c * 2048 + oc * 128 + 128],
                           yb[:, c * T + tc * 512: c * T + (tc + 1) * 512],
                           start=(c == 0), stop=(c == NC - 1))
                seg.inc(cm.s_t, 1, eng="tensor")
                seg.wait("scalar", cm.s_t)
                seg.op("scalar", "activation", out=fmid[:, oc * 512:(oc + 1) * 512], in_=f_ps[par][:],
                       func=AF.Gelu, bias=bf1_sb[:, l * 16 + oc: l * 16 + oc + 1])
                seg.inc(cm.s_s, 1, eng="scalar")
                guard[par] = (cm.s_s, cm.s_s.n)
            fm_s = cm.s_s.n
            seg.wait("tensor", cm.s_s, fm_s)
            for dc in range(NC):
                par = dc % 2
                if guard[par]:
                    seg.wait("tensor", guard[par][0], guard[par][1])
                for kc in range(16):
                    seg.op("tensor", "matmul", f_ps[par][:],
                           wf2_sb[:, kc * 1024 + dc * 128: kc * 1024 + dc * 128 + 128],
                           fmid[:, kc * 512:(kc + 1) * 512],
                           start=(kc == 0), stop=(kc == 15))
                seg.inc(cm.s_t, 1, eng="tensor")
                seg.wait("vector", cm.s_t)
                xsl = slice(dc * T + tc * 512, dc * T + (tc + 1) * 512)
                seg.op("vector", "scalar_tensor_tensor", out=hcur[:, xsl], in0=f_ps[par][:],
                       scalar=bf2_sb[:, l * 8 + dc: l * 8 + dc + 1], in1=hcur[:, xsl],
                       op0=OP.add, op1=OP.add)
                seg.inc(cm.s_v, 1, eng="vector")
                guard[par] = (cm.s_v, cm.s_v.n)
        fy_v = cm.s_v.n
        es_f.close()

        es_ln2 = ExitStack()
        seg.wait("vector", cm.s_v, fy_v)
        lnr = emit_ln_fm(seg, cm, hcur, T, hcur, es_ln2, name=lname + "ln",
                         gb=(lng_sb[:, l * 8:(l + 1) * 8], lnb_sb[:, l * 8:(l + 1) * 8]),
                         gb_ready=(s_par, par_n))
        es_ln2.close()
        hcur_v = lnr["h_v"]

    es_fin = ExitStack()
    xt = [seg.sbuf(name + f"xt{i}", [128, 2048], F32, stack=es_fin) for i in range(2)]
    s_fin = [seg.sem(name + "sfin0"), seg.sem(name + "sfin1")]
    s_st2 = [seg.sem(name + "sst0"), seg.sem(name + "sst1")]
    seg.wait("sync", cm.s_v, hcur_v)
    for c in range(NC):
        par = c % 2
        if c >= 2:
            seg.wait("sync", s_st2[par], s_st2[par].n)
        seg.dma("sync", xt[par][:, 0:T], x_in[c * 128:(c + 1) * 128, :], s_fin[par])
        seg.wait("vector", s_fin[par], s_fin[par].n)
        seg.op("vector", "tensor_tensor", out=xt[par][:, 0:T], in0=xt[par][:, 0:T],
               in1=hcur[:, c * T:(c + 1) * T], op=OP.add)
        seg.inc(cm.s_v, 1, eng="vector")
        seg.wait("sync", cm.s_v)
        seg.dma("sync", x_out[c * 128:(c + 1) * 128, :], xt[par][:, 0:T], s_st2[par])
    seg.wait("sync", s_st2[0], s_st2[0].n)
    seg.wait("sync", s_st2[1], s_st2[1].n)
    es_fin.close()

    if standalone:
        seg.emit()
        return seg
    return None


def host_prep_s7(inputs, core, T=2048):
    p = core % 2
    d = {
        "wkqv7": inputs["L1_attn_b_Wkqv"],
        "wm17": inputs["L1_attn_b_Wm1"],
        "wm27": inputs["L1_attn_b_Wm2"],
        "wf17": inputs["L1_attn_b_Wf1"],
        "wf27": inputs["L1_attn_b_Wf2"],
    }
    def fm(v, w):
        return np.ascontiguousarray(v.reshape(4, w, 128).transpose(2, 0, 1))
    d["bm1f"] = fm(inputs["L1_attn_b_bm1"], 16)
    d["bm2f"] = fm(inputs["L1_attn_b_bm2"], 8)
    d["gngf"] = fm(inputs["L1_attn_b_gn_g"], 8)
    d["gnbf"] = fm(inputs["L1_attn_b_gn_b"], 8)
    d["bf1f"] = fm(inputs["L1_attn_b_bf1"], 16)
    d["bf2f"] = fm(inputs["L1_attn_b_bf2"], 8)
    d["lngf"] = fm(inputs["L1_attn_b_ln_g"], 8)
    d["lnbf"] = fm(inputs["L1_attn_b_ln_b"], 8)
    tri = (np.arange(128)[:, None] <= np.arange(128)[None, :]).astype(np.float32)
    m0 = tri if p == 0 else np.ones((128, 128), np.float32)
    m1 = np.zeros((128, 128), np.float32) if p == 0 else tri
    d["dmask7"] = np.concatenate([m0, m1], axis=1).astype(ml_dtypes.bfloat16)
    own = own_token_index(core).astype(np.float64)
    d["rn32inv"] = (1.0 / (32.0 * (1.0 + np.sqrt(own))))[None, :].astype(np.float32)
    return d




def _to32(a):
    return np.ascontiguousarray(a, dtype=np.float32)


def kernel(**inputs):
    x = np.ascontiguousarray(inputs["L0_attn_a_hidden_states"], dtype=np.float32)
    T = 2048
    d1 = host_prep_s1(inputs)
    d2 = host_prep_s2(inputs)
    d3 = [host_prep_s3(inputs, c) for c in range(8)]
    d5 = [host_prep_s5(inputs, c) for c in range(8)]
    d6 = [host_prep_s6(inputs, c) for c in range(8)]
    d7 = [host_prep_s7(inputs, c) for c in range(8)]

    nc = bass.Bass("TRN2", target_bir_lowering=False)
    x_in_ext = nc.dram_tensor("x_in", [1024, T], F32, kind="ExternalInput").ap()
    x_out_ext = nc.dram_tensor("x_out", [1024, T], F32, kind="ExternalOutput").ap()
    parks = [nc.dram_tensor(f"park{i}", [1024, T], F32, kind="Internal").ap() for i in range(4)]
    base_init = nc.sbuf_base
    xs = nc.alloc_sbuf_tensor("xs_g", [128, 8 * T], F32)
    base_xs = nc.sbuf_base
    dram_keep = nc.local_dram_base
    shared_keep = nc.shared_dram_base

    def reset(sbase):
        nc.reset(previous_sbuf_base=sbase, previous_local_dram_base=dram_keep,
                 previous_shared_dram_base=shared_keep)

    seg = Seg(nc); cm = Common(seg)
    for c in range(8):
        seg.dma("sync", xs[:, c * T:(c + 1) * T], x_in_ext[c * 128:(c + 1) * 128, :], cm.s_ld)
    seg.wait("sync", cm.s_ld)
    seg.emit()
    reset(base_xs)

    def run_seg(fn, **kw):
        s = Seg(nc); c = Common(s)
        fn(nc, standalone=False, xs=None if "io_dram" in kw else xs, cm=c, seg=s, **kw)
        s.emit()

    run_seg(build_s1, name="f1", dram_prefix="s1_")
    reset(base_xs)
    run_seg(build_s2, name="f2", dram_prefix="s2_")
    reset(base_xs)
    run_seg(build_s3, name="f3", dram_prefix="s3_")
    reset(base_xs)
    run_seg(build_s2, name="f4", dram_prefix="s4_")
    reset(base_xs)
    run_seg(build_s5, name="f5", dram_prefix="s5_")
    reset(base_xs)

    seg = Seg(nc); cm = Common(seg)
    for c in range(8):
        seg.dma("sync", parks[0][c * 128:(c + 1) * 128, :], xs[:, c * T:(c + 1) * T], cm.s_st)
    seg.wait("sync", cm.s_st)
    seg.emit()
    reset(base_init)

    run_seg(build_s6, name="f6", dram_prefix="s6_", io_dram=(parks[0], parks[1]))
    reset(base_init)
    run_seg(build_s7, name="f7", dram_prefix="s7_", io_dram=(parks[1], parks[2]))
    reset(base_init)
    run_seg(build_s6, name="f8", dram_prefix="s8_", io_dram=(parks[2], parks[3]))
    reset(base_init)

    seg = Seg(nc); cm = Common(seg)
    seg.dma("sync", x_out_ext[:, :], parks[3][:, :], cm.s_st)
    seg.wait("sync", cm.s_st)
    seg.emit()

    in_maps = []
    for c in range(8):
        m = {"x_in": slice_x_fm(x, c)}
        for k, v in [("wqT_e", d1["wqT_e"]), ("wqT_o", d1["wqT_o"]), ("wkT_e", d1["wkT_e"]),
                     ("wkT_o", d1["wkT_o"]), ("wvT", d1["wvT"]), ("b_eo", d1["b_eo"]),
                     ("bv2", d1["bv2"])]:
            m["s1_" + k] = v
        m["s1_t_e_own"] = slice_cols(d1["t_full_e"], c)
        for pre in ("s2_", "s4_"):
            m[pre + "wgT"] = d2[0]; m[pre + "wuT"] = d2[1]; m[pre + "wdT"] = d2[2]
        for k, v in d3[c].items():
            m["s3_" + k] = v
        for k, v in d5[c].items():
            m["s5_" + k] = v
        for k, v in d6[c].items():
            m["s6_" + k] = v
            m["s8_" + k] = v
        for k, v in d7[c].items():
            m["s7_" + k] = v
        in_maps.append(m)
    res = run_bass_kernel_spmd(nc, in_maps, core_ids=list(range(8)))
    return unslice_x_fm([np.asarray(res.results[i]["x_out"], dtype=np.float32) for i in range(8)])
